# version 06
# baseline (speedup 1.0000x reference)
import math
import os
import numpy as np

import concourse.bass as bass
import concourse.mybir as mybir
from concourse.bass_utils import run_bass_kernel_spmd

F32 = mybir.dt.float32
BF16 = mybir.dt.bfloat16
Alu = mybir.AluOpType
Act = mybir.ActivationFunctionType
AX = mybir.AxisListType.X

D = 1024
DI = 2048
K, Hd = 32, 64
KS = 4
C = 2 * DI + K
NXC = DI // 128
NZC = 2 * NXC + 1
NDC = D // 128
ES, EF = 128, 32
GRP = 32
NPACK = 32
MAGIC = 1.5 * 2 ** 23
TWO_PI = 2.0 * math.pi
ND_K = K - 4
ST_PW = 2 * NXC
N_CORES = 8


def st_re(j):
    return 2 * j


def st_im(j):
    return 2 * j + 1


def MM(t, *a, **kw):
    kw.setdefault("skip_group_check", True)
    return t.matmul(*a, **kw)


def build_nc(TPC, TT, SEG_TILES, seq_len, phases=(1, 2, 3), dev=False):
    if dev is True:
        dev = {"stash", "gst", "part", "x1", "oh", "xdc", "inner", "outg",
               "perm", "l2s", "slp", "tw", "pw", "z", "lnfm"}
    elif not dev:
        dev = set()
    NT = TPC // TT
    BPT = TT // 128
    NSEG = NT // SEG_TILES
    SEG_T = SEG_TILES * TT
    GSEG = SEG_T // GRP
    BLK_SEG = SEG_T // 128
    GPT = TT // GRP
    GPB = 128 // GRP
    SUB = 16 if GSEG >= 16 else GSEG
    NSUB = GSEG // SUB
    MSL = 4 * SUB
    NPG = 8
    NPGRP = NPACK // NPG
    ZW = TT + 3

    nc = bass.Bass("TRN2", target_bir_lowering=False,
                   detect_race_conditions=False)

    def din(n, s, d=F32):
        return nc.dram_tensor(n, s, d, kind="ExternalInput")

    x_in = din("x", [TPC + 128, D])
    win = din("win", [NDC, 128, C])
    convd = din("convd", [128, KS * NZC * 128])
    cbias_in = din("cbias", [128, NZC])
    wout_in = din("wout", [16, 128, D])
    theta_in = din("theta_fm", [128, NXC])
    dslope = din("dslope", [K, 1])
    sscale = din("sscale", [K, 1])
    dlogit = din("dlogit", [1, 3])
    wre_in = din("wre2", [128, 64])
    wim_in = din("wim2", [128, 64])
    nsc_in = din("nsc", [128, 1])
    wc_in = din("wc", [NDC, 128, ES])
    bc_in = din("bcv", [ES, 1])
    w1p_in = din("w1p", [NDC, 128, ES * EF])
    w2p_in = din("w2p", [NPACK, 128, D])
    b1p_in = din("b1p", [128, NPACK])
    b2p_in = din("b2p", [128, D])
    esel_in = din("esel", [32, NXC * 128])
    qsel_in = din("qsel", [128, NXC * 32])
    bmask_in = din("bmask", [128, 4])
    tie_in = din("tie", [128, GRP])
    iota_in = din("iota_t", [K, TT])
    twsgn_in = din("twsgn", [K, 1])
    twbm_in = din("twbm", [K, 1])
    flag_in = din("flag_first", [128, 1])
    ident_in = din("ident", [128, 128])

    out = nc.dram_tensor("out", [TPC, D], F32, kind="ExternalOutput")

    stash = nc.dram_tensor("stash", [NT, 128, NZC * TT], BF16)
    gstash = nc.dram_tensor("gstash", [NT, 128, NXC * TT], BF16)
    ln2st = nc.dram_tensor("ln2st", [NT, 128, BPT * D], BF16)
    ohst = nc.dram_tensor("ohst", [max(NSEG, 1), 128, SEG_T], BF16)
    cc_i = nc.dram_tensor("cc_i", [128, NZC], F32)
    cc_o = nc.dram_tensor("cc_o", [128, NZC], F32)
    ob = nc.dram_tensor("ob", [16, 2, 4, SUB, D], BF16)
    if dev:
        d_stash = nc.dram_tensor("d_stash", [NT, 128, NZC * TT], F32,
                                 kind="ExternalOutput")
        d_gst = nc.dram_tensor("d_gst", [NT, 128, NXC * TT], F32,
                               kind="ExternalOutput")
        d_part = nc.dram_tensor("d_part", [128, NZC * NT], F32,
                                kind="ExternalOutput")
        d_x1 = nc.dram_tensor("d_x1", [TPC, D], F32, kind="ExternalOutput")
        d_oh = nc.dram_tensor("d_oh", [max(NSEG, 1), 128, SEG_T], F32,
                              kind="ExternalOutput")
        d_xdc = nc.dram_tensor("d_xdc", [128, BLK_SEG * 512], F32,
                               kind="ExternalOutput")
        d_inner = nc.dram_tensor("d_inner", [128, NPACK * GSEG], F32,
                                 kind="ExternalOutput")
        d_outg = nc.dram_tensor("d_outg", [128, D], F32,
                                kind="ExternalOutput")
        d_perm = nc.dram_tensor("d_perm", [128, BLK_SEG * 512], F32,
                                kind="ExternalOutput")
        d_l2s = nc.dram_tensor("d_l2s", [128, SEG_TILES * BPT * D], F32,
                               kind="ExternalOutput")
        d_slp = nc.dram_tensor("d_slp", [K, 8], F32, kind="ExternalOutput")
        d_tw = nc.dram_tensor("d_tw", [K, NT + 2], F32,
                              kind="ExternalOutput")
        d_pw = nc.dram_tensor("d_pw", [K, 2 * TT], F32,
                              kind="ExternalOutput")
        d_z = nc.dram_tensor("d_z", [128, 5 * (TT + 3)], F32,
                             kind="ExternalOutput")
        d_lnfm = nc.dram_tensor("d_lnfm", [128, NDC * TT], F32,
                                kind="ExternalOutput")
        d_carry = nc.dram_tensor("d_carry", [128, 2 * NZC], F32,
                                 kind="ExternalOutput")

    prog = {k: [] for k in ("sync", "gpsimd", "tensor", "vector", "scalar")}

    class Ctr:
        def __init__(self, h, step=1):
            self.h, self.n, self.step = h, 0, step

        def inc(self, instr, k=None):
            k = self.step if k is None else k
            instr.then_inc(self.h, k)
            self.n += k

        def ev(self):
            return (self.h, self.n)

    sem_cm = [nc.semaphore(n) for n in
              ("ld", "wl", "te", "ve", "se", "ge", "so", "ao", "ccs", "sq",
               "chv", "chs")]
    sems = [s.__enter__() for s in sem_cm]
    (S_ld, S_wl, S_te, S_ve, S_se, S_ge, S_so, S_ao, S_cc, S_sq,
     S_chv, S_chs) = sems
    _chain_n = {"v": 0, "s": 0}

    def fence(eng, instr, which="v"):
        _chain_n[which] += 1
        h = S_chv if which == "v" else S_chs
        instr.then_inc(h, 1)
        eng.wait_ge(h, _chain_n[which])
    ld = Ctr(S_ld, 16); wl = Ctr(S_wl, 16); te = Ctr(S_te); ve = Ctr(S_ve)
    se = Ctr(S_se); ge = Ctr(S_ge); so = Ctr(S_so, 16); ao = Ctr(S_ao, 16)
    ccc = Ctr(S_cc); sq = Ctr(S_sq, 16)

    def emit_multi(engname, ctr, after, body, n):
        evs = tuple(e for e in after if e is not None and e[1] > 0)
        ctr.n += ctr.step * n
        h, st = ctr.h, ctr.step

        def th(eng, _evs=evs, _body=body):
            for hh, nn in _evs:
                eng.wait_ge(hh, nn)
            cnt = _body(eng, lambda ins: ins.then_inc(h, st))
            assert cnt == n, f"emit_multi expected {n} DMAs, got {cnt}"
        prog[engname].append(th)
        return ctr.ev()

    def emit(engname, ctr, after, body):
        evs = tuple(e for e in after if e is not None and e[1] > 0)
        if ctr is not None:
            ctr.n += ctr.step
            tgt = (ctr.h, ctr.step)
        else:
            tgt = None

        def th(eng, _evs=evs, _body=body, _tgt=tgt):
            for h, n in _evs:
                eng.wait_ge(h, n)
            last = _body(eng)
            if _tgt is not None:
                assert last is not None, "emit body returned no instruction"
                last.then_inc(_tgt[0], _tgt[1])
        prog[engname].append(th)
        return ctr.ev() if ctr is not None else None

    def T(after, body):
        return emit("tensor", te, after, body)

    def V(after, body):
        return emit("vector", ve, after, body)

    def S(after, body):
        return emit("scalar", se, after, body)

    def G(after, body, ctr=None):
        return emit("gpsimd", ctr if ctr is not None else ge, after, body)

    def SY(after, body, ctr=None):
        return emit("sync", ctr if ctr is not None else so, after, body)

    per_cm = [
        nc.sbuf_tensor("consts", [128, 16], F32),
        nc.sbuf_tensor("partials", [128, NZC * NT], F32),
        nc.sbuf_tensor("carry", [128, NZC], F32),
        nc.sbuf_tensor("flagsb", [128, 1], F32),
        nc.sbuf_tensor("smll", [128, 24], F32),
        nc.sbuf_tensor("identb", [128, 128], BF16),
        nc.sbuf_tensor("eselb", [32, NXC * 128], BF16),
        nc.sbuf_tensor("ccsb", [128, NZC], F32),
        nc.psum_tensor("pall", [128, 3072], F32),
        nc.psum_tensor("pbf", [128, 2048], BF16),
    ]
    (consts, partials, carry, flagsb, smll, identb, eselb, ccsb,
     pall, pbf) = [c.__enter__() for c in per_cm]

    CONST_VALS = [0.5, 1.0, MAGIC, -MAGIC, TWO_PI, math.pi / 2.0,
                  1e-5, -20.0, 20.0, 1.0 / (2.0 * Hd), 1e-4, -1.0,
                  0.0, -0.5, 1.0 / D]
    CV = {v: i for i, v in enumerate(CONST_VALS)}
    for v, i in CV.items():
        nc.const_aps.register(F32, v, consts[:, i:i + 1])

    def cst(v, p=128):
        return consts[0:p, CV[v]:CV[v] + 1]

    def bank(i, w=512, part=128, off=0):
        return pall[0:part, i * 512 + off:i * 512 + off + w]

    def pbf_slot(s, w=128):
        off = (s % 2) * 1024 + (s // 2) * 256
        return pbf[:, off:off + w]

    KP = os.environ.get("KPART", "")
    if os.environ.get("KFRAME"):
        ev0 = None
        ev_id = None
    elif KP == "id":
        ev0 = None
        ev_id = G([], lambda g: g.dma_start(out=identb[:, :],
                                            in_=ident_in.ap()), wl)
    elif KP == "c0":
        ev_id = None
        ev0 = V([], lambda v: [v.memset(consts[:, i:i + 1], val)
                               for val, i in CV.items()] and
                v.memset(partials[:, :], 0.0) and
                v.memset(carry[:, :], 0.0))
    elif KP == "c1":
        ev_id = None
        ev0 = V([], lambda v: v.memset(consts[:, 0:1], 0.5))
    else:
        ev0 = V([], lambda v: [v.memset(consts[:, i:i + 1], val)
                               for val, i in CV.items()] and
                v.memset(partials[:, :], 0.0) and
                v.memset(carry[:, :], 0.0))
        ev_id = G([], lambda g: g.dma_start(out=identb[:, :],
                                            in_=ident_in.ap()), wl)

    def ln_norm(src, dst, nb, stc, after, scr):
        LNCUT = int(os.environ.get("LNCUT", "9"))
        evm0 = V(after, lambda v: [
            v.reduce_sum(stc[:, b:b + 1], src[:, b * D:(b + 1) * D], AX)
            for b in range(nb)][-1])
        evm = V([evm0], lambda v: v.tensor_scalar_mul(
            stc[:, 0:nb], stc[:, 0:nb], 1.0 / D))
        if LNCUT < 2:
            return evm, evm
        evq = S(after, lambda s: [
            s.activation(scr[:, 0:D], src[:, b * D:(b + 1) * D], Act.Square,
                         scale=cst(1.0), bias=cst(0.0),
                         accum_out=stc[:, nb + b:nb + b + 1])
            for b in range(nb)][-1])
        if LNCUT < 3:
            return evq, evq
        def v_var(v):
            fence(v, v.tensor_scalar_mul(stc[:, nb:2 * nb],
                                         stc[:, nb:2 * nb], 1.0 / D))
            fence(v, v.tensor_mul(stc[:, 2 * nb:3 * nb], stc[:, 0:nb],
                                  stc[:, 0:nb]))
            return v.tensor_sub(stc[:, nb:2 * nb], stc[:, nb:2 * nb],
                                stc[:, 2 * nb:3 * nb])
        evv = V([evm, evq], v_var)
        if LNCUT < 4:
            return evv, evv
        evr = S([evv], lambda s: [
            s.activation(stc[:, 2 * nb:3 * nb], stc[:, nb:2 * nb], Act.Ln,
                         scale=cst(1.0), bias=cst(1e-5)),
            s.activation(stc[:, 2 * nb:3 * nb], stc[:, 2 * nb:3 * nb],
                         Act.Exp, scale=cst(-0.5), bias=cst(0.0)),
        ][-1])
        if LNCUT < 5:
            return evr, evr
        evb = V([evr], lambda v: v.scalar_tensor_tensor(
            stc[:, 3 * nb:4 * nb], stc[:, 0:nb], -1.0,
            stc[:, 2 * nb:3 * nb], Alu.mult, Alu.mult))
        if LNCUT < 6:
            return evb, evb
        evn = S([evb], lambda s: [
            s.activation(dst[:, b * D:(b + 1) * D], src[:, b * D:(b + 1) * D],
                         Act.Identity, scale=stc[:, 2 * nb + b:2 * nb + b + 1],
                         bias=stc[:, 3 * nb + b:3 * nb + b + 1])
            for b in range(nb)][-1])
        return evn, evr

    def tm_to_fm(src, nb, put, after):
        if not isinstance(after, list):
            after = [after]
        if os.environ.get("TFCUT"):
            e = V(after, lambda v: v.engine_nop())
            return e, [e]
        evs = []
        for b in range(nb):
            for dc in range(NDC):
                slot = (b * NDC + dc) % 4
                gate = [evs[-2][0]] if len(evs) >= 2 else []
                evt = T(after + gate,
                        lambda t, b=b, dc=dc, slot=slot: t.transpose(
                            pbf_slot(slot),
                            src[:, b * D + dc * 128:b * D + (dc + 1) * 128],
                            identb[:, :]))
                eve = V([evt], lambda v, b=b, dc=dc, slot=slot: v.tensor_copy(
                    put(b, dc), pbf_slot(slot)))
                evs.append((eve, evt))
        return evs[-1][0], [e[0] for e in evs]

    p1_cm = [
        nc.sbuf_tensor("theta2p", [128, NXC], F32),
        nc.sbuf_tensor("thetasb", [128, NXC], F32),
        nc.sbuf_tensor("slp", [K, 8], F32),
        nc.sbuf_tensor("twsc", [K, 1], F32),
        nc.sbuf_tensor("twbias", [K, NT], F32),
        nc.sbuf_tensor("sscb", [K, 1], F32),
        nc.sbuf_tensor("wpoly", [128, 3], F32),
        nc.sbuf_tensor("iotasb", [K, TT], F32),
        nc.sbuf_tensor("winb", [128, NDC * C], BF16),
        nc.sbuf_tensor("convb", [128, KS * NZC * 128], BF16),
        nc.sbuf_tensor("xtm", [128, BPT * D], F32),
        nc.sbuf_tensor("lntm", [128, BPT * D], BF16),
        nc.sbuf_tensor("lnfm", [128, NDC * TT], BF16),
        nc.sbuf_tensor("zsb", [128, 6 * ZW], BF16),
        nc.sbuf_tensor("zhalo", [128, NZC * 3], BF16),
        nc.sbuf_tensor("stsb", [128, NZC * TT], BF16),
        nc.sbuf_tensor("gtsb", [128, (NXC // 2) * TT], BF16),
        nc.sbuf_tensor("lnst", [128, 4 * 4], F32),
        nc.sbuf_tensor("pwsb", [K, 2 * TT], F32),
        nc.sbuf_tensor("scp1", [128, max(D, 2 * TT)], F32),
        nc.sbuf_tensor("chainr", [128, 2 * TT], F32),
        nc.sbuf_tensor("sincos", [128, 2 * TT], BF16),
        nc.sbuf_tensor("polyb", [128, TT], BF16),
        nc.sbuf_tensor("remb", [128, 2 * TT], BF16),
        nc.sbuf_tensor("ones_row", [1, 128], F32),
        nc.sbuf_tensor("cbsb", [128, NZC], F32),
        nc.sbuf_tensor("usb", [128, TT], F32),
    ]
    (theta2p, thetasb, slp, twsc, twbias, sscb, wpoly, iotasb, winb, convb,
     xtm, lntm, lnfm, zsb, zhalo, stsb, gtsb, lnst, pwsb,
     scp1, chainr, sincos, polyb, remb, ones_row, cbsb, usb) = \
        [c.__enter__() for c in p1_cm]

    P1SETUP = int(os.environ.get("P1SETUP", "99"))
    if 1 in phases and not os.environ.get("KFRAME"):
        def g_loads(g, inc):
            n = 0
            for dst, srcp in [
                (theta2p[:, :], theta_in.ap()),
                (thetasb[:, :], theta_in.ap()),
                (slp[:, 0:1], dslope.ap()),
                (sscb[:, :], sscale.ap()),
                (smll[0:1, 0:3], dlogit.ap()),
                (flagsb[:, :], flag_in.ap()),
                (iotasb[:, :], iota_in.ap()),
                (slp[:, 7:8], twsgn_in.ap()),
                (smll[0:K, 12:13], twbm_in.ap()),
                (convb[:, :], convd.ap()),
                (eselb[:, :], esel_in.ap()),
                (cbsb[:, :], cbias_in.ap()),
            ]:
                inc(g.dma_start(out=dst, in_=srcp))
                n += 1
            for dc in range(NDC):
                inc(g.dma_start(out=winb[:, dc * C:(dc + 1) * C],
                                in_=win.ap()[dc, :, :]))
                n += 1
            return n
        ev_gl = emit_multi("gpsimd", wl, [], g_loads, 12 + NDC) if P1SETUP >= 1 else None

        ev_or = V([ev0], lambda v: v.memset(ones_row[:, :], 1.0)) if P1SETUP >= 2 else None
        ev_stz = V([], lambda v: v.memset(stsb[:, :], 0.0)) if P1SETUP >= 3 else None
        ev1 = None if P1SETUP < 4 else V([ev_gl, ev0], lambda v: [
            v.tensor_scalar_mul(theta2p[:, :], theta2p[:, :], 1.0 / TWO_PI),
            v.scalar_tensor_tensor(slp[:, 1:2], slp[:, 0:1], -1.0,
                                   slp[:, 0:1], Alu.mult, Alu.max),
            v.tensor_scalar_max(slp[:, 2:3], slp[:, 0:1], 0.0),
        ][-1])
        ev2 = None if P1SETUP < 5 else S([ev1], lambda s: [
            s.activation(slp[:, 3:4], slp[:, 1:2], Act.Exp,
                         scale=cst(-1.0, K), bias=cst(0.0, K)),
            s.activation(slp[:, 4:5], slp[:, 3:4], Act.Ln,
                         scale=cst(1.0, K), bias=cst(1.0, K)),
            s.activation(smll[0:1, 4:7], smll[0:1, 0:3], Act.Exp,
                         scale=cst(1.0, 1), bias=cst(0.0, 1)),
        ][-1])

        def v_setup2(v):
            fence(v, v.tensor_add(slp[:, 5:6], slp[:, 2:3],
                                  slp[:, 4:5]))
            v.tensor_mul(twsc[:, :], slp[:, 5:6], slp[:, 7:8])
            fence(v, v.tensor_mul(slp[:, 6:7], slp[:, 5:6],
                                  smll[0:K, 12:13]))
            for i in range(NT):
                v.scalar_tensor_tensor(twbias[:, i:i + 1], twsc[:, :],
                                       float(i * TT), slp[:, 6:7],
                                       Alu.mult, Alu.add)
            return v.reduce_sum(smll[0:1, 7:8], smll[0:1, 4:7], AX)
        ev3a = V([ev2], v_setup2) if P1SETUP >= 6 else None
        ev3b = None if P1SETUP < 6 else V([ev3a], lambda v: v.reciprocal(
            smll[0:1, 8:9], smll[0:1, 7:8]))
        ev3 = None if P1SETUP < 6 else V([ev3b], lambda v:
            v.scalar_tensor_tensor(
                smll[0:1, 9:12], smll[0:1, 4:7], 1.0,
                smll[0:1, 8:9].broadcast_to([1, 3]), Alu.mult, Alu.mult))
        ev4 = T([ev3, ev_or], lambda t: MM(t, 
            bank(0, 3), ones_row[:, :], smll[0:1, 9:12],
            start=True, stop=True))
        ev_wp = None if P1SETUP < 8 else V([ev4], lambda v: [
            v.tensor_copy(wpoly[:, 0:2], bank(0, 2)),
            v.tensor_scalar_mul(wpoly[:, 2:3], bank(0, 1, off=2), -1.0),
        ][-1])

        xpfx = xtm
        lnpfx = lnfm
        SKIP_PFX = os.environ.get('P1CUT') in ('nopfx', 'setup')
        if not SKIP_PFX:
            evx = SY([], lambda sy: sy.dma_start(out=xpfx[:, 0:D],
                                                 in_=x_in.ap()[0:128, :]),
                     ld)
            evn, _ = ln_norm(xtm, lntm, 1, lnst, [evx, ev0, ev_id], scp1)
            evfm, _ = tm_to_fm(lntm, 1,
                               lambda b, dc: lnfm[:, dc * 128:dc * 128 + 128],
                               evn)
        else:
            evn = evfm = None
        ev_zh = []
        for oc in (range(NZC) if not SKIP_PFX and os.environ.get('P1CUT') != 'zh' else []):
            pc = 128 if oc < NZC - 1 else 32
            bnk = oc % 2

            def t_pfx(t, oc=oc, pc=pc, bnk=bnk):
                last = None
                for dc in range(NDC):
                    last = MM(t, 
                        bank(bnk, 128, pc), winb[:, dc * C + oc * 128:
                                                 dc * C + oc * 128 + pc],
                        lnpfx[:, dc * 128:(dc + 1) * 128],
                        start=(dc == 0), stop=(dc == NDC - 1))
                return last
            gate = [ev_zh[-2][1]] if len(ev_zh) >= 2 else [ev_wp, evfm]
            evt = T(gate + [evfm, ev_gl], t_pfx)
            eve = V([evt], lambda v, oc=oc, pc=pc, bnk=bnk: v.tensor_copy(
                zhalo[0:pc, oc * 3:oc * 3 + 3], bank(bnk, 3, pc, off=125)))
            ev_zh.append((evt, eve))
        ev_zhalo = ev_zh[-1][1] if ev_zh else None
        if SKIP_PFX:
            ev_zhalo = V([], lambda v: v.memset(zhalo[:, :], 0.0))

        slot_rd = [None] * 6
        zc_rd = [None, None]
        pwe_rd = [None, None]
        zb_free = [None, None]
        prev_use = [evn]
        st_prev = None
        gst_prev = None
        if "slp" in dev:
            G([ev3, ev_wp], lambda g: g.dma_start(
                out=d_slp.ap(), in_=slp[:, :]), ao)
        if "tw" in dev:
            G([ev3, ev_wp], lambda g: g.dma_start(
                out=d_tw.ap()[:, 0:NT], in_=twbias[:, :],
                allow_slow_non_contiguous=True), ao)
            G([ev3, ev_wp], lambda g: g.dma_start(
                out=d_tw.ap()[:, NT:NT + 1], in_=twsc[:, :],
                allow_slow_non_contiguous=True), ao)
            G([ev3, ev_wp], lambda g: g.dma_start(
                out=d_tw.ap()[:, NT + 1:NT + 2], in_=sscb[:, :],
                allow_slow_non_contiguous=True), ao)
        halo_wr = [ev_zhalo] * NZC
        b4_rd = [None]
        prev_vc = [None]

        tb = []
        for i in (range(NT) if os.environ.get("P1CUT") not in ("setup", "pfx", "zh") else []):
            evx = SY([prev_use[0]], lambda sy, i=i: sy.dma_start(
                out=xtm[:, :].rearrange("p (b d) -> p b d", d=D),
                in_=x_in.ap()[128 + i * TT:128 + (i + 1) * TT, :].rearrange(
                    "(b p) d -> p b d", p=128)), ld)
            evn, _ = ln_norm(xtm, lntm, BPT, lnst, [evx] + tb, scp1)
            prev_use[0] = evn
            evfm, _ = tm_to_fm(lntm, BPT,
                               lambda b, dc: lnfm[:, dc * TT + b * 128:
                                                  dc * TT + (b + 1) * 128],
                               [evn, ev_zhalo] + tb)
            if os.environ.get("P1CUT") == "fm":
                tb = [te.ev(), ve.ev(), se.ev()]
                continue

            oc_order = [NZC - 1] + list(range(NZC - 1))
            ev_pw_bf = None
            ev_pwe = {}
            for idx, oc in enumerate(oc_order):
                pc = 32 if oc == NZC - 1 else 128
                zb = idx % 2
                slot = oc % 5

                def t_ip(t, oc=oc, pc=pc, zb=zb):
                    last = None
                    for dc in range(NDC):
                        last = MM(t, 
                            bank(zb, TT, pc),
                            winb[:, dc * C + oc * 128:dc * C + oc * 128 + pc],
                            lnfm[:, dc * TT:(dc + 1) * TT],
                            start=(dc == 0), stop=(dc == NDC - 1))
                    return last
                ev_ip = T([evfm, zb_free[zb]], t_ip)

                def v_ev(v, oc=oc, pc=pc, zb=zb, slot=slot):
                    v.tensor_copy(zsb[0:pc, slot * ZW:slot * ZW + 3],
                                  zhalo[0:pc, oc * 3:oc * 3 + 3])
                    return v.tensor_copy(
                        zsb[0:pc, slot * ZW + 3:slot * ZW + 3 + TT],
                        bank(zb, TT, pc))
                ev_ev = V([ev_ip, slot_rd[slot], halo_wr[oc]], v_ev)
                zb_free[zb] = ev_ev
                if os.environ.get("P1CUT") == "ip":
                    slot_rd[slot] = ev_ev
                    continue

                if oc == NZC - 1:
                    cb = 3
                else:
                    cb = 2

                def t_cv(t, oc=oc, pc=pc, slot=slot, cb=cb):
                    last = None
                    for tap in range(KS):
                        last = MM(t, 
                            bank(cb, TT, pc),
                            convb[0:pc, (tap * NZC + oc) * 128:
                                  (tap * NZC + oc) * 128 + pc],
                            zsb[0:pc, slot * ZW + tap:slot * ZW + tap + TT],
                            start=(tap == 0), stop=(tap == KS - 1))
                    return last
                rd_ev = zc_rd[0] if cb == 2 else b4_rd[0]
                ev_cv = T([ev_ev, rd_ev], t_cv)

                halo_wr[oc] = V([ev_cv], lambda v, oc=oc, pc=pc, slot=slot:
                                v.tensor_copy(
                    zhalo[0:pc, oc * 3:oc * 3 + 3],
                    zsb[0:pc, slot * ZW + TT:slot * ZW + TT + 3]))
                slot_rd[slot] = halo_wr[oc]

                if "z" in dev and i == 0 and oc == 0:
                    G([ev_ev], lambda g: g.dma_start(
                        out=d_z.ap()[:, 0:TT + 3], in_=zsb[:, 0:TT + 3]), ao)
                    G([evfm], lambda g: g.dma_start(
                        out=d_lnfm.ap(), in_=lnfm[:, :]), ao)
                if oc == NZC - 1:
                    def v_sc(v, cb=cb):
                        fence(v, v.tensor_scalar(pwsb[:, 0:TT],
                                                 bank(cb, TT, 32),
                                                 cbsb[0:K, NZC - 1:NZC],
                                                 sscb[:, :],
                                                 Alu.add, Alu.mult))
                        return v.tensor_scalar(pwsb[:, 0:TT], pwsb[:, 0:TT],
                                               20.0, -20.0, Alu.min, Alu.max)
                    ev_sc = V([ev_cv], v_sc)
                    b4_rd[0] = ev_sc
                    ev_p = S([ev_sc], lambda s, i=i: [
                        s.activation(pwsb[:, 0:TT], pwsb[:, 0:TT], Act.Exp,
                                     scale=cst(1.0, K), bias=cst(0.0, K)),
                        s.activation(pwsb[:, TT:2 * TT], iotasb[:, :],
                                     Act.Exp, scale=twsc[:, :],
                                     bias=twbias[:, i:i + 1]),
                    ][-1])
                    if "pw" in dev and i == 0:
                        G([ev_p], lambda g: g.dma_start(
                            out=d_pw.ap(), in_=pwsb[:, :]), ao)
                    ev_pw_bf = V([ev_p, st_prev, ev_stz], lambda v, i=i: v.scalar_tensor_tensor(
                        stsb[0:K, ST_PW * TT:ST_PW * TT + TT],
                        pwsb[:, 0:TT], 1.0, pwsb[:, TT:2 * TT],
                        Alu.mult, Alu.mult,
                        accum_out=partials[0:K, ST_PW * NT + i:
                                           ST_PW * NT + i + 1]))
                    continue

                if oc < NXC:
                    j = oc
                    pb = 4 + (j % 2)

                    ev_pwe[j] = T([ev_pw_bf, pwe_rd[j % 2]],
                                  lambda t, j=j, pb=pb: MM(t, 
                        bank(pb, TT),
                        eselb[:, j * 128:(j + 1) * 128],
                        stsb[0:K, ST_PW * TT:ST_PW * TT + TT],
                        start=True, stop=True))

                    ev_u = V([ev_cv, prev_vc[0]], lambda v, j=j, cb=cb:
                             v.tensor_scalar_add(usb[:, :], bank(cb, TT),
                                                 cbsb[:, j:j + 1]))

                    def s_chain(s, j=j, cb=cb):
                        xv = usb[:, :]
                        s.activation(scp1[:, 0:TT], xv, Act.Identity,
                                     scale=theta2p[:, j:j + 1],
                                     bias=cst(MAGIC))
                        s.activation(scp1[:, 0:TT], scp1[:, 0:TT], Act.Identity,
                                     scale=cst(1.0), bias=cst(-MAGIC))
                        return s.activation(scp1[:, TT:2 * TT] if False else
                                            scp1[:, 0:TT], scp1[:, 0:TT],
                                            Act.Identity, scale=cst(TWO_PI),
                                            bias=cst(0.0))
                    ev_s1 = S([ev_u], s_chain)

                    def v_r(v, j=j, cb=cb):
                        xv = usb[:, :]
                        v.scalar_tensor_tensor(chainr[:, 0:TT], xv,
                                               thetasb[:, j:j + 1],
                                               scp1[:, 0:TT],
                                               Alu.mult, Alu.subtract)
                        return v.scalar_tensor_tensor(
                            chainr[:, TT:2 * TT], chainr[:, 0:TT], -1.0,
                            chainr[:, 0:TT], Alu.mult, Alu.max)
                    ev_r = V([ev_s1], v_r)

                    def s_tr(s, j=j, cb=cb):
                        xv = usb[:, :]
                        s.activation(sincos[:, 0:TT], chainr[:, 0:TT],
                                     Act.Sin, scale=cst(1.0), bias=cst(0.0))
                        s.activation(sincos[:, TT:2 * TT],
                                     chainr[:, TT:2 * TT], Act.Sin,
                                     scale=cst(-1.0), bias=cst(math.pi / 2))
                        return s.activation(scp1[:, 0:TT], xv, Act.Identity,
                                            scale=wpoly[:, 2:3],
                                            bias=wpoly[:, 1:2])
                    ev_s2 = S([ev_r], s_tr)

                    def v_chain(v, j=j, cb=cb, i=i):
                        xv = usb[:, :]
                        v.tensor_tensor(chainr[:, 0:TT], scp1[:, 0:TT], xv,
                                        Alu.mult)
                        v.tensor_scalar_add(polyb[:, 0:TT], chainr[:, 0:TT],
                                            wpoly[:, 0:1])
                        v.tensor_tensor(remb[:, 0:TT], polyb[:, 0:TT],
                                        sincos[:, TT:2 * TT], Alu.mult)
                        v.tensor_tensor(remb[:, TT:2 * TT], polyb[:, 0:TT],
                                        sincos[:, 0:TT], Alu.mult)
                        v.scalar_tensor_tensor(
                            stsb[:, st_re(j) * TT:(st_re(j) + 1) * TT],
                            remb[:, 0:TT], 1.0, bank(4 + (j % 2), TT),
                            Alu.mult, Alu.mult,
                            accum_out=partials[:, st_re(j) * NT + i:
                                               st_re(j) * NT + i + 1])
                        return v.scalar_tensor_tensor(
                            stsb[:, st_im(j) * TT:(st_im(j) + 1) * TT],
                            remb[:, TT:2 * TT], 1.0, bank(4 + (j % 2), TT),
                            Alu.mult, Alu.mult,
                            accum_out=partials[:, st_im(j) * NT + i:
                                               st_im(j) * NT + i + 1])
                    ev_vc = V([ev_s2, ev_pwe[j]], v_chain)
                    zc_rd[0] = ev_u
                    pwe_rd[j % 2] = ev_vc
                    prev_vc[0] = ev_vc
                else:
                    jg = oc - NXC
                    ev_g = S([ev_cv, gst_prev], lambda s, jg=jg, cb=cb, oc=oc:
                             s.activation(
                        gtsb[:, (jg % 8) * TT:(jg % 8 + 1) * TT],
                        bank(cb, TT), Act.Silu, scale=cst(1.0),
                        bias=cbsb[:, oc:oc + 1]))
                    zc_rd[0] = ev_g
                    if jg % 8 == 7:
                        if "gst" in dev:
                            G([ev_g], lambda g, i=i, jg=jg: g.dma_start(
                                out=d_gst.ap()[i, :, (jg - 7) * TT:
                                               (jg + 1) * TT],
                                in_=gtsb[:, :]), ao)
                        gst_prev = SY([ev_g], lambda sy, i=i, jg=jg:
                                      sy.dma_start(
                            out=gstash.ap()[i, :, (jg - 7) * TT:
                                            (jg + 1) * TT],
                            in_=gtsb[:, :]), so)

            last_v = ve.ev()
            last_s = se.ev()
            ev_st = SY([last_v, st_prev], lambda sy, i=i: sy.dma_start(
                out=stash.ap()[i, :, :], in_=stsb[:, :]), so)
            st_prev = ev_st
            tb = [te.ev(), ve.ev(), se.ev()]
            if "stash" in dev:
                st_prev = G([last_v, st_prev], lambda g, i=i: g.dma_start(
                    out=d_stash.ap()[i, :, :], in_=stsb[:, :]), ao)

        if "part" in dev:
            G([ve.ev()], lambda g: g.dma_start(
                out=d_part.ap(), in_=partials[:, :]), ao)


    def barrier():
        evs = [te.ev(), ve.ev(), se.ev(), ge.ev(), (S_ld, ld.n),
               (S_so, so.n), (S_wl, wl.n), (S_ao, ao.n)]
        outs = []
        outs.append(T(evs, lambda t: t.nop()))
        outs.append(V(evs, lambda v: v.engine_nop()))
        outs.append(S(evs, lambda s: s.nop()))
        outs.append(G(evs, lambda g: g.nop()))
        outs.append(SY(evs, lambda sy: sy.nop(), None))
        return [o for o in outs if o is not None]

    if (1 in phases) and (2 in phases):
        ev_tot0 = V([ve.ev(), se.ev()], lambda v: v.tensor_reduce(
            ccsb[:, :],
            partials[:, :].rearrange("p (s t) -> p s t", t=NT),
            AX, Alu.add))
        ev_tot = V([ev_tot0], lambda v: v.tensor_scalar_mul(
            ccsb[:, :], ccsb[:, :], flagsb[:, :]))
        ev_ci = G([ev_tot], lambda g: g.dma_start(out=cc_i.ap(),
                                                  in_=ccsb[:, :]), ao)
        ev_cc = emit("gpsimd", ccc, [ev_ci], lambda g: g.collective_compute(
            "AllReduce", Alu.add,
            replica_groups=[[2 * p, 2 * p + 1] for p in range(4)],
            ins=[cc_i.ap()], outs=[cc_o.ap()]))
        ev_co = G([ev_cc], lambda g: g.dma_start(out=ccsb[:, :],
                                                 in_=cc_o.ap()), wl)
        def v_cin(v):
            fence(v, v.tensor_scalar_mul(smll[:, 13:14], flagsb[:, :], -1.0))
            fence(v, v.tensor_scalar_add(smll[:, 13:14], smll[:, 13:14],
                                         1.0))
            return v.tensor_scalar_mul(carry[:, :], ccsb[:, :],
                                       smll[:, 13:14])
        ev_cin = V([ev_co], v_cin)
        bar1 = barrier()
        if "hbm" in dev:
            G(bar1, lambda g: g.dma_start(out=d_stash.ap(),
                                          in_=stash.ap()), ao)
            G(bar1, lambda g: g.dma_start(out=d_gst.ap(),
                                          in_=gstash.ap()), ao)
            G(bar1, lambda g: g.dma_start(out=d_part.ap(),
                                          in_=partials[:, :]), ao)
            G(bar1, lambda g: g.dma_start(out=d_carry.ap()[:, 0:NZC],
                                          in_=carry[:, :]), ao)
            G(bar1, lambda g: g.dma_start(out=d_carry.ap()[:, NZC:2 * NZC],
                                          in_=ccsb[:, :]), ao)

    for c in reversed(p1_cm):
        c.__exit__(None, None, None)

    if 2 in phases:
        p2_cm = [
            nc.sbuf_tensor("woutb", [128, 16 * D], BF16),
            nc.sbuf_tensor("wcb", [128, NDC * ES], BF16),
            nc.sbuf_tensor("bdre", [128, 128], BF16),
            nc.sbuf_tensor("bdim", [128, 128], BF16),
            nc.sbuf_tensor("wresb", [128, 64], F32),
            nc.sbuf_tensor("wimsb", [128, 64], F32),
            nc.sbuf_tensor("nscsb", [128, 1], F32),
            nc.sbuf_tensor("qselb", [128, NXC * 32], BF16),
            nc.sbuf_tensor("bcsb2", [ES, 1], F32),
            nc.sbuf_tensor("tiesb", [128, GRP], F32),
            nc.sbuf_tensor("stsb2", [128, NZC * TT], BF16),
            nc.sbuf_tensor("gtsb2", [128, NXC * TT], BF16),
            nc.sbuf_tensor("numrot", [128, 4 * TT], BF16),
            nc.sbuf_tensor("sqrot", [128, 2 * TT], BF16),
            nc.sbuf_tensor("densb", [32, TT], F32),
            nc.sbuf_tensor("invd", [32, TT], F32),
            nc.sbuf_tensor("snormb", [32, TT], BF16),
            nc.sbuf_tensor("ypsb", [128, NXC * TT], BF16),
            nc.sbuf_tensor("y1b", [128, 2 * TT], BF16),
            nc.sbuf_tensor("ygsb", [128, NXC * TT], BF16),
            nc.sbuf_tensor("xtm2", [128, BPT * D], F32),
            nc.sbuf_tensor("x1sb", [128, BPT * D], F32),
            nc.sbuf_tensor("ln2tm", [128, BPT * D], BF16),
            nc.sbuf_tensor("lnfm2", [128, NDC * TT], BF16),
            nc.sbuf_tensor("lgsb", [128, TT], F32),
            nc.sbuf_tensor("gmx", [128, GPT], F32),
            nc.sbuf_tensor("ohsb", [128, SEG_T], BF16),
            nc.sbuf_tensor("lnst2", [128, 16], F32),
            nc.sbuf_tensor("scp2", [128, max(D, 2 * TT)], F32),
            nc.sbuf_tensor("t2sb", [32, 2 * TT], F32),
        ]
        (woutb, wcb, bdre, bdim, wresb, wimsb, nscsb, qselb, bcsb2, tiesb,
         stsb2, gtsb2, numrot, sqrot, densb, invd, snormb, ypsb, y1b, ygsb,
         xtm2, x1sb, ln2tm, lnfm2, lgsb, gmx, ohsb, lnst2, scp2,
         t2sb) = [c.__enter__() for c in p2_cm]

        def g2_loads(g, inc):
            n = 0
            for i16 in range(16):
                inc(g.dma_start(out=woutb[:, i16 * D:(i16 + 1) * D],
                                in_=wout_in.ap()[i16, :, :]))
                n += 1
            for dc in range(NDC):
                inc(g.dma_start(out=wcb[:, dc * ES:(dc + 1) * ES],
                                in_=wc_in.ap()[dc, :, :]))
                n += 1
            for dst, srcp in [
                (wresb[:, :], wre_in.ap()), (wimsb[:, :], wim_in.ap()),
                (nscsb[:, :], nsc_in.ap()), (qselb[:, :], qsel_in.ap()),
                (bcsb2[:, :], bc_in.ap()), (tiesb[:, :], tie_in.ap()),
            ]:
                inc(g.dma_start(out=dst, in_=srcp))
                n += 1
            return n
        ev_g2 = emit_multi("gpsimd", wl, bar1, g2_loads, 16 + NDC + 6)

        ev_bdz = V(bar1, lambda v: [v.memset(bdre[:, :], 0.0),
                                    v.memset(bdim[:, :], 0.0)][-1])
        ev_bd = V([ev_g2, ev_bdz], lambda v: [
            v.tensor_scalar_mul(bdre[0:64, 0:64], wresb[0:64, :],
                                nscsb[0:64, :]),
            v.tensor_scalar_mul(bdre[64:128, 64:128], wresb[64:128, :],
                                nscsb[0:64, :]),
            v.tensor_scalar_mul(bdim[0:64, 0:64], wimsb[0:64, :],
                                nscsb[64:128, :]),
            v.tensor_scalar_mul(bdim[64:128, 64:128], wimsb[64:128, :],
                                nscsb[64:128, :]),
        ][-1])

        ev_st_rd = None
        ev_gt_rd = None
        slot_free = [None] * 4
        sq_free = [None, None]
        yp_free = [None, None]
        se_free = [None, None]
        pout_free = [None, None]
        ev_oh_dma = None
        ev_q_rd = None
        tb2 = []
        x1_free = []
        ln2_free = []

        for i in range(NT):
            seg = i // SEG_TILES
            tis = i % SEG_TILES

            ev_sld = SY([ev_st_rd] + bar1, lambda sy, i=i: sy.dma_start(
                out=stsb2[:, :], in_=stash.ap()[i, :, :]), ld)
            ev_gld = SY([ev_gt_rd], lambda sy, i=i: sy.dma_start(
                out=gtsb2[:, :], in_=gstash.ap()[i, :, :]), ld)
            ev_xld = SY([pout_free[0], pout_free[1]],
                        lambda sy, i=i: sy.dma_start(
                out=xtm2[:, :].rearrange("p (b d) -> p b d", d=D),
                in_=x_in.ap()[128 + i * TT:128 + (i + 1) * TT, :].rearrange(
                    "(b p) d -> p b d", p=128)), ld)

            if i > 0:
                ev_cu = V([], lambda v, i=i: v.tensor_add(
                    carry[:, :], carry[:, :],
                    partials[:, :].rearrange("p (s t) -> p t s",
                                             t=NT)[:, i - 1, :]))
            else:
                ev_cu = ev_cin if (1 in phases) else None

            last_scan = {}
            ev_ypre = {}
            for j in range(NXC):
                sre_slot = (2 * j) % 4
                sim_slot = (2 * j + 1) % 4

                sfre = slot_free[sre_slot] or []
                sfim = slot_free[sim_slot] or []
                ev_s_re = V([ev_sld, ev_cu] + sfre + tb2,
                            lambda v, j=j, s=sre_slot: v.tensor_tensor_scan(
                    numrot[:, s * TT:(s + 1) * TT],
                    stsb2[:, st_re(j) * TT:(st_re(j) + 1) * TT],
                    stsb2[:, st_re(j) * TT:(st_re(j) + 1) * TT],
                    carry[:, st_re(j):st_re(j) + 1], Alu.add, Alu.bypass))
                ev_s_im = V(sfim,
                            lambda v, j=j, s=sim_slot: v.tensor_tensor_scan(
                    numrot[:, s * TT:(s + 1) * TT],
                    stsb2[:, st_im(j) * TT:(st_im(j) + 1) * TT],
                    stsb2[:, st_im(j) * TT:(st_im(j) + 1) * TT],
                    carry[:, st_im(j):st_im(j) + 1], Alu.add, Alu.bypass))

                ev_q_re = S([ev_s_re, sq_free[0]], lambda s, j=j,
                            ss=sre_slot: s.activation(
                    sqrot[:, 0:TT], numrot[:, ss * TT:(ss + 1) * TT],
                    Act.Square, scale=cst(1.0), bias=cst(0.0)))
                ev_q_im = S([ev_s_im, sq_free[1]], lambda s, j=j,
                            ss=sim_slot: s.activation(
                    sqrot[:, TT:2 * TT], numrot[:, ss * TT:(ss + 1) * TT],
                    Act.Square, scale=cst(1.0), bias=cst(0.0)))

                ev_qm_re = T([ev_q_re, ev_q_rd], lambda t, j=j: MM(t, 
                    bank(0, TT, 32), qselb[:, j * 32:(j + 1) * 32],
                    sqrot[:, 0:TT], start=(j == 0), stop=False))
                ev_qm_im = T([ev_q_im], lambda t, j=j: MM(t, 
                    bank(0, TT, 32), qselb[:, j * 32:(j + 1) * 32],
                    sqrot[:, TT:2 * TT], start=False, stop=(j == NXC - 1)))
                sq_free[0] = ev_qm_re
                sq_free[1] = ev_qm_im

                ypb = 1 + (j % 2)
                ev_yp = T([ev_s_re, ev_s_im, ev_bd, yp_free[j % 2]],
                          lambda t, j=j, ypb=ypb, sr=sre_slot,
                          si=sim_slot: [
                    MM(t, bank(ypb, TT), bdre[:, :],
                             numrot[:, sr * TT:(sr + 1) * TT],
                             start=True, stop=False),
                    MM(t, bank(ypb, TT), bdim[:, :],
                             numrot[:, si * TT:(si + 1) * TT],
                             start=False, stop=True),
                ][-1])
                ev_ype = V([ev_yp], lambda v, j=j, ypb=ypb: v.tensor_copy(
                    ypsb[:, j * TT:(j + 1) * TT], bank(ypb, TT)))
                yp_free[j % 2] = ev_ype
                slot_free[sre_slot] = [ev_ype, ev_q_re]
                slot_free[sim_slot] = [ev_ype, ev_q_im]
                ev_ypre[j] = ev_ype

            ev_den = V([ev_sld, ev_cu], lambda v: v.tensor_tensor_scan(
                densb[:, :], stsb2[0:K, ST_PW * TT:ST_PW * TT + TT],
                stsb2[0:K, ST_PW * TT:ST_PW * TT + TT],
                carry[0:K, ST_PW:ST_PW + 1], Alu.add, Alu.bypass))
            def v_inv(v):
                fence(v, v.tensor_scalar_max(invd[:, :], densb[:, :], 1e-4))
                return v.reciprocal(invd[:, :], invd[:, :])
            ev_inv = V([ev_den], v_inv)
            def v_t2(v):
                fence(v, v.tensor_tensor(t2sb[:, 0:TT], bank(0, TT, 32),
                                         invd[:, :], Alu.mult))
                return v.tensor_tensor(t2sb[:, 0:TT], t2sb[:, 0:TT],
                                       invd[:, :], Alu.mult)
            ev_t2 = V([ev_inv, sq_free[1]], v_t2)
            ev_q_rd = ev_t2
            ev_rs = S([ev_t2], lambda s: [
                s.activation(t2sb[:, TT:2 * TT], t2sb[:, 0:TT], Act.Ln,
                             scale=cst(1.0 / (2.0 * Hd), 32),
                             bias=cst(1e-5, 32)),
                s.activation(t2sb[:, TT:2 * TT], t2sb[:, TT:2 * TT],
                             Act.Exp, scale=cst(-0.5, 32), bias=cst(0.0, 32)),
            ][-1])
            ev_sn = V([ev_rs], lambda v: v.tensor_tensor(
                snormb[:, :], t2sb[:, TT:2 * TT], invd[:, :], Alu.mult))

            ev_yg = {}
            for j in range(NXC):
                seb = 3
                ev_se = T([ev_sn, se_free[0]], lambda t, j=j, seb=seb:
                          MM(t, bank(seb, TT), eselb[:, j * 128:
                                                        (j + 1) * 128],
                                   snormb[:, :], start=True, stop=True))
                ev_y1 = V([ev_se, ev_ypre[j]], lambda v, j=j, seb=seb:
                          v.tensor_tensor(y1b[:, (j % 2) * TT:
                                              (j % 2 + 1) * TT],
                                          ypsb[:, j * TT:(j + 1) * TT],
                                          bank(seb, TT), Alu.mult))
                ev_yg[j] = V([ev_y1, ev_gld], lambda v, j=j: v.tensor_tensor(
                    ygsb[:, j * TT:(j + 1) * TT],
                    y1b[:, (j % 2) * TT:(j % 2 + 1) * TT],
                    gtsb2[:, j * TT:(j + 1) * TT], Alu.mult))
                se_free[0] = ev_y1
            ev_st_rd = ve.ev()
            ev_gt_rd = ve.ev()

            ev_x1 = {}
            for b in range(BPT):
                evs_po = [ev_yg[j] for j in range(NXC)]
                def t_op(t, b=b):
                    last = None
                    for ch in range(16):
                        for hf in range(2):
                            last = MM(t, 
                                bank(4 + hf, 512),
                                ygsb[:, ch * TT + b * 128:
                                     ch * TT + (b + 1) * 128],
                                woutb[:, ch * D + hf * 512:
                                      ch * D + hf * 512 + 512],
                                start=(ch == 0), stop=(ch == 15))
                    return last
                ev_po = T(evs_po + [pout_free[0], pout_free[1], ev_g2],
                          t_op)
                ev_r = V([ev_po, ev_xld] + x1_free, lambda v, b=b: [
                    v.tensor_tensor(x1sb[:, b * D:b * D + 512],
                                    bank(4, 512),
                                    xtm2[:, b * D:b * D + 512], Alu.add),
                    v.tensor_tensor(x1sb[:, b * D + 512:(b + 1) * D],
                                    bank(5, 512),
                                    xtm2[:, b * D + 512:(b + 1) * D],
                                    Alu.add),
                ][-1])
                pout_free[0] = ev_r
                pout_free[1] = ev_r
                ev_x1[b] = ev_r
            ev_x1_all = [ev_x1[b] for b in range(BPT)]
            ev_xst = SY(ev_x1_all, lambda sy, i=i: sy.dma_start(
                out=out.ap()[i * TT:(i + 1) * TT, :].rearrange(
                    "(b p) d -> p b d", p=128),
                in_=x1sb[:, :].rearrange("p (b d) -> p b d", d=D)), so)
            if "x1" in dev:
                G(ev_x1_all, lambda g, i=i: g.dma_start(
                    out=d_x1.ap()[i * TT:(i + 1) * TT, :].rearrange(
                        "(b p) d -> p b d", p=128),
                    in_=x1sb[:, :].rearrange("p (b d) -> p b d", d=D)), ao)

            ev_ln2, _ = ln_norm(x1sb, ln2tm, BPT, lnst2,
                                ev_x1_all + ln2_free, scp2)
            ev_l2st = SY([ev_ln2], lambda sy, i=i: sy.dma_start(
                out=ln2st.ap()[i, :, :], in_=ln2tm[:, :]), so)
            ev_fm2, _ = tm_to_fm(ln2tm, BPT,
                                 lambda b, dc: lnfm2[:, dc * TT + b * 128:
                                                     dc * TT + (b + 1) * 128],
                                 [ev_ln2] + ln2_free)

            def t_lg(t):
                last = None
                for dc in range(NDC):
                    last = MM(t, bank(0, TT), wcb[:, dc * ES:
                                                     (dc + 1) * ES],
                                    lnfm2[:, dc * TT:(dc + 1) * TT],
                                    start=(dc == 0), stop=(dc == NDC - 1))
                return last
            ev_lg = T([ev_fm2, ev_q_rd, ev_g2], t_lg)
            ev_lge = V([ev_lg], lambda v: v.scalar_tensor_tensor(
                lgsb[:, :].rearrange("p (g t) -> p g t", t=GRP),
                bank(0, TT).rearrange("p (g t) -> p g t", t=GRP),
                bcsb2[:, :],
                tiesb[:, :].unsqueeze(1).broadcast_to([128, GPT, GRP]),
                Alu.add, Alu.add))
            ev_q_rd = ev_lge
            ev_mx = V([ev_lge], lambda v: v.tensor_reduce(
                gmx[:, :], lgsb[:, :].rearrange("p (g t) -> p g t", t=GRP),
                AX, Alu.max))
            ev_oh = V([ev_mx, ev_oh_dma], lambda v, tis=tis: v.tensor_tensor(
                ohsb[:, tis * TT:(tis + 1) * TT].rearrange(
                    "p (g t) -> p g t", t=GRP),
                lgsb[:, :].rearrange("p (g t) -> p g t", t=GRP),
                gmx[:, :].unsqueeze(2).broadcast_to([128, GPT, GRP]),
                Alu.is_equal))

            if tis == SEG_TILES - 1:
                ev_oh_dma = SY([ev_oh], lambda sy, seg=seg: sy.dma_start(
                    out=ohst.ap()[seg, :, :], in_=ohsb[:, :]), so)
                if "oh" in dev:
                    G([ev_oh], lambda g, seg=seg: g.dma_start(
                        out=d_oh.ap()[seg, :, :], in_=ohsb[:, :]), ao)
            x1_free = [ se.ev(), ev_xst ]
            ln2_free = [ev_l2st, ev_lg]
            tb2 = [te.ev(), ve.ev(), se.ev()]

        for c in reversed(p2_cm):
            c.__exit__(None, None, None)


    if 3 in phases:
        bar2 = barrier()
        p3_cm = [
            nc.sbuf_tensor("w2b", [128, NPACK * D], BF16),
            nc.sbuf_tensor("w1buf", [128, 2 * ES * EF], BF16),
            nc.sbuf_tensor("ln2seg", [128, SEG_TILES * BPT * D], BF16),
            nc.sbuf_tensor("permseg", [128, BLK_SEG * 512], BF16),
            nc.sbuf_tensor("ohsegb", [128, SEG_T], BF16),
            nc.sbuf_tensor("ohT", [128, 128], BF16),
            nc.sbuf_tensor("xdc", [128, BLK_SEG * 512], BF16),
            nc.sbuf_tensor("innersb", [128, NPACK * GSEG], BF16),
            nc.sbuf_tensor("padsb", [128, 2 * MSL], BF16),
            nc.sbuf_tensor("outsb", [128, 2 * D], BF16),
            nc.sbuf_tensor("outg", [128, 2 * D], BF16),
            nc.sbuf_tensor("b1pb", [128, NPACK], F32),
            nc.sbuf_tensor("b2pb", [128, D], BF16),
            nc.sbuf_tensor("bmaskb", [128, 4], BF16),
            nc.sbuf_tensor("x2c", [128, 2 * D], F32),
        ]
        (w2b, w1buf, ln2seg, permseg, ohsegb, ohT, xdc, innersb, padsb,
         outsb, outg, b1pb, b2pb, bmaskb, x2c) = \
            [c.__enter__() for c in p3_cm]

        def g3_loads(g, inc):
            n = 0
            for pk in range(NPACK):
                inc(g.dma_start(out=w2b[:, pk * D:(pk + 1) * D],
                                in_=w2p_in.ap()[pk, :, :]))
                n += 1
            for dst, srcp in [(b1pb[:, :], b1p_in.ap()),
                              (b2pb[:, :], b2p_in.ap()),
                              (bmaskb[:, :], bmask_in.ap())]:
                inc(g.dma_start(out=dst, in_=srcp))
                n += 1
            return n
        ev_g3 = emit_multi("gpsimd", wl, bar2, g3_loads, NPACK + 3)

        ev_perm_rd = None
        ev_oh_rd = None
        ev_w1_rd = [None, None]
        ev_xdc_rd = None
        ev_inner_rd = None
        ev_outsb_rd = None
        ev_outg_rd = [None, None]
        ev_x2c_rd = [None, None]
        ev_px2_rd = [None, None]

        for seg in range(NSEG):
            ev_ohl = SY([ev_oh_rd] + bar2, lambda sy, seg=seg: sy.dma_start(
                out=ohsegb[:, :], in_=ohst.ap()[seg, :, :]), ld)
            def l2l(sy, inc, seg=seg):
                for tis in range(SEG_TILES):
                    inc(sy.dma_start(
                        out=ln2seg[:, tis * BPT * D:(tis + 1) * BPT * D],
                        in_=ln2st.ap()[seg * SEG_TILES + tis, :, :]))
                return SEG_TILES
            ev_l2l = emit_multi("sync", ld, [ev_perm_rd] + bar2, l2l,
                                SEG_TILES)

            ev_pm = []
            for blk in range(BLK_SEG):
                evt = T([ev_ohl,
                         ev_pm[-2][0] if len(ev_pm) >= 2 else None],
                        lambda t, blk=blk: t.transpose(
                    pbf_slot(blk % 4), ohsegb[:, blk * 128:(blk + 1) * 128],
                    identb[:, :]))
                eve = V([evt, ev_perm_rd], lambda v, blk=blk: v.tensor_copy(
                    ohT[:, :], pbf_slot(blk % 4)))
                evz = V([eve], lambda v, blk=blk: v.memset(
                    permseg[:, blk * 512:(blk + 1) * 512], 0.0))
                def cpy(v, blk=blk):
                    last = None
                    for g_ in range(GPB):
                        last = v.tensor_copy(
                            permseg[32 * g_:32 * (g_ + 1),
                                    blk * 512 + g_ * 128:
                                    blk * 512 + (g_ + 1) * 128],
                            ohT[32 * g_:32 * (g_ + 1), :])
                    return last
                evc = V([evz], cpy)
                ev_pm.append((evc, evt))
            ev_perm = ev_pm[-1][0]

            if "perm" in dev and seg == 0:
                G([ev_perm, ev_l2l], lambda g: g.dma_start(
                    out=d_perm.ap(), in_=permseg[:, :]), ao)
                G([ev_perm, ev_l2l], lambda g: g.dma_start(
                    out=d_l2s.ap(), in_=ln2seg[:, :]), ao)
            ev_w1mm_last = None
            for dc in range(NDC):
                buf = dc % 2
                ev_wl1 = G([ev_g3, ev_w1_rd[buf]], lambda g, dc=dc, buf=buf:
                           g.dma_start(out=w1buf[:, buf * ES * EF:
                                                 (buf + 1) * ES * EF],
                                       in_=w1p_in.ap()[dc, :, :]), wl)
                ev_ga = {}
                for blk in range(BLK_SEG):
                    tis, b = blk // BPT, blk % BPT
                    evg = T([ev_perm, ev_l2l,
                             ev_ga[blk - 1][0] if blk - 1 in ev_ga else
                             ev_xdc_rd,
                             None if (blk or dc) else ev_px2_rd[0],
                             None if (blk or dc) else ev_x2c_rd[0],
                             None if (blk or dc) else ev_x2c_rd[1]],
                            lambda t, tis=tis, b=b, dc=dc, blk=blk: MM(t, 
                        bank(5, 512),
                        ln2seg[:, (tis * BPT + b) * D + dc * 128:
                               (tis * BPT + b) * D + (dc + 1) * 128],
                        permseg[:, blk * 512:(blk + 1) * 512],
                        start=True, stop=True))
                    eve = V([evg], lambda v, blk=blk: v.tensor_copy(
                        xdc[:, blk * 512:(blk + 1) * 512], bank(5, 512)))
                    ev_ga[blk] = (eve, evg)
                ev_xall = ev_ga[BLK_SEG - 1][0]

                if "xdc" in dev and seg == 0 and dc == 0:
                    G([ev_xall], lambda g: g.dma_start(
                        out=d_xdc.ap(), in_=xdc[:, :]), ao)
                for es in range(ES):
                    esl = es % 4
                    pk = es // 4
                    gates = [ev_xall, ev_wl1]
                    if dc == 0 and es == 0:
                        gates += [ev_inner_rd, ev_px2_rd[0]]
                    ev_w1mm_last = T(
                        gates,
                        lambda t, es=es, esl=esl, pk=pk, dc=dc, buf=buf:
                        MM(t, 
                            pall[32 * esl:32 * (esl + 1),
                                 pk * GSEG:(pk + 1) * GSEG],
                            w1buf[:, buf * ES * EF + es * EF:
                                  buf * ES * EF + (es + 1) * EF],
                            xdc[:, :].rearrange(
                                "p (blk g es) -> p es blk g",
                                g=GPB, es=128)[:, es, :, :],
                            start=(dc == 0 and
                                   (pk * GSEG * 4) % 2048 == 0),
                            stop=(dc == NDC - 1),
                            tile_position=(0, 32 * esl)))
                ev_w1_rd[buf] = ev_w1mm_last
                ev_xdc_rd = ev_w1mm_last

            def s_ie(s):
                last = None
                for pk in range(NPACK):
                    last = s.activation(
                        innersb[:, pk * GSEG:(pk + 1) * GSEG],
                        pall[:, pk * GSEG:(pk + 1) * GSEG], Act.Relu,
                        scale=cst(1.0), bias=b1pb[:, pk:pk + 1])
                return last
            ev_ie = S([ev_w1mm_last, ev_g3], s_ie)

            if "inner" in dev and seg == 0:
                G([ev_ie], lambda g: g.dma_start(
                    out=d_inner.ap(), in_=innersb[:, :]), ao)
            ev_obst = {}
            for sub in range(NSUB):
                ev_w2 = {}
                for pk in range(NPACK):
                    pr = pk % 2
                    bnk0 = 2 * pr
                    ev_pb = V([ev_ie, ev_w2.get(pk - 2, (None,) * 3)[0]],
                              lambda v, pk=pk, sub=sub, pr=pr:
                              v.tensor_tensor(
                        padsb[:, pr * MSL:(pr + 1) * MSL].rearrange(
                            "p (e g) -> p e g", e=4),
                        innersb[:, pk * GSEG + sub * SUB:
                                pk * GSEG + (sub + 1) * SUB].unsqueeze(
                            1).broadcast_to([128, 4, SUB]),
                        bmaskb[:, :].unsqueeze(2).broadcast_to(
                            [128, 4, SUB]),
                        Alu.mult))

                    def t_w2(t, pk=pk, pr=pr, bnk0=bnk0):
                        last = None
                        for hf in range(2):
                            last = MM(t, 
                                pall[0:MSL, (bnk0 + hf) * 512:
                                     (bnk0 + hf) * 512 + 512],
                                padsb[:, pr * MSL:(pr + 1) * MSL],
                                w2b[:, pk * D + hf * 512:
                                    pk * D + hf * 512 + 512],
                                start=True, stop=True)
                        return last
                    evm = T([ev_pb, ev_g3,
                             ev_w2.get(pk - 2, (None,) * 3)[0],
                             ev_outsb_rd if pk < 2 else None,
                             ev_ie if pk < 2 else None], t_w2)

                    prev_st = ev_obst.get((sub, pk - 2)) or \
                        ev_obst.get((sub - 1, pk + NPACK - 2))
                    if pr == 0:
                        def v_ev(v, pk=pk, bnk0=bnk0, pr=pr):
                            v.tensor_copy(
                                outsb[0:MSL, pr * D:pr * D + 512],
                                pall[0:MSL, bnk0 * 512:bnk0 * 512 + 512])
                            return v.tensor_copy(
                                outsb[0:MSL, pr * D + 512:(pr + 1) * D],
                                pall[0:MSL, (bnk0 + 1) * 512:
                                     (bnk0 + 1) * 512 + 512])
                        eve = V([evm, prev_st], v_ev)
                    else:
                        def s_ev(s, pk=pk, bnk0=bnk0, pr=pr):
                            s.activation(
                                outsb[0:MSL, pr * D:pr * D + 512],
                                pall[0:MSL, bnk0 * 512:bnk0 * 512 + 512],
                                Act.Identity, scale=cst(1.0, MSL),
                                bias=cst(0.0, MSL))
                            return s.activation(
                                outsb[0:MSL, pr * D + 512:(pr + 1) * D],
                                pall[0:MSL, (bnk0 + 1) * 512:
                                     (bnk0 + 1) * 512 + 512],
                                Act.Identity, scale=cst(1.0, MSL),
                                bias=cst(0.0, MSL))
                        eve = S([evm, prev_st], s_ev)
                    ev_st_ob = emit("sync", sq, [eve],
                                    lambda sy, pk=pk, pr=pr: sy.dma_start(
                        out=ob.ap()[pk // 2, pk % 2].rearrange(
                            "e g d -> (e g) d"),
                        in_=outsb[0:MSL, pr * D:(pr + 1) * D]))
                    ev_obst[(sub, pk)] = ev_st_ob
                    ev_w2[pk] = (eve, evm, ev_st_ob)
                ev_ob_all = [ev_obst[(sub, NPACK - 2)],
                             ev_obst[(sub, NPACK - 1)]]

                ev_og_last = None
                for bls in range(SUB // GPB):
                    blk = sub * (SUB // GPB) + bls

                    def t_b2(t, blk=blk):
                        last = None
                        for hf in range(2):
                            last = MM(t,
                                pall[:, (4 + hf) * 512:
                                     (4 + hf) * 512 + 512],
                                ohsegb[:, blk * 128:(blk + 1) * 128],
                                b2pb[:, hf * 512:(hf + 1) * 512],
                                start=True, stop=False)
                        return last
                    evb2 = T([ev_g3, ev_px2_rd[0], ev_ie], t_b2)

                    gev = []
                    for gl in range(GPB):
                        g_loc = bls * GPB + gl
                        g_glob = sub * SUB + g_loc
                        slot = gl % 2

                        ev_og = SY(ev_ob_all + [ev_outg_rd[slot]],
                                   lambda sy, g_loc=g_loc, slot=slot:
                                   sy.dma_start(
                            out=outg[:, slot * D:(slot + 1) * D],
                            in_=ob.ap()[:, :, :, g_loc, :].rearrange(
                                "q p e d -> (q p e) d")), ld)
                        ev_og_last = ev_og
                        if "outg" in dev and seg == 0 and sub == 0 and g_glob == 0:
                            G([ev_og], lambda g: g.dma_start(
                                out=d_outg.ap(),
                                in_=outg[:, 0 * D:1 * D]), ao)

                        def t_sc(t, g_glob=g_glob, gl=gl, slot=slot):
                            last = None
                            for hf in range(2):
                                last = MM(t,
                                    pall[32 * gl:32 * (gl + 1),
                                         (4 + hf) * 512:
                                         (4 + hf) * 512 + 512],
                                    ohsegb[:, g_glob * 32:
                                           (g_glob + 1) * 32],
                                    outg[:, slot * D + hf * 512:
                                         slot * D + hf * 512 + 512],
                                    start=False, stop=(gl == GPB - 1),
                                    tile_position=(0, 32 * gl))
                            return last
                        evsc = T([ev_og, evb2], t_sc)
                        ev_outg_rd[slot] = evsc
                        gev.append(evsc)

                    evx2 = V(gev + [ev_x2c_rd[bls % 2]],
                             lambda v, bls=bls: [
                        v.tensor_copy(x2c[:, (bls % 2) * D:
                                          (bls % 2) * D + 512],
                                      bank(4, 512)),
                        v.tensor_copy(x2c[:, (bls % 2) * D + 512:
                                          (bls % 2 + 1) * D],
                                      bank(5, 512)),
                    ][-1])
                    ev_px2_rd[0] = evx2
                    row0 = seg * SEG_T + blk * 128
                    ev_ac = G([evx2], lambda g, row0=row0, bls=bls:
                              g.dma_start(
                        out=out.ap()[row0:row0 + 128, :],
                        in_=x2c[:, (bls % 2) * D:(bls % 2 + 1) * D],
                        accum_op=Alu.add), ao)
                    ev_x2c_rd[bls % 2] = ev_ac
                ev_obst[(sub, NPACK - 2)] = ev_og_last
                ev_obst[(sub, NPACK - 1)] = ev_og_last
                ev_outsb_rd = te.ev()
            ev_perm_rd = te.ev()
            ev_oh_rd = te.ev()
            ev_inner_rd = te.ev()

        for c in reversed(p3_cm):
            c.__exit__(None, None, None)

    with nc.Block() as block:
        @block.sync
        def _(eng):
            for th in prog["sync"]:
                th(eng)
            eng.wait_ge(S_so, so.n)

        @block.gpsimd
        def _(eng):
            for th in prog["gpsimd"]:
                th(eng)
            eng.wait_ge(S_ao, ao.n)

        @block.tensor
        def _(eng):
            for th in prog["tensor"]:
                th(eng)

        @block.vector
        def _(eng):
            for th in prog["vector"]:
                th(eng)

        @block.scalar
        def _(eng):
            for th in prog["scalar"]:
                th(eng)

    for c in reversed(per_cm + sem_cm):
        c.__exit__(None, None, None)
    return nc


def stage_core(inputs, core, TPC, TT, seq_len, n_cores):
    NT = TPC // TT
    seq = core // 2 if n_cores > 1 else 0
    half = core % 2 if n_cores > 1 else 0
    t0 = half * TPC

    xfull = np.asarray(inputs["L0_attn_x"], dtype=np.float32)
    xs = xfull[seq]
    x_ext = np.zeros((TPC + 128, D), np.float32)
    x_ext[128:] = xs[t0:t0 + TPC]
    if t0 >= 3:
        x_ext[125:128] = xs[t0 - 3:t0]

    W_in = np.asarray(inputs["L0_attn_in_proj_w"], np.float32)
    conv_k = np.asarray(inputs["L0_attn_conv_k"], np.float32)
    conv_b = np.asarray(inputs["L0_attn_conv_b"], np.float32)
    theta = np.asarray(inputs["L0_attn_theta"], np.float32).reshape(K, Hd)
    dsl = np.asarray(inputs["L0_attn_decay_slopes"], np.float32)
    asl = np.asarray(inputs["L0_attn_anchor_slopes"], np.float32)
    ssc = np.asarray(inputs["L0_attn_score_scale"], np.float32)
    dlg = np.asarray(inputs["L0_attn_deriv_logits"], np.float32)
    W_re = np.asarray(inputs["L0_attn_W_re"], np.float32)
    W_im = np.asarray(inputs["L0_attn_W_im"], np.float32)
    nsc = np.asarray(inputs["L0_attn_norm_scale"], np.float32)
    W_out = np.asarray(inputs["L0_attn_out_proj_w"], np.float32)
    Wc = np.asarray(inputs["L0_ffn_Wc"], np.float32)
    bc = np.asarray(inputs["L0_ffn_bc"], np.float32)
    W1 = np.asarray(inputs["L0_ffn_W1"], np.float32)
    b1 = np.asarray(inputs["L0_ffn_b1"], np.float32)
    W2 = np.asarray(inputs["L0_ffn_W2"], np.float32)
    b2 = np.asarray(inputs["L0_ffn_b2"], np.float32)

    convd = np.zeros((128, KS * NZC * 128), np.float32)
    for tap in range(KS):
        for oc in range(NZC):
            pc = 128 if oc < NZC - 1 else 32
            w = conv_k[tap, 0, oc * 128:oc * 128 + pc]
            blk = np.zeros((128, 128), np.float32)
            blk[:pc, :pc] = np.diag(w)
            convd[:, (tap * NZC + oc) * 128:(tap * NZC + oc + 1) * 128] = blk

    cb_fm = np.zeros((128, NZC), np.float32)
    for oc in range(NZC):
        pc = 128 if oc < NZC - 1 else 32
        cb_fm[:pc, oc] = conv_b[oc * 128:oc * 128 + pc]

    theta_fm = np.zeros((128, NXC), np.float32)
    for j in range(NXC):
        for p in range(128):
            theta_fm[p, j] = theta[2 * j + (p >= 64), p % 64]

    esel = np.zeros((32, NXC * 128), np.float32)
    for j in range(NXC):
        for p in range(128):
            esel[2 * j + (p >= 64), j * 128 + p] = 1.0
    qsel = np.zeros((128, NXC * 32), np.float32)
    for j in range(NXC):
        for p in range(128):
            qsel[p, j * 32 + 2 * j + (p >= 64)] = 1.0
    bmask = np.zeros((128, 4), np.float32)
    for p in range(128):
        bmask[p, p // 32] = 1.0
    tie = np.broadcast_to(
        np.linspace(0.0, 1e-6, GRP, dtype=np.float32)[None, :],
        (128, GRP)).copy()
    iota_t = np.broadcast_to(
        (t0 + np.arange(TT, dtype=np.float32))[None, :], (K, TT)).copy()

    W1es = W1.reshape(ES, D, EF)
    w1p = np.zeros((NDC, 128, ES * EF), np.float32)
    for dc in range(NDC):
        w1p[dc] = W1es[:, dc * 128:(dc + 1) * 128, :].transpose(
            1, 0, 2).reshape(128, ES * EF)
    W2es = W2.reshape(ES, D, EF)
    w2p = np.zeros((NPACK, 128, D), np.float32)
    for pk in range(NPACK):
        for esl in range(4):
            w2p[pk, esl * EF:(esl + 1) * EF, :] = \
                W2es[4 * pk + esl].T
    b1es = b1.reshape(ES, EF)
    b1p = np.zeros((128, NPACK), np.float32)
    for pk in range(NPACK):
        for esl in range(4):
            b1p[esl * EF:(esl + 1) * EF, pk] = b1es[4 * pk + esl]
    b2p = b2.reshape(ES, D)

    return {
        "x": x_ext,
        "win": W_in.reshape(NDC, 128, C),
        "convd": convd,
        "cbias": cb_fm,
        "wout": W_out.reshape(16, 128, D),
        "theta_fm": theta_fm,
        "dslope": np.concatenate([dsl, asl]).reshape(K, 1),
        "sscale": ssc.reshape(K, 1),
        "dlogit": dlg.reshape(1, 3),
        "wre2": np.concatenate([W_re, W_re], 0),
        "wim2": np.concatenate([W_im, W_im], 0),
        "nsc": nsc.reshape(128, 1),
        "wc": Wc.reshape(D, ES).reshape(NDC, 128, ES),
        "bcv": bc.reshape(ES, 1),
        "w1p": w1p, "w2p": w2p, "b1p": b1p, "b2p": b2p,
        "esel": esel, "qsel": qsel, "bmask": bmask, "tie": tie,
        "iota_t": iota_t,
        "twsgn": np.concatenate([np.ones(ND_K), -np.ones(4)]).reshape(
            K, 1).astype(np.float32),
        "twbm": np.concatenate([np.full(ND_K, -float(seq_len - 1)),
                                np.zeros(4)]).reshape(K, 1).astype(
                                    np.float32),
        "flag_first": np.full((128, 1), 1.0 if half == 0 else 0.0,
                              np.float32),
        "ident": np.eye(128, dtype=np.float32),
    }


_TPC, _TT, _SEGT, _SL = 4096, 512, 4, 8192


def kernel(**inputs):
    nc = build_nc(_TPC, _TT, _SEGT, _SL, phases=(1, 2, 3), dev=False)
    in_maps = [stage_core(inputs, c, _TPC, _TT, _SL, N_CORES)
               for c in range(N_CORES)]
    res = run_bass_kernel_spmd(nc, in_maps, core_ids=list(range(N_CORES)))
    out = np.zeros((4, _SL, D), np.float32)
    for c in range(N_CORES):
        seq, half = c // 2, c % 2
        out[seq, half * _TPC:(half + 1) * _TPC] = \
            np.asarray(res.results[c]["out"]).reshape(_TPC, D)
    return out


# version 07
# speedup vs baseline: 1.0088x; 1.0088x over previous
import math
import os
import numpy as np

import concourse.bass as bass
import concourse.mybir as mybir
from concourse.bass_utils import run_bass_kernel_spmd

F32 = mybir.dt.float32
BF16 = mybir.dt.bfloat16
Alu = mybir.AluOpType
Act = mybir.ActivationFunctionType
AX = mybir.AxisListType.X

D = 1024
DI = 2048
K, Hd = 32, 64
KS = 4
C = 2 * DI + K
NXC = DI // 128
NZC = 2 * NXC + 1
NDC = D // 128
ES, EF = 128, 32
GRP = 32
NPACK = 32
MAGIC = 1.5 * 2 ** 23
TWO_PI = 2.0 * math.pi
ND_K = K - 4
ST_PW = 2 * NXC
N_CORES = 8


def st_re(j):
    return 2 * j


def st_im(j):
    return 2 * j + 1


def MM(t, *a, **kw):
    kw.setdefault("skip_group_check", True)
    return t.matmul(*a, **kw)


def build_nc(TPC, TT, SEG_TILES, seq_len, phases=(1, 2, 3), dev=False):
    if dev is True:
        dev = {"stash", "gst", "part", "x1", "oh", "xdc", "inner", "outg",
               "perm", "l2s", "slp", "tw", "pw", "z", "lnfm"}
    elif not dev:
        dev = set()
    NT = TPC // TT
    BPT = TT // 128
    NSEG = NT // SEG_TILES
    SEG_T = SEG_TILES * TT
    GSEG = SEG_T // GRP
    BLK_SEG = SEG_T // 128
    GPT = TT // GRP
    GPB = 128 // GRP
    SUB = 16 if GSEG >= 16 else GSEG
    NSUB = GSEG // SUB
    MSL = 4 * SUB
    NPG = 8
    NPGRP = NPACK // NPG
    ZW = TT + 3

    nc = bass.Bass("TRN2", target_bir_lowering=False,
                   detect_race_conditions=False)

    def din(n, s, d=F32):
        return nc.dram_tensor(n, s, d, kind="ExternalInput")

    x_in = din("x", [TPC + 128, D])
    win = din("win", [NDC, 128, C])
    convd = din("convd", [128, KS * NZC * 128])
    cbias_in = din("cbias", [128, NZC])
    wout_in = din("wout", [16, 128, D])
    theta_in = din("theta_fm", [128, NXC])
    dslope = din("dslope", [K, 1])
    sscale = din("sscale", [K, 1])
    dlogit = din("dlogit", [1, 3])
    wre_in = din("wre2", [128, 64])
    wim_in = din("wim2", [128, 64])
    nsc_in = din("nsc", [128, 1])
    wc_in = din("wc", [NDC, 128, ES])
    bc_in = din("bcv", [ES, 1])
    w1p_in = din("w1p", [NDC, 128, ES * EF])
    w2p_in = din("w2p", [NPACK, 128, D])
    b1p_in = din("b1p", [128, NPACK])
    b2p_in = din("b2p", [128, D])
    esel_in = din("esel", [32, NXC * 128])
    qsel_in = din("qsel", [128, NXC * 32])
    bmask_in = din("bmask", [128, 4])
    tie_in = din("tie", [128, GRP])
    iota_in = din("iota_t", [K, TT])
    twsgn_in = din("twsgn", [K, 1])
    twbm_in = din("twbm", [K, 1])
    flag_in = din("flag_first", [128, 1])
    ident_in = din("ident", [128, 128])

    out = nc.dram_tensor("out", [TPC, D], F32, kind="ExternalOutput")

    stash = nc.dram_tensor("stash", [NT, 128, NZC * TT], BF16)
    gstash = nc.dram_tensor("gstash", [NT, 128, NXC * TT], BF16)
    ln2st = nc.dram_tensor("ln2st", [NT, 128, BPT * D], BF16)
    ohst = nc.dram_tensor("ohst", [max(NSEG, 1), 128, SEG_T], BF16)
    cc_i = nc.dram_tensor("cc_i", [128, NZC], F32)
    cc_o = nc.dram_tensor("cc_o", [128, NZC], F32)
    ob = nc.dram_tensor("ob", [16, 2, 4, SUB, D], BF16)
    if dev:
        d_stash = nc.dram_tensor("d_stash", [NT, 128, NZC * TT], F32,
                                 kind="ExternalOutput")
        d_gst = nc.dram_tensor("d_gst", [NT, 128, NXC * TT], F32,
                               kind="ExternalOutput")
        d_part = nc.dram_tensor("d_part", [128, NZC * NT], F32,
                                kind="ExternalOutput")
        d_x1 = nc.dram_tensor("d_x1", [TPC, D], F32, kind="ExternalOutput")
        d_oh = nc.dram_tensor("d_oh", [max(NSEG, 1), 128, SEG_T], F32,
                              kind="ExternalOutput")
        d_xdc = nc.dram_tensor("d_xdc", [128, BLK_SEG * 512], F32,
                               kind="ExternalOutput")
        d_inner = nc.dram_tensor("d_inner", [128, NPACK * GSEG], F32,
                                 kind="ExternalOutput")
        d_outg = nc.dram_tensor("d_outg", [128, D], F32,
                                kind="ExternalOutput")
        d_perm = nc.dram_tensor("d_perm", [128, BLK_SEG * 512], F32,
                                kind="ExternalOutput")
        d_l2s = nc.dram_tensor("d_l2s", [128, SEG_TILES * BPT * D], F32,
                               kind="ExternalOutput")
        d_slp = nc.dram_tensor("d_slp", [K, 8], F32, kind="ExternalOutput")
        d_tw = nc.dram_tensor("d_tw", [K, NT + 2], F32,
                              kind="ExternalOutput")
        d_pw = nc.dram_tensor("d_pw", [K, 2 * TT], F32,
                              kind="ExternalOutput")
        d_z = nc.dram_tensor("d_z", [128, 5 * (TT + 3)], F32,
                             kind="ExternalOutput")
        d_lnfm = nc.dram_tensor("d_lnfm", [128, NDC * TT], F32,
                                kind="ExternalOutput")
        d_carry = nc.dram_tensor("d_carry", [128, 2 * NZC], F32,
                                 kind="ExternalOutput")

    prog = {k: [] for k in ("sync", "gpsimd", "tensor", "vector", "scalar")}

    class Ctr:
        def __init__(self, h, step=1):
            self.h, self.n, self.step = h, 0, step

        def inc(self, instr, k=None):
            k = self.step if k is None else k
            instr.then_inc(self.h, k)
            self.n += k

        def ev(self):
            return (self.h, self.n)

    sem_cm = [nc.semaphore(n) for n in
              ("ld", "wl", "te", "ve", "se", "ge", "so", "ao", "ccs", "sq",
               "chv", "chs")]
    sems = [s.__enter__() for s in sem_cm]
    (S_ld, S_wl, S_te, S_ve, S_se, S_ge, S_so, S_ao, S_cc, S_sq,
     S_chv, S_chs) = sems
    _chain_n = {"v": 0, "s": 0}

    def fence(eng, instr, which="v"):
        _chain_n[which] += 1
        h = S_chv if which == "v" else S_chs
        instr.then_inc(h, 1)
        eng.wait_ge(h, _chain_n[which])
    ld = Ctr(S_ld, 16); wl = Ctr(S_wl, 16); te = Ctr(S_te); ve = Ctr(S_ve)
    se = Ctr(S_se); ge = Ctr(S_ge); so = Ctr(S_so, 16); ao = Ctr(S_ao, 16)
    ccc = Ctr(S_cc); sq = Ctr(S_sq, 16)

    def emit_multi(engname, ctr, after, body, n):
        evs = tuple(e for e in after if e is not None and e[1] > 0)
        ctr.n += ctr.step * n
        h, st = ctr.h, ctr.step

        def th(eng, _evs=evs, _body=body):
            for hh, nn in _evs:
                eng.wait_ge(hh, nn)
            cnt = _body(eng, lambda ins: ins.then_inc(h, st))
            assert cnt == n, f"emit_multi expected {n} DMAs, got {cnt}"
        prog[engname].append(th)
        return ctr.ev()

    def emit(engname, ctr, after, body):
        evs = tuple(e for e in after if e is not None and e[1] > 0)
        if ctr is not None:
            ctr.n += ctr.step
            tgt = (ctr.h, ctr.step)
        else:
            tgt = None

        def th(eng, _evs=evs, _body=body, _tgt=tgt):
            for h, n in _evs:
                eng.wait_ge(h, n)
            last = _body(eng)
            if _tgt is not None:
                assert last is not None, "emit body returned no instruction"
                last.then_inc(_tgt[0], _tgt[1])
        prog[engname].append(th)
        return ctr.ev() if ctr is not None else None

    def T(after, body):
        return emit("tensor", te, after, body)

    def V(after, body):
        return emit("vector", ve, after, body)

    def S(after, body):
        return emit("scalar", se, after, body)

    def G(after, body, ctr=None):
        return emit("gpsimd", ctr if ctr is not None else ge, after, body)

    def SY(after, body, ctr=None):
        return emit("sync", ctr if ctr is not None else so, after, body)

    per_cm = [
        nc.sbuf_tensor("consts", [128, 16], F32),
        nc.sbuf_tensor("partials", [128, NZC * NT], F32),
        nc.sbuf_tensor("carry", [128, NZC], F32),
        nc.sbuf_tensor("flagsb", [128, 1], F32),
        nc.sbuf_tensor("smll", [128, 24], F32),
        nc.sbuf_tensor("identb", [128, 128], BF16),
        nc.sbuf_tensor("eselb", [32, NXC * 128], BF16),
        nc.sbuf_tensor("ccsb", [128, NZC], F32),
        nc.psum_tensor("pall", [128, 3072], F32),
        nc.psum_tensor("pbf", [128, 2048], BF16),
    ]
    (consts, partials, carry, flagsb, smll, identb, eselb, ccsb,
     pall, pbf) = [c.__enter__() for c in per_cm]

    CONST_VALS = [0.5, 1.0, MAGIC, -MAGIC, TWO_PI, math.pi / 2.0,
                  1e-5, -20.0, 20.0, 1.0 / (2.0 * Hd), 1e-4, -1.0,
                  0.0, -0.5, 1.0 / D]
    CV = {v: i for i, v in enumerate(CONST_VALS)}
    for v, i in CV.items():
        nc.const_aps.register(F32, v, consts[:, i:i + 1])

    def cst(v, p=128):
        return consts[0:p, CV[v]:CV[v] + 1]

    def bank(i, w=512, part=128, off=0):
        return pall[0:part, i * 512 + off:i * 512 + off + w]

    def pbf_slot(s, w=128):
        off = (s % 2) * 1024 + (s // 2) * 256
        return pbf[:, off:off + w]

    KP = os.environ.get("KPART", "")
    if os.environ.get("KFRAME"):
        ev0 = None
        ev_id = None
    elif KP == "id":
        ev0 = None
        ev_id = G([], lambda g: g.dma_start(out=identb[:, :],
                                            in_=ident_in.ap()), wl)
    elif KP == "c0":
        ev_id = None
        ev0 = V([], lambda v: [v.memset(consts[:, i:i + 1], val)
                               for val, i in CV.items()] and
                v.memset(partials[:, :], 0.0) and
                v.memset(carry[:, :], 0.0))
    elif KP == "c1":
        ev_id = None
        ev0 = V([], lambda v: v.memset(consts[:, 0:1], 0.5))
    else:
        ev0 = V([], lambda v: [v.memset(consts[:, i:i + 1], val)
                               for val, i in CV.items()] and
                v.memset(partials[:, :], 0.0) and
                v.memset(carry[:, :], 0.0))
        ev_id = G([], lambda g: g.dma_start(out=identb[:, :],
                                            in_=ident_in.ap()), wl)

    def ln_norm(src, dst, nb, stc, after, scr):
        LNCUT = int(os.environ.get("LNCUT", "9"))
        evm0 = V(after, lambda v: [
            v.reduce_sum(stc[:, b:b + 1], src[:, b * D:(b + 1) * D], AX)
            for b in range(nb)][-1])
        evm = V([evm0], lambda v: v.tensor_scalar_mul(
            stc[:, 0:nb], stc[:, 0:nb], 1.0 / D))
        if LNCUT < 2:
            return evm, evm
        evq = S(after, lambda s: [
            s.activation(scr[:, 0:D], src[:, b * D:(b + 1) * D], Act.Square,
                         scale=cst(1.0), bias=cst(0.0),
                         accum_out=stc[:, nb + b:nb + b + 1])
            for b in range(nb)][-1])
        if LNCUT < 3:
            return evq, evq
        def v_var(v):
            fence(v, v.tensor_scalar_mul(stc[:, nb:2 * nb],
                                         stc[:, nb:2 * nb], 1.0 / D))
            fence(v, v.tensor_mul(stc[:, 2 * nb:3 * nb], stc[:, 0:nb],
                                  stc[:, 0:nb]))
            return v.tensor_sub(stc[:, nb:2 * nb], stc[:, nb:2 * nb],
                                stc[:, 2 * nb:3 * nb])
        evv = V([evm, evq], v_var)
        if LNCUT < 4:
            return evv, evv
        evr = S([evv], lambda s: [
            s.activation(stc[:, 2 * nb:3 * nb], stc[:, nb:2 * nb], Act.Ln,
                         scale=cst(1.0), bias=cst(1e-5)),
            s.activation(stc[:, 2 * nb:3 * nb], stc[:, 2 * nb:3 * nb],
                         Act.Exp, scale=cst(-0.5), bias=cst(0.0)),
        ][-1])
        if LNCUT < 5:
            return evr, evr
        evb = V([evr], lambda v: v.scalar_tensor_tensor(
            stc[:, 3 * nb:4 * nb], stc[:, 0:nb], -1.0,
            stc[:, 2 * nb:3 * nb], Alu.mult, Alu.mult))
        if LNCUT < 6:
            return evb, evb
        evn = S([evb], lambda s: [
            s.activation(dst[:, b * D:(b + 1) * D], src[:, b * D:(b + 1) * D],
                         Act.Identity, scale=stc[:, 2 * nb + b:2 * nb + b + 1],
                         bias=stc[:, 3 * nb + b:3 * nb + b + 1])
            for b in range(nb)][-1])
        return evn, evr

    def tm_to_fm(src, nb, put, after):
        if not isinstance(after, list):
            after = [after]
        if os.environ.get("TFCUT"):
            e = V(after, lambda v: v.engine_nop())
            return e, [e]
        evs = []
        for b in range(nb):
            for dc in range(NDC):
                slot = (b * NDC + dc) % 4
                gate = [evs[-2][0]] if len(evs) >= 2 else []
                evt = T(after + gate,
                        lambda t, b=b, dc=dc, slot=slot: t.transpose(
                            pbf_slot(slot),
                            src[:, b * D + dc * 128:b * D + (dc + 1) * 128],
                            identb[:, :]))
                eve = V([evt], lambda v, b=b, dc=dc, slot=slot: v.tensor_copy(
                    put(b, dc), pbf_slot(slot)))
                evs.append((eve, evt))
        return evs[-1][0], [e[0] for e in evs]

    p1_cm = [
        nc.sbuf_tensor("theta2p", [128, NXC], F32),
        nc.sbuf_tensor("thetasb", [128, NXC], F32),
        nc.sbuf_tensor("slp", [K, 8], F32),
        nc.sbuf_tensor("twsc", [K, 1], F32),
        nc.sbuf_tensor("twbias", [K, NT], F32),
        nc.sbuf_tensor("sscb", [K, 1], F32),
        nc.sbuf_tensor("wpoly", [128, 3], F32),
        nc.sbuf_tensor("iotasb", [K, TT], F32),
        nc.sbuf_tensor("winb", [128, NDC * C], BF16),
        nc.sbuf_tensor("convb", [128, KS * NZC * 128], BF16),
        nc.sbuf_tensor("xtm", [128, BPT * D], F32),
        nc.sbuf_tensor("lntm", [128, BPT * D], BF16),
        nc.sbuf_tensor("lnfm", [128, NDC * TT], BF16),
        nc.sbuf_tensor("zsb", [128, 6 * ZW], BF16),
        nc.sbuf_tensor("zhalo", [128, NZC * 3], BF16),
        nc.sbuf_tensor("stsb", [128, NZC * TT], BF16),
        nc.sbuf_tensor("gtsb", [128, (NXC // 2) * TT], BF16),
        nc.sbuf_tensor("lnst", [128, 4 * 4], F32),
        nc.sbuf_tensor("pwsb", [K, 2 * TT], F32),
        nc.sbuf_tensor("scp1", [128, max(D, 2 * TT)], F32),
        nc.sbuf_tensor("chainr", [128, 2 * TT], F32),
        nc.sbuf_tensor("sincos", [128, 2 * TT], BF16),
        nc.sbuf_tensor("polyb", [128, TT], BF16),
        nc.sbuf_tensor("remb", [128, 2 * TT], BF16),
        nc.sbuf_tensor("ones_row", [1, 128], F32),
        nc.sbuf_tensor("cbsb", [128, NZC], F32),
        nc.sbuf_tensor("usb", [128, TT], F32),
    ]
    (theta2p, thetasb, slp, twsc, twbias, sscb, wpoly, iotasb, winb, convb,
     xtm, lntm, lnfm, zsb, zhalo, stsb, gtsb, lnst, pwsb,
     scp1, chainr, sincos, polyb, remb, ones_row, cbsb, usb) = \
        [c.__enter__() for c in p1_cm]

    P1SETUP = int(os.environ.get("P1SETUP", "99"))
    if 1 in phases and not os.environ.get("KFRAME"):
        def g_loads(g, inc):
            n = 0
            for dst, srcp in [
                (theta2p[:, :], theta_in.ap()),
                (thetasb[:, :], theta_in.ap()),
                (slp[:, 0:1], dslope.ap()),
                (sscb[:, :], sscale.ap()),
                (smll[0:1, 0:3], dlogit.ap()),
                (flagsb[:, :], flag_in.ap()),
                (iotasb[:, :], iota_in.ap()),
                (slp[:, 7:8], twsgn_in.ap()),
                (smll[0:K, 12:13], twbm_in.ap()),
                (convb[:, :], convd.ap()),
                (eselb[:, :], esel_in.ap()),
                (cbsb[:, :], cbias_in.ap()),
            ]:
                inc(g.dma_start(out=dst, in_=srcp))
                n += 1
            for dc in range(NDC):
                inc(g.dma_start(out=winb[:, dc * C:(dc + 1) * C],
                                in_=win.ap()[dc, :, :]))
                n += 1
            return n
        ev_gl = emit_multi("gpsimd", wl, [], g_loads, 12 + NDC) if P1SETUP >= 1 else None

        ev_or = V([ev0], lambda v: v.memset(ones_row[:, :], 1.0)) if P1SETUP >= 2 else None
        ev_stz = V([], lambda v: v.memset(stsb[:, :], 0.0)) if P1SETUP >= 3 else None
        ev1 = None if P1SETUP < 4 else V([ev_gl, ev0], lambda v: [
            v.tensor_scalar_mul(theta2p[:, :], theta2p[:, :], 1.0 / TWO_PI),
            v.scalar_tensor_tensor(slp[:, 1:2], slp[:, 0:1], -1.0,
                                   slp[:, 0:1], Alu.mult, Alu.max),
            v.tensor_scalar_max(slp[:, 2:3], slp[:, 0:1], 0.0),
        ][-1])
        ev2 = None if P1SETUP < 5 else S([ev1], lambda s: [
            s.activation(slp[:, 3:4], slp[:, 1:2], Act.Exp,
                         scale=cst(-1.0, K), bias=cst(0.0, K)),
            s.activation(slp[:, 4:5], slp[:, 3:4], Act.Ln,
                         scale=cst(1.0, K), bias=cst(1.0, K)),
            s.activation(smll[0:1, 4:7], smll[0:1, 0:3], Act.Exp,
                         scale=cst(1.0, 1), bias=cst(0.0, 1)),
        ][-1])

        def v_setup2(v):
            fence(v, v.tensor_add(slp[:, 5:6], slp[:, 2:3],
                                  slp[:, 4:5]))
            v.tensor_mul(twsc[:, :], slp[:, 5:6], slp[:, 7:8])
            fence(v, v.tensor_mul(slp[:, 6:7], slp[:, 5:6],
                                  smll[0:K, 12:13]))
            for i in range(NT):
                v.scalar_tensor_tensor(twbias[:, i:i + 1], twsc[:, :],
                                       float(i * TT), slp[:, 6:7],
                                       Alu.mult, Alu.add)
            return v.reduce_sum(smll[0:1, 7:8], smll[0:1, 4:7], AX)
        ev3a = V([ev2], v_setup2) if P1SETUP >= 6 else None
        ev3b = None if P1SETUP < 6 else V([ev3a], lambda v: v.reciprocal(
            smll[0:1, 8:9], smll[0:1, 7:8]))
        ev3 = None if P1SETUP < 6 else V([ev3b], lambda v:
            v.scalar_tensor_tensor(
                smll[0:1, 9:12], smll[0:1, 4:7], 1.0,
                smll[0:1, 8:9].broadcast_to([1, 3]), Alu.mult, Alu.mult))
        ev4 = T([ev3, ev_or], lambda t: MM(t, 
            bank(0, 3), ones_row[:, :], smll[0:1, 9:12],
            start=True, stop=True))
        ev_wp = None if P1SETUP < 8 else V([ev4], lambda v: [
            v.tensor_copy(wpoly[:, 0:2], bank(0, 2)),
            v.tensor_scalar_mul(wpoly[:, 2:3], bank(0, 1, off=2), -1.0),
        ][-1])

        xpfx = xtm
        lnpfx = lnfm
        SKIP_PFX = os.environ.get('P1CUT') in ('nopfx', 'setup')
        if not SKIP_PFX:
            evx = SY([], lambda sy: sy.dma_start(out=xpfx[:, 0:D],
                                                 in_=x_in.ap()[0:128, :]),
                     ld)
            evn, _ = ln_norm(xtm, lntm, 1, lnst, [evx, ev0, ev_id], scp1)
            evfm, _ = tm_to_fm(lntm, 1,
                               lambda b, dc: lnfm[:, dc * 128:dc * 128 + 128],
                               evn)
        else:
            evn = evfm = None
        ev_zh = []
        for oc in (range(NZC) if not SKIP_PFX and os.environ.get('P1CUT') != 'zh' else []):
            pc = 128 if oc < NZC - 1 else 32
            bnk = oc % 2

            def t_pfx(t, oc=oc, pc=pc, bnk=bnk):
                last = None
                for dc in range(NDC):
                    last = MM(t, 
                        bank(bnk, 128, pc), winb[:, dc * C + oc * 128:
                                                 dc * C + oc * 128 + pc],
                        lnpfx[:, dc * 128:(dc + 1) * 128],
                        start=(dc == 0), stop=(dc == NDC - 1))
                return last
            gate = [ev_zh[-2][1]] if len(ev_zh) >= 2 else [ev_wp, evfm]
            evt = T(gate + [evfm, ev_gl], t_pfx)
            eve = V([evt], lambda v, oc=oc, pc=pc, bnk=bnk: v.tensor_copy(
                zhalo[0:pc, oc * 3:oc * 3 + 3], bank(bnk, 3, pc, off=125)))
            ev_zh.append((evt, eve))
        ev_zhalo = ev_zh[-1][1] if ev_zh else None
        if SKIP_PFX:
            ev_zhalo = V([], lambda v: v.memset(zhalo[:, :], 0.0))

        slot_rd = [None] * 6
        zc_rd = [None, None]
        pwe_rd = [None, None]
        zb_free = [None, None]
        prev_use = [evn]
        st_prev = None
        gst_prev = None
        if "slp" in dev:
            G([ev3, ev_wp], lambda g: g.dma_start(
                out=d_slp.ap(), in_=slp[:, :]), ao)
        if "tw" in dev:
            G([ev3, ev_wp], lambda g: g.dma_start(
                out=d_tw.ap()[:, 0:NT], in_=twbias[:, :],
                allow_slow_non_contiguous=True), ao)
            G([ev3, ev_wp], lambda g: g.dma_start(
                out=d_tw.ap()[:, NT:NT + 1], in_=twsc[:, :],
                allow_slow_non_contiguous=True), ao)
            G([ev3, ev_wp], lambda g: g.dma_start(
                out=d_tw.ap()[:, NT + 1:NT + 2], in_=sscb[:, :],
                allow_slow_non_contiguous=True), ao)
        halo_wr = [ev_zhalo] * NZC
        b4_rd = [None]
        prev_vc = [None]
        prev_fm = evfm
        prev_p_S = None

        tb = []
        for i in (range(NT) if os.environ.get("P1CUT") not in ("setup", "pfx", "zh") else []):
            evx = SY([prev_use[0]], lambda sy, i=i: sy.dma_start(
                out=xtm[:, :].rearrange("p (b d) -> p b d", d=D),
                in_=x_in.ap()[128 + i * TT:128 + (i + 1) * TT, :].rearrange(
                    "(b p) d -> p b d", p=128)), ld)
            evn, _ = ln_norm(xtm, lntm, BPT, lnst,
                             [evx, prev_fm, prev_vc[0]], scp1)
            prev_use[0] = evn
            evfm, _ = tm_to_fm(lntm, BPT,
                               lambda b, dc: lnfm[:, dc * TT + b * 128:
                                                  dc * TT + (b + 1) * 128],
                               [evn, ev_zhalo])
            prev_fm = evfm
            if os.environ.get("P1CUT") == "fm":
                tb = []
                continue

            oc_order = [NZC - 1] + list(range(NZC - 1))
            ev_pw_bf = None
            ev_pwe = {}
            for idx, oc in enumerate(oc_order):
                pc = 32 if oc == NZC - 1 else 128
                zb = idx % 2
                slot = oc % 5

                def t_ip(t, oc=oc, pc=pc, zb=zb):
                    last = None
                    for dc in range(NDC):
                        last = MM(t, 
                            bank(zb, TT, pc),
                            winb[:, dc * C + oc * 128:dc * C + oc * 128 + pc],
                            lnfm[:, dc * TT:(dc + 1) * TT],
                            start=(dc == 0), stop=(dc == NDC - 1))
                    return last
                ev_ip = T([evfm, zb_free[zb]], t_ip)

                def v_ev(v, oc=oc, pc=pc, zb=zb, slot=slot):
                    v.tensor_copy(zsb[0:pc, slot * ZW:slot * ZW + 3],
                                  zhalo[0:pc, oc * 3:oc * 3 + 3])
                    return v.tensor_copy(
                        zsb[0:pc, slot * ZW + 3:slot * ZW + 3 + TT],
                        bank(zb, TT, pc))
                ev_ev = V([ev_ip, slot_rd[slot], halo_wr[oc]], v_ev)
                zb_free[zb] = ev_ev
                if os.environ.get("P1CUT") == "ip":
                    slot_rd[slot] = ev_ev
                    continue

                if oc == NZC - 1:
                    cb = 3
                else:
                    cb = 2

                def t_cv(t, oc=oc, pc=pc, slot=slot, cb=cb):
                    last = None
                    for tap in range(KS):
                        last = MM(t, 
                            bank(cb, TT, pc),
                            convb[0:pc, (tap * NZC + oc) * 128:
                                  (tap * NZC + oc) * 128 + pc],
                            zsb[0:pc, slot * ZW + tap:slot * ZW + tap + TT],
                            start=(tap == 0), stop=(tap == KS - 1))
                    return last
                rd_ev = zc_rd[0] if cb == 2 else b4_rd[0]
                ev_cv = T([ev_ev, rd_ev], t_cv)

                halo_wr[oc] = V([ev_cv], lambda v, oc=oc, pc=pc, slot=slot:
                                v.tensor_copy(
                    zhalo[0:pc, oc * 3:oc * 3 + 3],
                    zsb[0:pc, slot * ZW + TT:slot * ZW + TT + 3]))
                slot_rd[slot] = halo_wr[oc]

                if "z" in dev and i == 0 and oc == 0:
                    G([ev_ev], lambda g: g.dma_start(
                        out=d_z.ap()[:, 0:TT + 3], in_=zsb[:, 0:TT + 3]), ao)
                    G([evfm], lambda g: g.dma_start(
                        out=d_lnfm.ap(), in_=lnfm[:, :]), ao)
                if oc == NZC - 1:
                    def v_sc(v, cb=cb):
                        fence(v, v.tensor_scalar(pwsb[:, 0:TT],
                                                 bank(cb, TT, 32),
                                                 cbsb[0:K, NZC - 1:NZC],
                                                 sscb[:, :],
                                                 Alu.add, Alu.mult))
                        return v.tensor_scalar(pwsb[:, 0:TT], pwsb[:, 0:TT],
                                               20.0, -20.0, Alu.min, Alu.max)
                    ev_sc = V([ev_cv, prev_p_S], v_sc)
                    b4_rd[0] = ev_sc
                    ev_p = S([ev_sc], lambda s, i=i: [
                        s.activation(pwsb[:, 0:TT], pwsb[:, 0:TT], Act.Exp,
                                     scale=cst(1.0, K), bias=cst(0.0, K)),
                        s.activation(pwsb[:, TT:2 * TT], iotasb[:, :],
                                     Act.Exp, scale=twsc[:, :],
                                     bias=twbias[:, i:i + 1]),
                    ][-1])
                    prev_p_S = ev_p
                    if "pw" in dev and i == 0:
                        G([ev_p], lambda g: g.dma_start(
                            out=d_pw.ap(), in_=pwsb[:, :]), ao)
                    ev_pw_bf = V([ev_p, st_prev, ev_stz], lambda v, i=i: v.scalar_tensor_tensor(
                        stsb[0:K, ST_PW * TT:ST_PW * TT + TT],
                        pwsb[:, 0:TT], 1.0, pwsb[:, TT:2 * TT],
                        Alu.mult, Alu.mult,
                        accum_out=partials[0:K, ST_PW * NT + i:
                                           ST_PW * NT + i + 1]))
                    continue

                if oc < NXC:
                    j = oc
                    pb = 4 + (j % 2)

                    ev_pwe[j] = T([ev_pw_bf, pwe_rd[j % 2]],
                                  lambda t, j=j, pb=pb: MM(t, 
                        bank(pb, TT),
                        eselb[:, j * 128:(j + 1) * 128],
                        stsb[0:K, ST_PW * TT:ST_PW * TT + TT],
                        start=True, stop=True))

                    ev_u = V([ev_cv, prev_vc[0]], lambda v, j=j, cb=cb:
                             v.tensor_scalar_add(usb[:, :], bank(cb, TT),
                                                 cbsb[:, j:j + 1]))

                    def s_chain(s, j=j, cb=cb):
                        xv = usb[:, :]
                        s.activation(scp1[:, 0:TT], xv, Act.Identity,
                                     scale=theta2p[:, j:j + 1],
                                     bias=cst(MAGIC))
                        s.activation(scp1[:, 0:TT], scp1[:, 0:TT], Act.Identity,
                                     scale=cst(1.0), bias=cst(-MAGIC))
                        return s.activation(scp1[:, TT:2 * TT] if False else
                                            scp1[:, 0:TT], scp1[:, 0:TT],
                                            Act.Identity, scale=cst(TWO_PI),
                                            bias=cst(0.0))
                    ev_s1 = S([ev_u], s_chain)

                    def v_r(v, j=j, cb=cb):
                        xv = usb[:, :]
                        v.scalar_tensor_tensor(chainr[:, 0:TT], xv,
                                               thetasb[:, j:j + 1],
                                               scp1[:, 0:TT],
                                               Alu.mult, Alu.subtract)
                        return v.scalar_tensor_tensor(
                            chainr[:, TT:2 * TT], chainr[:, 0:TT], -1.0,
                            chainr[:, 0:TT], Alu.mult, Alu.max)
                    ev_r = V([ev_s1], v_r)

                    def s_tr(s, j=j, cb=cb):
                        xv = usb[:, :]
                        s.activation(sincos[:, 0:TT], chainr[:, 0:TT],
                                     Act.Sin, scale=cst(1.0), bias=cst(0.0))
                        s.activation(sincos[:, TT:2 * TT],
                                     chainr[:, TT:2 * TT], Act.Sin,
                                     scale=cst(-1.0), bias=cst(math.pi / 2))
                        return s.activation(scp1[:, 0:TT], xv, Act.Identity,
                                            scale=wpoly[:, 2:3],
                                            bias=wpoly[:, 1:2])
                    ev_s2 = S([ev_r], s_tr)

                    def v_chain(v, j=j, cb=cb, i=i):
                        xv = usb[:, :]
                        v.tensor_tensor(chainr[:, 0:TT], scp1[:, 0:TT], xv,
                                        Alu.mult)
                        v.tensor_scalar_add(polyb[:, 0:TT], chainr[:, 0:TT],
                                            wpoly[:, 0:1])
                        v.tensor_tensor(remb[:, 0:TT], polyb[:, 0:TT],
                                        sincos[:, TT:2 * TT], Alu.mult)
                        v.tensor_tensor(remb[:, TT:2 * TT], polyb[:, 0:TT],
                                        sincos[:, 0:TT], Alu.mult)
                        v.scalar_tensor_tensor(
                            stsb[:, st_re(j) * TT:(st_re(j) + 1) * TT],
                            remb[:, 0:TT], 1.0, bank(4 + (j % 2), TT),
                            Alu.mult, Alu.mult,
                            accum_out=partials[:, st_re(j) * NT + i:
                                               st_re(j) * NT + i + 1])
                        return v.scalar_tensor_tensor(
                            stsb[:, st_im(j) * TT:(st_im(j) + 1) * TT],
                            remb[:, TT:2 * TT], 1.0, bank(4 + (j % 2), TT),
                            Alu.mult, Alu.mult,
                            accum_out=partials[:, st_im(j) * NT + i:
                                               st_im(j) * NT + i + 1])
                    ev_vc = V([ev_s2, ev_pwe[j]], v_chain)
                    zc_rd[0] = ev_u
                    pwe_rd[j % 2] = ev_vc
                    prev_vc[0] = ev_vc
                else:
                    jg = oc - NXC
                    ev_g = S([ev_cv, gst_prev], lambda s, jg=jg, cb=cb, oc=oc:
                             s.activation(
                        gtsb[:, (jg % 8) * TT:(jg % 8 + 1) * TT],
                        bank(cb, TT), Act.Silu, scale=cst(1.0),
                        bias=cbsb[:, oc:oc + 1]))
                    zc_rd[0] = ev_g
                    if jg % 8 == 7:
                        if "gst" in dev:
                            G([ev_g], lambda g, i=i, jg=jg: g.dma_start(
                                out=d_gst.ap()[i, :, (jg - 7) * TT:
                                               (jg + 1) * TT],
                                in_=gtsb[:, :]), ao)
                        gst_prev = SY([ev_g], lambda sy, i=i, jg=jg:
                                      sy.dma_start(
                            out=gstash.ap()[i, :, (jg - 7) * TT:
                                            (jg + 1) * TT],
                            in_=gtsb[:, :]), so)

            last_v = ve.ev()
            last_s = se.ev()
            ev_st = SY([last_v, st_prev], lambda sy, i=i: sy.dma_start(
                out=stash.ap()[i, :, :], in_=stsb[:, :]), so)
            st_prev = ev_st
            tb = []
            if "stash" in dev:
                st_prev = G([last_v, st_prev], lambda g, i=i: g.dma_start(
                    out=d_stash.ap()[i, :, :], in_=stsb[:, :]), ao)

        if "part" in dev:
            G([ve.ev()], lambda g: g.dma_start(
                out=d_part.ap(), in_=partials[:, :]), ao)


    def barrier():
        evs = [te.ev(), ve.ev(), se.ev(), ge.ev(), (S_ld, ld.n),
               (S_so, so.n), (S_wl, wl.n), (S_ao, ao.n)]
        outs = []
        outs.append(T(evs, lambda t: t.nop()))
        outs.append(V(evs, lambda v: v.engine_nop()))
        outs.append(S(evs, lambda s: s.nop()))
        outs.append(G(evs, lambda g: g.nop()))
        outs.append(SY(evs, lambda sy: sy.nop(), None))
        return [o for o in outs if o is not None]

    if (1 in phases) and (2 in phases):
        ev_tot0 = V([ve.ev(), se.ev()], lambda v: v.tensor_reduce(
            ccsb[:, :],
            partials[:, :].rearrange("p (s t) -> p s t", t=NT),
            AX, Alu.add))
        ev_tot = V([ev_tot0], lambda v: v.tensor_scalar_mul(
            ccsb[:, :], ccsb[:, :], flagsb[:, :]))
        ev_ci = G([ev_tot], lambda g: g.dma_start(out=cc_i.ap(),
                                                  in_=ccsb[:, :]), ao)
        ev_cc = emit("gpsimd", ccc, [ev_ci], lambda g: g.collective_compute(
            "AllReduce", Alu.add,
            replica_groups=[[2 * p, 2 * p + 1] for p in range(4)],
            ins=[cc_i.ap()], outs=[cc_o.ap()]))
        ev_co = G([ev_cc], lambda g: g.dma_start(out=ccsb[:, :],
                                                 in_=cc_o.ap()), wl)
        def v_cin(v):
            fence(v, v.tensor_scalar_mul(smll[:, 13:14], flagsb[:, :], -1.0))
            fence(v, v.tensor_scalar_add(smll[:, 13:14], smll[:, 13:14],
                                         1.0))
            return v.tensor_scalar_mul(carry[:, :], ccsb[:, :],
                                       smll[:, 13:14])
        ev_cin = V([ev_co], v_cin)
        bar1 = barrier()
        if "hbm" in dev:
            G(bar1, lambda g: g.dma_start(out=d_stash.ap(),
                                          in_=stash.ap()), ao)
            G(bar1, lambda g: g.dma_start(out=d_gst.ap(),
                                          in_=gstash.ap()), ao)
            G(bar1, lambda g: g.dma_start(out=d_part.ap(),
                                          in_=partials[:, :]), ao)
            G(bar1, lambda g: g.dma_start(out=d_carry.ap()[:, 0:NZC],
                                          in_=carry[:, :]), ao)
            G(bar1, lambda g: g.dma_start(out=d_carry.ap()[:, NZC:2 * NZC],
                                          in_=ccsb[:, :]), ao)

    for c in reversed(p1_cm):
        c.__exit__(None, None, None)

    if 2 in phases:
        p2_cm = [
            nc.sbuf_tensor("woutb", [128, 16 * D], BF16),
            nc.sbuf_tensor("wcb", [128, NDC * ES], BF16),
            nc.sbuf_tensor("bdre", [128, 128], BF16),
            nc.sbuf_tensor("bdim", [128, 128], BF16),
            nc.sbuf_tensor("wresb", [128, 64], F32),
            nc.sbuf_tensor("wimsb", [128, 64], F32),
            nc.sbuf_tensor("nscsb", [128, 1], F32),
            nc.sbuf_tensor("qselb", [128, NXC * 32], BF16),
            nc.sbuf_tensor("bcsb2", [ES, 1], F32),
            nc.sbuf_tensor("tiesb", [128, GRP], F32),
            nc.sbuf_tensor("stsb2", [128, NZC * TT], BF16),
            nc.sbuf_tensor("gtsb2", [128, NXC * TT], BF16),
            nc.sbuf_tensor("numrot", [128, 4 * TT], BF16),
            nc.sbuf_tensor("sqrot", [128, 2 * TT], BF16),
            nc.sbuf_tensor("densb", [32, TT], F32),
            nc.sbuf_tensor("invd", [32, TT], F32),
            nc.sbuf_tensor("snormb", [32, TT], BF16),
            nc.sbuf_tensor("ypsb", [128, NXC * TT], BF16),
            nc.sbuf_tensor("y1b", [128, 2 * TT], BF16),
            nc.sbuf_tensor("ygsb", [128, NXC * TT], BF16),
            nc.sbuf_tensor("xtm2", [128, BPT * D], F32),
            nc.sbuf_tensor("x1sb", [128, BPT * D], F32),
            nc.sbuf_tensor("ln2tm", [128, BPT * D], BF16),
            nc.sbuf_tensor("lnfm2", [128, NDC * TT], BF16),
            nc.sbuf_tensor("lgsb", [128, TT], F32),
            nc.sbuf_tensor("gmx", [128, GPT], F32),
            nc.sbuf_tensor("ohsb", [128, SEG_T], BF16),
            nc.sbuf_tensor("lnst2", [128, 16], F32),
            nc.sbuf_tensor("scp2", [128, max(D, 2 * TT)], F32),
            nc.sbuf_tensor("t2sb", [32, 2 * TT], F32),
        ]
        (woutb, wcb, bdre, bdim, wresb, wimsb, nscsb, qselb, bcsb2, tiesb,
         stsb2, gtsb2, numrot, sqrot, densb, invd, snormb, ypsb, y1b, ygsb,
         xtm2, x1sb, ln2tm, lnfm2, lgsb, gmx, ohsb, lnst2, scp2,
         t2sb) = [c.__enter__() for c in p2_cm]

        def g2_loads(g, inc):
            n = 0
            for i16 in range(16):
                inc(g.dma_start(out=woutb[:, i16 * D:(i16 + 1) * D],
                                in_=wout_in.ap()[i16, :, :]))
                n += 1
            for dc in range(NDC):
                inc(g.dma_start(out=wcb[:, dc * ES:(dc + 1) * ES],
                                in_=wc_in.ap()[dc, :, :]))
                n += 1
            for dst, srcp in [
                (wresb[:, :], wre_in.ap()), (wimsb[:, :], wim_in.ap()),
                (nscsb[:, :], nsc_in.ap()), (qselb[:, :], qsel_in.ap()),
                (bcsb2[:, :], bc_in.ap()), (tiesb[:, :], tie_in.ap()),
            ]:
                inc(g.dma_start(out=dst, in_=srcp))
                n += 1
            return n
        ev_g2 = emit_multi("gpsimd", wl, bar1, g2_loads, 16 + NDC + 6)

        ev_bdz = V(bar1, lambda v: [v.memset(bdre[:, :], 0.0),
                                    v.memset(bdim[:, :], 0.0)][-1])
        ev_bd = V([ev_g2, ev_bdz], lambda v: [
            v.tensor_scalar_mul(bdre[0:64, 0:64], wresb[0:64, :],
                                nscsb[0:64, :]),
            v.tensor_scalar_mul(bdre[64:128, 64:128], wresb[64:128, :],
                                nscsb[0:64, :]),
            v.tensor_scalar_mul(bdim[0:64, 0:64], wimsb[0:64, :],
                                nscsb[64:128, :]),
            v.tensor_scalar_mul(bdim[64:128, 64:128], wimsb[64:128, :],
                                nscsb[64:128, :]),
        ][-1])

        ev_st_rd = None
        ev_gt_rd = None
        slot_free = [None] * 4
        sq_free = [None, None]
        yp_free = [None, None]
        se_free = [None, None]
        pout_free = [None, None]
        ev_oh_dma = None
        ev_q_rd = None
        tb2 = []
        x1_free = []
        ln2_free = []
        prev_rs = None
        snorm_rd = None
        op_rd = None

        for i in range(NT):
            seg = i // SEG_TILES
            tis = i % SEG_TILES

            ev_sld = SY([ev_st_rd] + bar1, lambda sy, i=i: sy.dma_start(
                out=stsb2[:, :], in_=stash.ap()[i, :, :]), ld)
            ev_gld = SY([ev_gt_rd], lambda sy, i=i: sy.dma_start(
                out=gtsb2[:, :], in_=gstash.ap()[i, :, :]), ld)
            ev_xld = SY([pout_free[0], pout_free[1]],
                        lambda sy, i=i: sy.dma_start(
                out=xtm2[:, :].rearrange("p (b d) -> p b d", d=D),
                in_=x_in.ap()[128 + i * TT:128 + (i + 1) * TT, :].rearrange(
                    "(b p) d -> p b d", p=128)), ld)

            if i > 0:
                ev_cu = V([], lambda v, i=i: v.tensor_add(
                    carry[:, :], carry[:, :],
                    partials[:, :].rearrange("p (s t) -> p t s",
                                             t=NT)[:, i - 1, :]))
            else:
                ev_cu = ev_cin if (1 in phases) else None

            last_scan = {}
            ev_ypre = {}
            for j in range(NXC):
                sre_slot = (2 * j) % 4
                sim_slot = (2 * j + 1) % 4

                sfre = slot_free[sre_slot] or []
                sfim = slot_free[sim_slot] or []
                ev_s_re = V([ev_sld, ev_cu] + sfre,
                            lambda v, j=j, s=sre_slot: v.tensor_tensor_scan(
                    numrot[:, s * TT:(s + 1) * TT],
                    stsb2[:, st_re(j) * TT:(st_re(j) + 1) * TT],
                    stsb2[:, st_re(j) * TT:(st_re(j) + 1) * TT],
                    carry[:, st_re(j):st_re(j) + 1], Alu.add, Alu.bypass))
                ev_s_im = V(sfim,
                            lambda v, j=j, s=sim_slot: v.tensor_tensor_scan(
                    numrot[:, s * TT:(s + 1) * TT],
                    stsb2[:, st_im(j) * TT:(st_im(j) + 1) * TT],
                    stsb2[:, st_im(j) * TT:(st_im(j) + 1) * TT],
                    carry[:, st_im(j):st_im(j) + 1], Alu.add, Alu.bypass))

                ev_q_re = S([ev_s_re, sq_free[0]], lambda s, j=j,
                            ss=sre_slot: s.activation(
                    sqrot[:, 0:TT], numrot[:, ss * TT:(ss + 1) * TT],
                    Act.Square, scale=cst(1.0), bias=cst(0.0)))
                ev_q_im = S([ev_s_im, sq_free[1]], lambda s, j=j,
                            ss=sim_slot: s.activation(
                    sqrot[:, TT:2 * TT], numrot[:, ss * TT:(ss + 1) * TT],
                    Act.Square, scale=cst(1.0), bias=cst(0.0)))

                ev_qm_re = T([ev_q_re, ev_q_rd], lambda t, j=j: MM(t, 
                    bank(0, TT, 32), qselb[:, j * 32:(j + 1) * 32],
                    sqrot[:, 0:TT], start=(j == 0), stop=False))
                ev_qm_im = T([ev_q_im], lambda t, j=j: MM(t, 
                    bank(0, TT, 32), qselb[:, j * 32:(j + 1) * 32],
                    sqrot[:, TT:2 * TT], start=False, stop=(j == NXC - 1)))
                sq_free[0] = ev_qm_re
                sq_free[1] = ev_qm_im

                ypb = 1 + (j % 2)
                ev_yp = T([ev_s_re, ev_s_im, ev_bd, yp_free[j % 2]],
                          lambda t, j=j, ypb=ypb, sr=sre_slot,
                          si=sim_slot: [
                    MM(t, bank(ypb, TT), bdre[:, :],
                             numrot[:, sr * TT:(sr + 1) * TT],
                             start=True, stop=False),
                    MM(t, bank(ypb, TT), bdim[:, :],
                             numrot[:, si * TT:(si + 1) * TT],
                             start=False, stop=True),
                ][-1])
                ev_ype = V([ev_yp], lambda v, j=j, ypb=ypb: v.tensor_copy(
                    ypsb[:, j * TT:(j + 1) * TT], bank(ypb, TT)))
                yp_free[j % 2] = ev_ype
                slot_free[sre_slot] = [ev_ype, ev_q_re]
                slot_free[sim_slot] = [ev_ype, ev_q_im]
                ev_ypre[j] = ev_ype

            ev_den = V([ev_sld, ev_cu], lambda v: v.tensor_tensor_scan(
                densb[:, :], stsb2[0:K, ST_PW * TT:ST_PW * TT + TT],
                stsb2[0:K, ST_PW * TT:ST_PW * TT + TT],
                carry[0:K, ST_PW:ST_PW + 1], Alu.add, Alu.bypass))
            ev_st_rd = ev_den
            def v_inv(v):
                fence(v, v.tensor_scalar_max(invd[:, :], densb[:, :], 1e-4))
                return v.reciprocal(invd[:, :], invd[:, :])
            ev_inv = V([ev_den], v_inv)
            def v_t2(v):
                fence(v, v.tensor_tensor(t2sb[:, 0:TT], bank(0, TT, 32),
                                         invd[:, :], Alu.mult))
                return v.tensor_tensor(t2sb[:, 0:TT], t2sb[:, 0:TT],
                                       invd[:, :], Alu.mult)
            ev_t2 = V([ev_inv, sq_free[1], prev_rs], v_t2)
            ev_q_rd = ev_t2
            ev_rs = S([ev_t2], lambda s: [
                s.activation(t2sb[:, TT:2 * TT], t2sb[:, 0:TT], Act.Ln,
                             scale=cst(1.0 / (2.0 * Hd), 32),
                             bias=cst(1e-5, 32)),
                s.activation(t2sb[:, TT:2 * TT], t2sb[:, TT:2 * TT],
                             Act.Exp, scale=cst(-0.5, 32), bias=cst(0.0, 32)),
            ][-1])
            prev_rs = ev_rs
            ev_sn = V([ev_rs, snorm_rd], lambda v: v.tensor_tensor(
                snormb[:, :], t2sb[:, TT:2 * TT], invd[:, :], Alu.mult))

            ev_yg = {}
            for j in range(NXC):
                seb = 3
                ev_se = T([ev_sn, se_free[0]], lambda t, j=j, seb=seb:
                          MM(t, bank(seb, TT), eselb[:, j * 128:
                                                        (j + 1) * 128],
                                   snormb[:, :], start=True, stop=True))
                ev_y1 = V([ev_se, ev_ypre[j]], lambda v, j=j, seb=seb:
                          v.tensor_tensor(y1b[:, (j % 2) * TT:
                                              (j % 2 + 1) * TT],
                                          ypsb[:, j * TT:(j + 1) * TT],
                                          bank(seb, TT), Alu.mult))
                ev_yg[j] = V([ev_y1, ev_gld, op_rd if j == 0 else None],
                             lambda v, j=j: v.tensor_tensor(
                    ygsb[:, j * TT:(j + 1) * TT],
                    y1b[:, (j % 2) * TT:(j % 2 + 1) * TT],
                    gtsb2[:, j * TT:(j + 1) * TT], Alu.mult))
                se_free[0] = ev_y1
                snorm_rd = ev_se
            ev_gt_rd = ev_yg[NXC - 1]

            ev_x1 = {}
            for b in range(BPT):
                evs_po = [ev_yg[j] for j in range(NXC)]
                def t_op(t, b=b):
                    last = None
                    for ch in range(16):
                        for hf in range(2):
                            last = MM(t, 
                                bank(4 + hf, 512),
                                ygsb[:, ch * TT + b * 128:
                                     ch * TT + (b + 1) * 128],
                                woutb[:, ch * D + hf * 512:
                                      ch * D + hf * 512 + 512],
                                start=(ch == 0), stop=(ch == 15))
                    return last
                ev_po = T(evs_po + [pout_free[0], pout_free[1], ev_g2],
                          t_op)
                op_rd = ev_po
                ev_r = V([ev_po, ev_xld] + x1_free, lambda v, b=b: [
                    v.tensor_tensor(x1sb[:, b * D:b * D + 512],
                                    bank(4, 512),
                                    xtm2[:, b * D:b * D + 512], Alu.add),
                    v.tensor_tensor(x1sb[:, b * D + 512:(b + 1) * D],
                                    bank(5, 512),
                                    xtm2[:, b * D + 512:(b + 1) * D],
                                    Alu.add),
                ][-1])
                pout_free[0] = ev_r
                pout_free[1] = ev_r
                ev_x1[b] = ev_r
            ev_x1_all = [ev_x1[b] for b in range(BPT)]
            ev_xst = SY(ev_x1_all, lambda sy, i=i: sy.dma_start(
                out=out.ap()[i * TT:(i + 1) * TT, :].rearrange(
                    "(b p) d -> p b d", p=128),
                in_=x1sb[:, :].rearrange("p (b d) -> p b d", d=D)), so)
            if "x1" in dev:
                G(ev_x1_all, lambda g, i=i: g.dma_start(
                    out=d_x1.ap()[i * TT:(i + 1) * TT, :].rearrange(
                        "(b p) d -> p b d", p=128),
                    in_=x1sb[:, :].rearrange("p (b d) -> p b d", d=D)), ao)

            ev_ln2, _ = ln_norm(x1sb, ln2tm, BPT, lnst2,
                                ev_x1_all + ln2_free, scp2)
            ev_l2st = SY([ev_ln2], lambda sy, i=i: sy.dma_start(
                out=ln2st.ap()[i, :, :], in_=ln2tm[:, :]), so)
            ev_fm2, _ = tm_to_fm(ln2tm, BPT,
                                 lambda b, dc: lnfm2[:, dc * TT + b * 128:
                                                     dc * TT + (b + 1) * 128],
                                 [ev_ln2] + ln2_free)

            def t_lg(t):
                last = None
                for dc in range(NDC):
                    last = MM(t, bank(0, TT), wcb[:, dc * ES:
                                                     (dc + 1) * ES],
                                    lnfm2[:, dc * TT:(dc + 1) * TT],
                                    start=(dc == 0), stop=(dc == NDC - 1))
                return last
            ev_lg = T([ev_fm2, ev_q_rd, ev_g2], t_lg)
            ev_lge = V([ev_lg], lambda v: v.scalar_tensor_tensor(
                lgsb[:, :].rearrange("p (g t) -> p g t", t=GRP),
                bank(0, TT).rearrange("p (g t) -> p g t", t=GRP),
                bcsb2[:, :],
                tiesb[:, :].unsqueeze(1).broadcast_to([128, GPT, GRP]),
                Alu.add, Alu.add))
            ev_q_rd = ev_lge
            ev_mx = V([ev_lge], lambda v: v.tensor_reduce(
                gmx[:, :], lgsb[:, :].rearrange("p (g t) -> p g t", t=GRP),
                AX, Alu.max))
            ev_oh = V([ev_mx, ev_oh_dma], lambda v, tis=tis: v.tensor_tensor(
                ohsb[:, tis * TT:(tis + 1) * TT].rearrange(
                    "p (g t) -> p g t", t=GRP),
                lgsb[:, :].rearrange("p (g t) -> p g t", t=GRP),
                gmx[:, :].unsqueeze(2).broadcast_to([128, GPT, GRP]),
                Alu.is_equal))

            if tis == SEG_TILES - 1:
                ev_oh_dma = SY([ev_oh], lambda sy, seg=seg: sy.dma_start(
                    out=ohst.ap()[seg, :, :], in_=ohsb[:, :]), so)
                if "oh" in dev:
                    G([ev_oh], lambda g, seg=seg: g.dma_start(
                        out=d_oh.ap()[seg, :, :], in_=ohsb[:, :]), ao)
            x1_free = [ se.ev(), ev_xst ]
            ln2_free = [ev_l2st, ev_lg]

        for c in reversed(p2_cm):
            c.__exit__(None, None, None)


    if 3 in phases:
        bar2 = barrier()
        p3_cm = [
            nc.sbuf_tensor("w2b", [128, NPACK * D], BF16),
            nc.sbuf_tensor("w1buf", [128, 2 * ES * EF], BF16),
            nc.sbuf_tensor("ln2seg", [128, SEG_TILES * BPT * D], BF16),
            nc.sbuf_tensor("permseg", [128, BLK_SEG * 512], BF16),
            nc.sbuf_tensor("ohsegb", [128, SEG_T], BF16),
            nc.sbuf_tensor("ohT", [128, 128], BF16),
            nc.sbuf_tensor("xdc", [128, BLK_SEG * 512], BF16),
            nc.sbuf_tensor("innersb", [128, NPACK * GSEG], BF16),
            nc.sbuf_tensor("padsb", [128, 2 * MSL], BF16),
            nc.sbuf_tensor("outsb", [128, 2 * D], BF16),
            nc.sbuf_tensor("outg", [128, 2 * D], BF16),
            nc.sbuf_tensor("b1pb", [128, NPACK], F32),
            nc.sbuf_tensor("b2pb", [128, D], BF16),
            nc.sbuf_tensor("bmaskb", [128, 4], BF16),
            nc.sbuf_tensor("x2c", [128, 2 * D], F32),
        ]
        (w2b, w1buf, ln2seg, permseg, ohsegb, ohT, xdc, innersb, padsb,
         outsb, outg, b1pb, b2pb, bmaskb, x2c) = \
            [c.__enter__() for c in p3_cm]

        def g3_loads(g, inc):
            n = 0
            for pk in range(NPACK):
                inc(g.dma_start(out=w2b[:, pk * D:(pk + 1) * D],
                                in_=w2p_in.ap()[pk, :, :]))
                n += 1
            for dst, srcp in [(b1pb[:, :], b1p_in.ap()),
                              (b2pb[:, :], b2p_in.ap()),
                              (bmaskb[:, :], bmask_in.ap())]:
                inc(g.dma_start(out=dst, in_=srcp))
                n += 1
            return n
        ev_g3 = emit_multi("gpsimd", wl, bar2, g3_loads, NPACK + 3)

        ev_perm_rd = None
        ev_oh_rd = None
        ev_w1_rd = [None, None]
        ev_xdc_rd = None
        ev_inner_rd = None
        ev_outsb_rd = None
        ev_outg_rd = [None, None]
        ev_x2c_rd = [None, None]
        ev_px2_rd = [None, None]

        for seg in range(NSEG):
            ev_ohl = SY([ev_oh_rd] + bar2, lambda sy, seg=seg: sy.dma_start(
                out=ohsegb[:, :], in_=ohst.ap()[seg, :, :]), ld)
            def l2l(sy, inc, seg=seg):
                for tis in range(SEG_TILES):
                    inc(sy.dma_start(
                        out=ln2seg[:, tis * BPT * D:(tis + 1) * BPT * D],
                        in_=ln2st.ap()[seg * SEG_TILES + tis, :, :]))
                return SEG_TILES
            ev_l2l = emit_multi("sync", ld, [ev_perm_rd] + bar2, l2l,
                                SEG_TILES)

            ev_pm = []
            for blk in range(BLK_SEG):
                evt = T([ev_ohl,
                         ev_pm[-2][0] if len(ev_pm) >= 2 else None],
                        lambda t, blk=blk: t.transpose(
                    pbf_slot(blk % 4), ohsegb[:, blk * 128:(blk + 1) * 128],
                    identb[:, :]))
                eve = V([evt, ev_perm_rd], lambda v, blk=blk: v.tensor_copy(
                    ohT[:, :], pbf_slot(blk % 4)))
                evz = V([eve], lambda v, blk=blk: v.memset(
                    permseg[:, blk * 512:(blk + 1) * 512], 0.0))
                def cpy(v, blk=blk):
                    last = None
                    for g_ in range(GPB):
                        last = v.tensor_copy(
                            permseg[32 * g_:32 * (g_ + 1),
                                    blk * 512 + g_ * 128:
                                    blk * 512 + (g_ + 1) * 128],
                            ohT[32 * g_:32 * (g_ + 1), :])
                    return last
                evc = V([evz], cpy)
                ev_pm.append((evc, evt))
            ev_perm = ev_pm[-1][0]

            if "perm" in dev and seg == 0:
                G([ev_perm, ev_l2l], lambda g: g.dma_start(
                    out=d_perm.ap(), in_=permseg[:, :]), ao)
                G([ev_perm, ev_l2l], lambda g: g.dma_start(
                    out=d_l2s.ap(), in_=ln2seg[:, :]), ao)
            ev_w1mm_last = None
            for dc in range(NDC):
                buf = dc % 2
                ev_wl1 = G([ev_g3, ev_w1_rd[buf]], lambda g, dc=dc, buf=buf:
                           g.dma_start(out=w1buf[:, buf * ES * EF:
                                                 (buf + 1) * ES * EF],
                                       in_=w1p_in.ap()[dc, :, :]), wl)
                ev_ga = {}
                for blk in range(BLK_SEG):
                    tis, b = blk // BPT, blk % BPT
                    evg = T([ev_perm, ev_l2l,
                             ev_ga[blk - 1][0] if blk - 1 in ev_ga else
                             ev_xdc_rd,
                             None if (blk or dc) else ev_px2_rd[0],
                             None if (blk or dc) else ev_x2c_rd[0],
                             None if (blk or dc) else ev_x2c_rd[1]],
                            lambda t, tis=tis, b=b, dc=dc, blk=blk: MM(t, 
                        bank(5, 512),
                        ln2seg[:, (tis * BPT + b) * D + dc * 128:
                               (tis * BPT + b) * D + (dc + 1) * 128],
                        permseg[:, blk * 512:(blk + 1) * 512],
                        start=True, stop=True))
                    eve = V([evg], lambda v, blk=blk: v.tensor_copy(
                        xdc[:, blk * 512:(blk + 1) * 512], bank(5, 512)))
                    ev_ga[blk] = (eve, evg)
                ev_xall = ev_ga[BLK_SEG - 1][0]

                if "xdc" in dev and seg == 0 and dc == 0:
                    G([ev_xall], lambda g: g.dma_start(
                        out=d_xdc.ap(), in_=xdc[:, :]), ao)
                for es in range(ES):
                    esl = es % 4
                    pk = es // 4
                    gates = [ev_xall, ev_wl1]
                    if dc == 0 and es == 0:
                        gates += [ev_inner_rd, ev_px2_rd[0]]
                    ev_w1mm_last = T(
                        gates,
                        lambda t, es=es, esl=esl, pk=pk, dc=dc, buf=buf:
                        MM(t, 
                            pall[32 * esl:32 * (esl + 1),
                                 pk * GSEG:(pk + 1) * GSEG],
                            w1buf[:, buf * ES * EF + es * EF:
                                  buf * ES * EF + (es + 1) * EF],
                            xdc[:, :].rearrange(
                                "p (blk g es) -> p es blk g",
                                g=GPB, es=128)[:, es, :, :],
                            start=(dc == 0 and
                                   (pk * GSEG * 4) % 2048 == 0),
                            stop=(dc == NDC - 1),
                            tile_position=(0, 32 * esl)))
                ev_w1_rd[buf] = ev_w1mm_last
                ev_xdc_rd = ev_w1mm_last

            def s_ie(s):
                last = None
                for pk in range(NPACK):
                    last = s.activation(
                        innersb[:, pk * GSEG:(pk + 1) * GSEG],
                        pall[:, pk * GSEG:(pk + 1) * GSEG], Act.Relu,
                        scale=cst(1.0), bias=b1pb[:, pk:pk + 1])
                return last
            ev_ie = S([ev_w1mm_last, ev_g3], s_ie)

            if "inner" in dev and seg == 0:
                G([ev_ie], lambda g: g.dma_start(
                    out=d_inner.ap(), in_=innersb[:, :]), ao)
            ev_obst = {}
            for sub in range(NSUB):
                ev_w2 = {}
                for pk in range(NPACK):
                    pr = pk % 2
                    bnk0 = 2 * pr
                    ev_pb = V([ev_ie, ev_w2.get(pk - 2, (None,) * 3)[0]],
                              lambda v, pk=pk, sub=sub, pr=pr:
                              v.tensor_tensor(
                        padsb[:, pr * MSL:(pr + 1) * MSL].rearrange(
                            "p (e g) -> p e g", e=4),
                        innersb[:, pk * GSEG + sub * SUB:
                                pk * GSEG + (sub + 1) * SUB].unsqueeze(
                            1).broadcast_to([128, 4, SUB]),
                        bmaskb[:, :].unsqueeze(2).broadcast_to(
                            [128, 4, SUB]),
                        Alu.mult))

                    def t_w2(t, pk=pk, pr=pr, bnk0=bnk0):
                        last = None
                        for hf in range(2):
                            last = MM(t, 
                                pall[0:MSL, (bnk0 + hf) * 512:
                                     (bnk0 + hf) * 512 + 512],
                                padsb[:, pr * MSL:(pr + 1) * MSL],
                                w2b[:, pk * D + hf * 512:
                                    pk * D + hf * 512 + 512],
                                start=True, stop=True)
                        return last
                    evm = T([ev_pb, ev_g3,
                             ev_w2.get(pk - 2, (None,) * 3)[0],
                             ev_outsb_rd if pk < 2 else None,
                             ev_ie if pk < 2 else None], t_w2)

                    prev_st = ev_obst.get((sub, pk - 2)) or \
                        ev_obst.get((sub - 1, pk + NPACK - 2))
                    if pr == 0:
                        def v_ev(v, pk=pk, bnk0=bnk0, pr=pr):
                            v.tensor_copy(
                                outsb[0:MSL, pr * D:pr * D + 512],
                                pall[0:MSL, bnk0 * 512:bnk0 * 512 + 512])
                            return v.tensor_copy(
                                outsb[0:MSL, pr * D + 512:(pr + 1) * D],
                                pall[0:MSL, (bnk0 + 1) * 512:
                                     (bnk0 + 1) * 512 + 512])
                        eve = V([evm, prev_st], v_ev)
                    else:
                        def s_ev(s, pk=pk, bnk0=bnk0, pr=pr):
                            s.activation(
                                outsb[0:MSL, pr * D:pr * D + 512],
                                pall[0:MSL, bnk0 * 512:bnk0 * 512 + 512],
                                Act.Identity, scale=cst(1.0, MSL),
                                bias=cst(0.0, MSL))
                            return s.activation(
                                outsb[0:MSL, pr * D + 512:(pr + 1) * D],
                                pall[0:MSL, (bnk0 + 1) * 512:
                                     (bnk0 + 1) * 512 + 512],
                                Act.Identity, scale=cst(1.0, MSL),
                                bias=cst(0.0, MSL))
                        eve = S([evm, prev_st], s_ev)
                    ev_st_ob = emit("sync", sq, [eve],
                                    lambda sy, pk=pk, pr=pr: sy.dma_start(
                        out=ob.ap()[pk // 2, pk % 2].rearrange(
                            "e g d -> (e g) d"),
                        in_=outsb[0:MSL, pr * D:(pr + 1) * D]))
                    ev_obst[(sub, pk)] = ev_st_ob
                    ev_w2[pk] = (eve, evm, ev_st_ob)
                ev_ob_all = [ev_obst[(sub, NPACK - 2)],
                             ev_obst[(sub, NPACK - 1)]]

                ev_og_last = None
                for bls in range(SUB // GPB):
                    blk = sub * (SUB // GPB) + bls

                    def t_b2(t, blk=blk):
                        last = None
                        for hf in range(2):
                            last = MM(t,
                                pall[:, (4 + hf) * 512:
                                     (4 + hf) * 512 + 512],
                                ohsegb[:, blk * 128:(blk + 1) * 128],
                                b2pb[:, hf * 512:(hf + 1) * 512],
                                start=True, stop=False)
                        return last
                    evb2 = T([ev_g3, ev_px2_rd[0], ev_ie], t_b2)

                    gev = []
                    for gl in range(GPB):
                        g_loc = bls * GPB + gl
                        g_glob = sub * SUB + g_loc
                        slot = gl % 2

                        ev_og = SY(ev_ob_all + [ev_outg_rd[slot]],
                                   lambda sy, g_loc=g_loc, slot=slot:
                                   sy.dma_start(
                            out=outg[:, slot * D:(slot + 1) * D],
                            in_=ob.ap()[:, :, :, g_loc, :].rearrange(
                                "q p e d -> (q p e) d")), ld)
                        ev_og_last = ev_og
                        if "outg" in dev and seg == 0 and sub == 0 and g_glob == 0:
                            G([ev_og], lambda g: g.dma_start(
                                out=d_outg.ap(),
                                in_=outg[:, 0 * D:1 * D]), ao)

                        def t_sc(t, g_glob=g_glob, gl=gl, slot=slot):
                            last = None
                            for hf in range(2):
                                last = MM(t,
                                    pall[32 * gl:32 * (gl + 1),
                                         (4 + hf) * 512:
                                         (4 + hf) * 512 + 512],
                                    ohsegb[:, g_glob * 32:
                                           (g_glob + 1) * 32],
                                    outg[:, slot * D + hf * 512:
                                         slot * D + hf * 512 + 512],
                                    start=False, stop=(gl == GPB - 1),
                                    tile_position=(0, 32 * gl))
                            return last
                        evsc = T([ev_og, evb2], t_sc)
                        ev_outg_rd[slot] = evsc
                        gev.append(evsc)

                    evx2 = V(gev + [ev_x2c_rd[bls % 2]],
                             lambda v, bls=bls: [
                        v.tensor_copy(x2c[:, (bls % 2) * D:
                                          (bls % 2) * D + 512],
                                      bank(4, 512)),
                        v.tensor_copy(x2c[:, (bls % 2) * D + 512:
                                          (bls % 2 + 1) * D],
                                      bank(5, 512)),
                    ][-1])
                    ev_px2_rd[0] = evx2
                    row0 = seg * SEG_T + blk * 128
                    ev_ac = G([evx2], lambda g, row0=row0, bls=bls:
                              g.dma_start(
                        out=out.ap()[row0:row0 + 128, :],
                        in_=x2c[:, (bls % 2) * D:(bls % 2 + 1) * D],
                        accum_op=Alu.add), ao)
                    ev_x2c_rd[bls % 2] = ev_ac
                ev_obst[(sub, NPACK - 2)] = ev_og_last
                ev_obst[(sub, NPACK - 1)] = ev_og_last
                ev_outsb_rd = te.ev()
            ev_perm_rd = te.ev()
            ev_oh_rd = te.ev()
            ev_inner_rd = te.ev()

        for c in reversed(p3_cm):
            c.__exit__(None, None, None)

    with nc.Block() as block:
        @block.sync
        def _(eng):
            for th in prog["sync"]:
                th(eng)
            eng.wait_ge(S_so, so.n)

        @block.gpsimd
        def _(eng):
            for th in prog["gpsimd"]:
                th(eng)
            eng.wait_ge(S_ao, ao.n)

        @block.tensor
        def _(eng):
            for th in prog["tensor"]:
                th(eng)

        @block.vector
        def _(eng):
            for th in prog["vector"]:
                th(eng)

        @block.scalar
        def _(eng):
            for th in prog["scalar"]:
                th(eng)

    for c in reversed(per_cm + sem_cm):
        c.__exit__(None, None, None)
    return nc


def stage_core(inputs, core, TPC, TT, seq_len, n_cores):
    NT = TPC // TT
    seq = core // 2 if n_cores > 1 else 0
    half = core % 2 if n_cores > 1 else 0
    t0 = half * TPC

    xfull = np.asarray(inputs["L0_attn_x"], dtype=np.float32)
    xs = xfull[seq]
    x_ext = np.zeros((TPC + 128, D), np.float32)
    x_ext[128:] = xs[t0:t0 + TPC]
    if t0 >= 3:
        x_ext[125:128] = xs[t0 - 3:t0]

    W_in = np.asarray(inputs["L0_attn_in_proj_w"], np.float32)
    conv_k = np.asarray(inputs["L0_attn_conv_k"], np.float32)
    conv_b = np.asarray(inputs["L0_attn_conv_b"], np.float32)
    theta = np.asarray(inputs["L0_attn_theta"], np.float32).reshape(K, Hd)
    dsl = np.asarray(inputs["L0_attn_decay_slopes"], np.float32)
    asl = np.asarray(inputs["L0_attn_anchor_slopes"], np.float32)
    ssc = np.asarray(inputs["L0_attn_score_scale"], np.float32)
    dlg = np.asarray(inputs["L0_attn_deriv_logits"], np.float32)
    W_re = np.asarray(inputs["L0_attn_W_re"], np.float32)
    W_im = np.asarray(inputs["L0_attn_W_im"], np.float32)
    nsc = np.asarray(inputs["L0_attn_norm_scale"], np.float32)
    W_out = np.asarray(inputs["L0_attn_out_proj_w"], np.float32)
    Wc = np.asarray(inputs["L0_ffn_Wc"], np.float32)
    bc = np.asarray(inputs["L0_ffn_bc"], np.float32)
    W1 = np.asarray(inputs["L0_ffn_W1"], np.float32)
    b1 = np.asarray(inputs["L0_ffn_b1"], np.float32)
    W2 = np.asarray(inputs["L0_ffn_W2"], np.float32)
    b2 = np.asarray(inputs["L0_ffn_b2"], np.float32)

    convd = np.zeros((128, KS * NZC * 128), np.float32)
    for tap in range(KS):
        for oc in range(NZC):
            pc = 128 if oc < NZC - 1 else 32
            w = conv_k[tap, 0, oc * 128:oc * 128 + pc]
            blk = np.zeros((128, 128), np.float32)
            blk[:pc, :pc] = np.diag(w)
            convd[:, (tap * NZC + oc) * 128:(tap * NZC + oc + 1) * 128] = blk

    cb_fm = np.zeros((128, NZC), np.float32)
    for oc in range(NZC):
        pc = 128 if oc < NZC - 1 else 32
        cb_fm[:pc, oc] = conv_b[oc * 128:oc * 128 + pc]

    theta_fm = np.zeros((128, NXC), np.float32)
    for j in range(NXC):
        for p in range(128):
            theta_fm[p, j] = theta[2 * j + (p >= 64), p % 64]

    esel = np.zeros((32, NXC * 128), np.float32)
    for j in range(NXC):
        for p in range(128):
            esel[2 * j + (p >= 64), j * 128 + p] = 1.0
    qsel = np.zeros((128, NXC * 32), np.float32)
    for j in range(NXC):
        for p in range(128):
            qsel[p, j * 32 + 2 * j + (p >= 64)] = 1.0
    bmask = np.zeros((128, 4), np.float32)
    for p in range(128):
        bmask[p, p // 32] = 1.0
    tie = np.broadcast_to(
        np.linspace(0.0, 1e-6, GRP, dtype=np.float32)[None, :],
        (128, GRP)).copy()
    iota_t = np.broadcast_to(
        (t0 + np.arange(TT, dtype=np.float32))[None, :], (K, TT)).copy()

    W1es = W1.reshape(ES, D, EF)
    w1p = np.zeros((NDC, 128, ES * EF), np.float32)
    for dc in range(NDC):
        w1p[dc] = W1es[:, dc * 128:(dc + 1) * 128, :].transpose(
            1, 0, 2).reshape(128, ES * EF)
    W2es = W2.reshape(ES, D, EF)
    w2p = np.zeros((NPACK, 128, D), np.float32)
    for pk in range(NPACK):
        for esl in range(4):
            w2p[pk, esl * EF:(esl + 1) * EF, :] = \
                W2es[4 * pk + esl].T
    b1es = b1.reshape(ES, EF)
    b1p = np.zeros((128, NPACK), np.float32)
    for pk in range(NPACK):
        for esl in range(4):
            b1p[esl * EF:(esl + 1) * EF, pk] = b1es[4 * pk + esl]
    b2p = b2.reshape(ES, D)

    return {
        "x": x_ext,
        "win": W_in.reshape(NDC, 128, C),
        "convd": convd,
        "cbias": cb_fm,
        "wout": W_out.reshape(16, 128, D),
        "theta_fm": theta_fm,
        "dslope": np.concatenate([dsl, asl]).reshape(K, 1),
        "sscale": ssc.reshape(K, 1),
        "dlogit": dlg.reshape(1, 3),
        "wre2": np.concatenate([W_re, W_re], 0),
        "wim2": np.concatenate([W_im, W_im], 0),
        "nsc": nsc.reshape(128, 1),
        "wc": Wc.reshape(D, ES).reshape(NDC, 128, ES),
        "bcv": bc.reshape(ES, 1),
        "w1p": w1p, "w2p": w2p, "b1p": b1p, "b2p": b2p,
        "esel": esel, "qsel": qsel, "bmask": bmask, "tie": tie,
        "iota_t": iota_t,
        "twsgn": np.concatenate([np.ones(ND_K), -np.ones(4)]).reshape(
            K, 1).astype(np.float32),
        "twbm": np.concatenate([np.full(ND_K, -float(seq_len - 1)),
                                np.zeros(4)]).reshape(K, 1).astype(
                                    np.float32),
        "flag_first": np.full((128, 1), 1.0 if half == 0 else 0.0,
                              np.float32),
        "ident": np.eye(128, dtype=np.float32),
    }


_TPC, _TT, _SEGT, _SL = 4096, 512, 4, 8192


def kernel(**inputs):
    nc = build_nc(_TPC, _TT, _SEGT, _SL, phases=(1, 2, 3), dev=False)
    in_maps = [stage_core(inputs, c, _TPC, _TT, _SL, N_CORES)
               for c in range(N_CORES)]
    res = run_bass_kernel_spmd(nc, in_maps, core_ids=list(range(N_CORES)))
    out = np.zeros((4, _SL, D), np.float32)
    for c in range(N_CORES):
        seq, half = c // 2, c % 2
        out[seq, half * _TPC:(half + 1) * _TPC] = \
            np.asarray(res.results[c]["out"]).reshape(_TPC, D)
    return out


# version 08
# speedup vs baseline: 1.0105x; 1.0016x over previous
import math
import os
import numpy as np

import concourse.bass as bass
import concourse.mybir as mybir
from concourse.bass_utils import run_bass_kernel_spmd

F32 = mybir.dt.float32
BF16 = mybir.dt.bfloat16
Alu = mybir.AluOpType
Act = mybir.ActivationFunctionType
AX = mybir.AxisListType.X

D = 1024
DI = 2048
K, Hd = 32, 64
KS = 4
C = 2 * DI + K
NXC = DI // 128
NZC = 2 * NXC + 1
NDC = D // 128
ES, EF = 128, 32
GRP = 32
NPACK = 32
MAGIC = 1.5 * 2 ** 23
TWO_PI = 2.0 * math.pi
ND_K = K - 4
ST_PW = 2 * NXC
N_CORES = 8


def st_re(j):
    return 2 * j


def st_im(j):
    return 2 * j + 1


def MM(t, *a, **kw):
    kw.setdefault("skip_group_check", True)
    return t.matmul(*a, **kw)


def build_nc(TPC, TT, SEG_TILES, seq_len, phases=(1, 2, 3), dev=False):
    if dev is True:
        dev = {"stash", "gst", "part", "x1", "oh", "xdc", "inner", "outg",
               "perm", "l2s", "slp", "tw", "pw", "z", "lnfm"}
    elif not dev:
        dev = set()
    NT = TPC // TT
    BPT = TT // 128
    NSEG = NT // SEG_TILES
    SEG_T = SEG_TILES * TT
    GSEG = SEG_T // GRP
    BLK_SEG = SEG_T // 128
    GPT = TT // GRP
    GPB = 128 // GRP
    SUB = 16 if GSEG >= 16 else GSEG
    NSUB = GSEG // SUB
    MSL = 4 * SUB
    NPG = 8
    NPGRP = NPACK // NPG
    ZW = TT + 3

    nc = bass.Bass("TRN2", target_bir_lowering=False,
                   detect_race_conditions=False)

    def din(n, s, d=F32):
        return nc.dram_tensor(n, s, d, kind="ExternalInput")

    x_in = din("x", [TPC + 128, D])
    win = din("win", [NDC, 128, C])
    convd = din("convd", [128, KS * NZC * 128])
    cbias_in = din("cbias", [128, NZC])
    wout_in = din("wout", [16, 128, D])
    theta_in = din("theta_fm", [128, NXC])
    dslope = din("dslope", [K, 1])
    sscale = din("sscale", [K, 1])
    dlogit = din("dlogit", [1, 3])
    wre_in = din("wre2", [128, 64])
    wim_in = din("wim2", [128, 64])
    nsc_in = din("nsc", [128, 1])
    wc_in = din("wc", [NDC, 128, ES])
    bc_in = din("bcv", [ES, 1])
    w1p_in = din("w1p", [NDC, 128, ES * EF])
    w2p_in = din("w2p", [NPACK, 128, D])
    b1p_in = din("b1p", [128, NPACK])
    b2p_in = din("b2p", [128, D])
    esel_in = din("esel", [32, NXC * 128])
    qsel_in = din("qsel", [128, NXC * 32])
    bmask_in = din("bmask", [128, 4])
    tie_in = din("tie", [128, GRP])
    iota_in = din("iota_t", [K, TT])
    twsgn_in = din("twsgn", [K, 1])
    twbm_in = din("twbm", [K, 1])
    flag_in = din("flag_first", [128, 1])
    ident_in = din("ident", [128, 128])

    out = nc.dram_tensor("out", [TPC, D], F32, kind="ExternalOutput")

    stash = nc.dram_tensor("stash", [NT, 128, NZC * TT], BF16)
    gstash = nc.dram_tensor("gstash", [NT, 128, NXC * TT], BF16)
    ln2st = nc.dram_tensor("ln2st", [NT, 128, BPT * D], BF16)
    ohst = nc.dram_tensor("ohst", [max(NSEG, 1), 128, SEG_T], BF16)
    cc_i = nc.dram_tensor("cc_i", [128, NZC], F32)
    cc_o = nc.dram_tensor("cc_o", [128, NZC], F32)
    ob = nc.dram_tensor("ob", [16, 2, 4, SUB, D], BF16)
    if dev:
        d_stash = nc.dram_tensor("d_stash", [NT, 128, NZC * TT], F32,
                                 kind="ExternalOutput")
        d_gst = nc.dram_tensor("d_gst", [NT, 128, NXC * TT], F32,
                               kind="ExternalOutput")
        d_part = nc.dram_tensor("d_part", [128, NZC * NT], F32,
                                kind="ExternalOutput")
        d_x1 = nc.dram_tensor("d_x1", [TPC, D], F32, kind="ExternalOutput")
        d_oh = nc.dram_tensor("d_oh", [max(NSEG, 1), 128, SEG_T], F32,
                              kind="ExternalOutput")
        d_xdc = nc.dram_tensor("d_xdc", [128, BLK_SEG * 512], F32,
                               kind="ExternalOutput")
        d_inner = nc.dram_tensor("d_inner", [128, NPACK * GSEG], F32,
                                 kind="ExternalOutput")
        d_outg = nc.dram_tensor("d_outg", [128, D], F32,
                                kind="ExternalOutput")
        d_perm = nc.dram_tensor("d_perm", [128, BLK_SEG * 512], F32,
                                kind="ExternalOutput")
        d_l2s = nc.dram_tensor("d_l2s", [128, SEG_TILES * BPT * D], F32,
                               kind="ExternalOutput")
        d_slp = nc.dram_tensor("d_slp", [K, 8], F32, kind="ExternalOutput")
        d_tw = nc.dram_tensor("d_tw", [K, NT + 2], F32,
                              kind="ExternalOutput")
        d_pw = nc.dram_tensor("d_pw", [K, 2 * TT], F32,
                              kind="ExternalOutput")
        d_z = nc.dram_tensor("d_z", [128, 5 * (TT + 3)], F32,
                             kind="ExternalOutput")
        d_lnfm = nc.dram_tensor("d_lnfm", [128, NDC * TT], F32,
                                kind="ExternalOutput")
        d_carry = nc.dram_tensor("d_carry", [128, 2 * NZC], F32,
                                 kind="ExternalOutput")

    prog = {k: [] for k in ("sync", "gpsimd", "tensor", "vector", "scalar")}

    class Ctr:
        def __init__(self, h, step=1):
            self.h, self.n, self.step = h, 0, step

        def inc(self, instr, k=None):
            k = self.step if k is None else k
            instr.then_inc(self.h, k)
            self.n += k

        def ev(self):
            return (self.h, self.n)

    sem_cm = [nc.semaphore(n) for n in
              ("ld", "wl", "te", "ve", "se", "ge", "so", "ao", "ccs", "sq",
               "chv", "chs")]
    sems = [s.__enter__() for s in sem_cm]
    (S_ld, S_wl, S_te, S_ve, S_se, S_ge, S_so, S_ao, S_cc, S_sq,
     S_chv, S_chs) = sems
    _chain_n = {"v": 0, "s": 0}

    def fence(eng, instr, which="v"):
        _chain_n[which] += 1
        h = S_chv if which == "v" else S_chs
        instr.then_inc(h, 1)
        eng.wait_ge(h, _chain_n[which])
    ld = Ctr(S_ld, 16); wl = Ctr(S_wl, 16); te = Ctr(S_te); ve = Ctr(S_ve)
    se = Ctr(S_se); ge = Ctr(S_ge); so = Ctr(S_so, 16); ao = Ctr(S_ao, 16)
    ccc = Ctr(S_cc); sq = Ctr(S_sq, 16)

    def emit_multi(engname, ctr, after, body, n):
        evs = tuple(e for e in after if e is not None and e[1] > 0)
        ctr.n += ctr.step * n
        h, st = ctr.h, ctr.step

        def th(eng, _evs=evs, _body=body):
            for hh, nn in _evs:
                eng.wait_ge(hh, nn)
            cnt = _body(eng, lambda ins: ins.then_inc(h, st))
            assert cnt == n, f"emit_multi expected {n} DMAs, got {cnt}"
        prog[engname].append(th)
        return ctr.ev()

    def emit(engname, ctr, after, body):
        evs = tuple(e for e in after if e is not None and e[1] > 0)
        if ctr is not None:
            ctr.n += ctr.step
            tgt = (ctr.h, ctr.step)
        else:
            tgt = None

        def th(eng, _evs=evs, _body=body, _tgt=tgt):
            for h, n in _evs:
                eng.wait_ge(h, n)
            last = _body(eng)
            if _tgt is not None:
                assert last is not None, "emit body returned no instruction"
                last.then_inc(_tgt[0], _tgt[1])
        prog[engname].append(th)
        return ctr.ev() if ctr is not None else None

    def T(after, body):
        return emit("tensor", te, after, body)

    def V(after, body):
        return emit("vector", ve, after, body)

    def S(after, body):
        return emit("scalar", se, after, body)

    def G(after, body, ctr=None):
        return emit("gpsimd", ctr if ctr is not None else ge, after, body)

    def SY(after, body, ctr=None):
        return emit("sync", ctr if ctr is not None else so, after, body)

    per_cm = [
        nc.sbuf_tensor("consts", [128, 16], F32),
        nc.sbuf_tensor("partials", [128, NZC * NT], F32),
        nc.sbuf_tensor("carry", [128, NZC], F32),
        nc.sbuf_tensor("flagsb", [128, 1], F32),
        nc.sbuf_tensor("smll", [128, 24], F32),
        nc.sbuf_tensor("identb", [128, 128], BF16),
        nc.sbuf_tensor("eselb", [32, NXC * 128], BF16),
        nc.sbuf_tensor("ccsb", [128, NZC], F32),
        nc.psum_tensor("pall", [128, 3072], F32),
        nc.psum_tensor("pbf", [128, 2048], BF16),
    ]
    (consts, partials, carry, flagsb, smll, identb, eselb, ccsb,
     pall, pbf) = [c.__enter__() for c in per_cm]

    CONST_VALS = [0.5, 1.0, MAGIC, -MAGIC, TWO_PI, math.pi / 2.0,
                  1e-5, -20.0, 20.0, 1.0 / (2.0 * Hd), 1e-4, -1.0,
                  0.0, -0.5, 1.0 / D]
    CV = {v: i for i, v in enumerate(CONST_VALS)}
    for v, i in CV.items():
        nc.const_aps.register(F32, v, consts[:, i:i + 1])

    def cst(v, p=128):
        return consts[0:p, CV[v]:CV[v] + 1]

    def bank(i, w=512, part=128, off=0):
        return pall[0:part, i * 512 + off:i * 512 + off + w]

    def pbf_slot(s, w=128):
        off = (s % 2) * 1024 + (s // 2) * 256
        return pbf[:, off:off + w]

    KP = os.environ.get("KPART", "")
    if os.environ.get("KFRAME"):
        ev0 = None
        ev_id = None
    elif KP == "id":
        ev0 = None
        ev_id = G([], lambda g: g.dma_start(out=identb[:, :],
                                            in_=ident_in.ap()), wl)
    elif KP == "c0":
        ev_id = None
        ev0 = V([], lambda v: [v.memset(consts[:, i:i + 1], val)
                               for val, i in CV.items()] and
                v.memset(partials[:, :], 0.0) and
                v.memset(carry[:, :], 0.0))
    elif KP == "c1":
        ev_id = None
        ev0 = V([], lambda v: v.memset(consts[:, 0:1], 0.5))
    else:
        ev0 = V([], lambda v: [v.memset(consts[:, i:i + 1], val)
                               for val, i in CV.items()] and
                v.memset(partials[:, :], 0.0) and
                v.memset(carry[:, :], 0.0))
        ev_id = G([], lambda g: g.dma_start(out=identb[:, :],
                                            in_=ident_in.ap()), wl)

    def ln_norm(src, dst, nb, stc, after, scr):
        LNCUT = int(os.environ.get("LNCUT", "9"))
        evm0 = V(after, lambda v: [
            v.reduce_sum(stc[:, b:b + 1], src[:, b * D:(b + 1) * D], AX)
            for b in range(nb)][-1])
        evm = V([evm0], lambda v: v.tensor_scalar_mul(
            stc[:, 0:nb], stc[:, 0:nb], 1.0 / D))
        if LNCUT < 2:
            return evm, evm
        evq = S(after, lambda s: [
            s.activation(scr[:, 0:D], src[:, b * D:(b + 1) * D], Act.Square,
                         scale=cst(1.0), bias=cst(0.0),
                         accum_out=stc[:, nb + b:nb + b + 1])
            for b in range(nb)][-1])
        if LNCUT < 3:
            return evq, evq
        def v_var(v):
            fence(v, v.tensor_scalar_mul(stc[:, nb:2 * nb],
                                         stc[:, nb:2 * nb], 1.0 / D))
            fence(v, v.tensor_mul(stc[:, 2 * nb:3 * nb], stc[:, 0:nb],
                                  stc[:, 0:nb]))
            return v.tensor_sub(stc[:, nb:2 * nb], stc[:, nb:2 * nb],
                                stc[:, 2 * nb:3 * nb])
        evv = V([evm, evq], v_var)
        if LNCUT < 4:
            return evv, evv
        evr = S([evv], lambda s: [
            s.activation(stc[:, 2 * nb:3 * nb], stc[:, nb:2 * nb], Act.Ln,
                         scale=cst(1.0), bias=cst(1e-5)),
            s.activation(stc[:, 2 * nb:3 * nb], stc[:, 2 * nb:3 * nb],
                         Act.Exp, scale=cst(-0.5), bias=cst(0.0)),
        ][-1])
        if LNCUT < 5:
            return evr, evr
        evb = V([evr], lambda v: v.scalar_tensor_tensor(
            stc[:, 3 * nb:4 * nb], stc[:, 0:nb], -1.0,
            stc[:, 2 * nb:3 * nb], Alu.mult, Alu.mult))
        if LNCUT < 6:
            return evb, evb
        evn = S([evb], lambda s: [
            s.activation(dst[:, b * D:(b + 1) * D], src[:, b * D:(b + 1) * D],
                         Act.Identity, scale=stc[:, 2 * nb + b:2 * nb + b + 1],
                         bias=stc[:, 3 * nb + b:3 * nb + b + 1])
            for b in range(nb)][-1])
        return evn, evr

    def tm_to_fm(src, nb, put, after):
        if not isinstance(after, list):
            after = [after]
        if os.environ.get("TFCUT"):
            e = V(after, lambda v: v.engine_nop())
            return e, [e]
        evs = []
        for b in range(nb):
            for dc in range(NDC):
                slot = (b * NDC + dc) % 4
                gate = [evs[-2][0]] if len(evs) >= 2 else []
                evt = T(after + gate,
                        lambda t, b=b, dc=dc, slot=slot: t.transpose(
                            pbf_slot(slot),
                            src[:, b * D + dc * 128:b * D + (dc + 1) * 128],
                            identb[:, :]))
                eve = V([evt], lambda v, b=b, dc=dc, slot=slot: v.tensor_copy(
                    put(b, dc), pbf_slot(slot)))
                evs.append((eve, evt))
        return evs[-1][0], [e[0] for e in evs]

    p1_cm = [
        nc.sbuf_tensor("theta2p", [128, NXC], F32),
        nc.sbuf_tensor("thetasb", [128, NXC], F32),
        nc.sbuf_tensor("slp", [K, 8], F32),
        nc.sbuf_tensor("twsc", [K, 1], F32),
        nc.sbuf_tensor("twbias", [K, NT], F32),
        nc.sbuf_tensor("sscb", [K, 1], F32),
        nc.sbuf_tensor("wpoly", [128, 3], F32),
        nc.sbuf_tensor("iotasb", [K, TT], F32),
        nc.sbuf_tensor("winb", [128, NDC * C], BF16),
        nc.sbuf_tensor("convb", [128, KS * NZC * 128], BF16),
        nc.sbuf_tensor("xtm", [128, BPT * D], F32),
        nc.sbuf_tensor("lntm", [128, BPT * D], BF16),
        nc.sbuf_tensor("lnfm", [128, NDC * TT], BF16),
        nc.sbuf_tensor("zsb", [128, 6 * ZW], BF16),
        nc.sbuf_tensor("zhalo", [128, NZC * 3], BF16),
        nc.sbuf_tensor("stsb", [128, NZC * TT], BF16),
        nc.sbuf_tensor("gtsb", [128, (NXC // 2) * TT], BF16),
        nc.sbuf_tensor("lnst", [128, 4 * 4], F32),
        nc.sbuf_tensor("pwsb", [K, 2 * TT], F32),
        nc.sbuf_tensor("scp1", [128, max(D, 2 * TT)], F32),
        nc.sbuf_tensor("chainr", [128, 2 * TT], F32),
        nc.sbuf_tensor("sincos", [128, 2 * TT], BF16),
        nc.sbuf_tensor("polyb", [128, TT], BF16),
        nc.sbuf_tensor("remb", [128, 2 * TT], BF16),
        nc.sbuf_tensor("ones_row", [1, 128], F32),
        nc.sbuf_tensor("cbsb", [128, NZC], F32),
        nc.sbuf_tensor("usb", [128, TT], F32),
    ]
    (theta2p, thetasb, slp, twsc, twbias, sscb, wpoly, iotasb, winb, convb,
     xtm, lntm, lnfm, zsb, zhalo, stsb, gtsb, lnst, pwsb,
     scp1, chainr, sincos, polyb, remb, ones_row, cbsb, usb) = \
        [c.__enter__() for c in p1_cm]

    P1SETUP = int(os.environ.get("P1SETUP", "99"))
    if 1 in phases and not os.environ.get("KFRAME"):
        def g_loads(g, inc):
            n = 0
            for dst, srcp in [
                (theta2p[:, :], theta_in.ap()),
                (thetasb[:, :], theta_in.ap()),
                (slp[:, 0:1], dslope.ap()),
                (sscb[:, :], sscale.ap()),
                (smll[0:1, 0:3], dlogit.ap()),
                (flagsb[:, :], flag_in.ap()),
                (iotasb[:, :], iota_in.ap()),
                (slp[:, 7:8], twsgn_in.ap()),
                (smll[0:K, 12:13], twbm_in.ap()),
                (convb[:, :], convd.ap()),
                (eselb[:, :], esel_in.ap()),
                (cbsb[:, :], cbias_in.ap()),
            ]:
                inc(g.dma_start(out=dst, in_=srcp))
                n += 1
            for dc in range(NDC):
                inc(g.dma_start(out=winb[:, dc * C:(dc + 1) * C],
                                in_=win.ap()[dc, :, :]))
                n += 1
            return n
        ev_gl = emit_multi("gpsimd", wl, [], g_loads, 12 + NDC) if P1SETUP >= 1 else None

        ev_or = V([ev0], lambda v: v.memset(ones_row[:, :], 1.0)) if P1SETUP >= 2 else None
        ev_stz = V([], lambda v: v.memset(stsb[:, :], 0.0)) if P1SETUP >= 3 else None
        ev1 = None if P1SETUP < 4 else V([ev_gl, ev0], lambda v: [
            v.tensor_scalar_mul(theta2p[:, :], theta2p[:, :], 1.0 / TWO_PI),
            v.scalar_tensor_tensor(slp[:, 1:2], slp[:, 0:1], -1.0,
                                   slp[:, 0:1], Alu.mult, Alu.max),
            v.tensor_scalar_max(slp[:, 2:3], slp[:, 0:1], 0.0),
        ][-1])
        ev2 = None if P1SETUP < 5 else S([ev1], lambda s: [
            s.activation(slp[:, 3:4], slp[:, 1:2], Act.Exp,
                         scale=cst(-1.0, K), bias=cst(0.0, K)),
            s.activation(slp[:, 4:5], slp[:, 3:4], Act.Ln,
                         scale=cst(1.0, K), bias=cst(1.0, K)),
            s.activation(smll[0:1, 4:7], smll[0:1, 0:3], Act.Exp,
                         scale=cst(1.0, 1), bias=cst(0.0, 1)),
        ][-1])

        def v_setup2(v):
            fence(v, v.tensor_add(slp[:, 5:6], slp[:, 2:3],
                                  slp[:, 4:5]))
            v.tensor_mul(twsc[:, :], slp[:, 5:6], slp[:, 7:8])
            fence(v, v.tensor_mul(slp[:, 6:7], slp[:, 5:6],
                                  smll[0:K, 12:13]))
            for i in range(NT):
                v.scalar_tensor_tensor(twbias[:, i:i + 1], twsc[:, :],
                                       float(i * TT), slp[:, 6:7],
                                       Alu.mult, Alu.add)
            return v.reduce_sum(smll[0:1, 7:8], smll[0:1, 4:7], AX)
        ev3a = V([ev2], v_setup2) if P1SETUP >= 6 else None
        ev3b = None if P1SETUP < 6 else V([ev3a], lambda v: v.reciprocal(
            smll[0:1, 8:9], smll[0:1, 7:8]))
        ev3 = None if P1SETUP < 6 else V([ev3b], lambda v:
            v.scalar_tensor_tensor(
                smll[0:1, 9:12], smll[0:1, 4:7], 1.0,
                smll[0:1, 8:9].broadcast_to([1, 3]), Alu.mult, Alu.mult))
        ev4 = T([ev3, ev_or], lambda t: MM(t, 
            bank(0, 3), ones_row[:, :], smll[0:1, 9:12],
            start=True, stop=True))
        ev_wp = None if P1SETUP < 8 else V([ev4], lambda v: [
            v.tensor_copy(wpoly[:, 0:2], bank(0, 2)),
            v.tensor_scalar_mul(wpoly[:, 2:3], bank(0, 1, off=2), -1.0),
        ][-1])

        xpfx = xtm
        lnpfx = lnfm
        SKIP_PFX = os.environ.get('P1CUT') in ('nopfx', 'setup')
        if not SKIP_PFX:
            evx = SY([], lambda sy: sy.dma_start(out=xpfx[:, 0:D],
                                                 in_=x_in.ap()[0:128, :]),
                     ld)
            evn, _ = ln_norm(xtm, lntm, 1, lnst, [evx, ev0, ev_id], scp1)
            evfm, _ = tm_to_fm(lntm, 1,
                               lambda b, dc: lnfm[:, dc * 128:dc * 128 + 128],
                               evn)
        else:
            evn = evfm = None
        ev_zh = []
        for oc in (range(NZC) if not SKIP_PFX and os.environ.get('P1CUT') != 'zh' else []):
            pc = 128 if oc < NZC - 1 else 32
            bnk = oc % 2

            def t_pfx(t, oc=oc, pc=pc, bnk=bnk):
                last = None
                for dc in range(NDC):
                    last = MM(t, 
                        bank(bnk, 128, pc), winb[:, dc * C + oc * 128:
                                                 dc * C + oc * 128 + pc],
                        lnpfx[:, dc * 128:(dc + 1) * 128],
                        start=(dc == 0), stop=(dc == NDC - 1))
                return last
            gate = [ev_zh[-2][1]] if len(ev_zh) >= 2 else [ev_wp, evfm]
            evt = T(gate + [evfm, ev_gl], t_pfx)
            eve = V([evt], lambda v, oc=oc, pc=pc, bnk=bnk: v.tensor_copy(
                zhalo[0:pc, oc * 3:oc * 3 + 3], bank(bnk, 3, pc, off=125)))
            ev_zh.append((evt, eve))
        ev_zhalo = ev_zh[-1][1] if ev_zh else None
        if SKIP_PFX:
            ev_zhalo = V([], lambda v: v.memset(zhalo[:, :], 0.0))

        slot_rd = [None] * 6
        zc_rd = [None, None]
        pwe_rd = [None, None]
        zb_free = [None, None]
        prev_use = [evn]
        st_prev = None
        gst_prev = None
        if "slp" in dev:
            G([ev3, ev_wp], lambda g: g.dma_start(
                out=d_slp.ap(), in_=slp[:, :]), ao)
        if "tw" in dev:
            G([ev3, ev_wp], lambda g: g.dma_start(
                out=d_tw.ap()[:, 0:NT], in_=twbias[:, :],
                allow_slow_non_contiguous=True), ao)
            G([ev3, ev_wp], lambda g: g.dma_start(
                out=d_tw.ap()[:, NT:NT + 1], in_=twsc[:, :],
                allow_slow_non_contiguous=True), ao)
            G([ev3, ev_wp], lambda g: g.dma_start(
                out=d_tw.ap()[:, NT + 1:NT + 2], in_=sscb[:, :],
                allow_slow_non_contiguous=True), ao)
        halo_wr = [ev_zhalo] * NZC
        b4_rd = [None]
        prev_vc = [None]
        prev_fm = evfm
        prev_p_S = None

        tb = []
        for i in (range(NT) if os.environ.get("P1CUT") not in ("setup", "pfx", "zh") else []):
            evx = SY([prev_use[0]], lambda sy, i=i: sy.dma_start(
                out=xtm[:, :].rearrange("p (b d) -> p b d", d=D),
                in_=x_in.ap()[128 + i * TT:128 + (i + 1) * TT, :].rearrange(
                    "(b p) d -> p b d", p=128)), ld)
            evn, _ = ln_norm(xtm, lntm, BPT, lnst,
                             [evx, prev_fm, prev_vc[0]], scp1)
            prev_use[0] = evn
            evfm, _ = tm_to_fm(lntm, BPT,
                               lambda b, dc: lnfm[:, dc * TT + b * 128:
                                                  dc * TT + (b + 1) * 128],
                               [evn, ev_zhalo])
            prev_fm = evfm
            if os.environ.get("P1CUT") == "fm":
                tb = []
                continue

            oc_order = [NZC - 1] + list(range(NZC - 1))
            ev_pw_bf = None
            ev_pwe = {}
            for idx, oc in enumerate(oc_order):
                pc = 32 if oc == NZC - 1 else 128
                zb = idx % 2
                slot = oc % 5

                def t_ip(t, oc=oc, pc=pc, zb=zb):
                    last = None
                    for dc in range(NDC):
                        last = MM(t, 
                            bank(zb, TT, pc),
                            winb[:, dc * C + oc * 128:dc * C + oc * 128 + pc],
                            lnfm[:, dc * TT:(dc + 1) * TT],
                            start=(dc == 0), stop=(dc == NDC - 1))
                    return last
                ev_ip = T([evfm, zb_free[zb]], t_ip)

                def v_ev(v, oc=oc, pc=pc, zb=zb, slot=slot):
                    v.tensor_copy(zsb[0:pc, slot * ZW:slot * ZW + 3],
                                  zhalo[0:pc, oc * 3:oc * 3 + 3])
                    return v.tensor_copy(
                        zsb[0:pc, slot * ZW + 3:slot * ZW + 3 + TT],
                        bank(zb, TT, pc))
                ev_ev = V([ev_ip, slot_rd[slot], halo_wr[oc]], v_ev)
                zb_free[zb] = ev_ev
                if os.environ.get("P1CUT") == "ip":
                    slot_rd[slot] = ev_ev
                    continue

                if oc == NZC - 1:
                    cb = 3
                else:
                    cb = 2

                def t_cv(t, oc=oc, pc=pc, slot=slot, cb=cb):
                    last = None
                    for tap in range(KS):
                        last = MM(t, 
                            bank(cb, TT, pc),
                            convb[0:pc, (tap * NZC + oc) * 128:
                                  (tap * NZC + oc) * 128 + pc],
                            zsb[0:pc, slot * ZW + tap:slot * ZW + tap + TT],
                            start=(tap == 0), stop=(tap == KS - 1))
                    return last
                rd_ev = zc_rd[0] if cb == 2 else b4_rd[0]
                ev_cv = T([ev_ev, rd_ev], t_cv)

                halo_wr[oc] = V([ev_cv], lambda v, oc=oc, pc=pc, slot=slot:
                                v.tensor_copy(
                    zhalo[0:pc, oc * 3:oc * 3 + 3],
                    zsb[0:pc, slot * ZW + TT:slot * ZW + TT + 3]))
                slot_rd[slot] = halo_wr[oc]

                if "z" in dev and i == 0 and oc == 0:
                    G([ev_ev], lambda g: g.dma_start(
                        out=d_z.ap()[:, 0:TT + 3], in_=zsb[:, 0:TT + 3]), ao)
                    G([evfm], lambda g: g.dma_start(
                        out=d_lnfm.ap(), in_=lnfm[:, :]), ao)
                if oc == NZC - 1:
                    def v_sc(v, cb=cb):
                        fence(v, v.tensor_scalar(pwsb[:, 0:TT],
                                                 bank(cb, TT, 32),
                                                 cbsb[0:K, NZC - 1:NZC],
                                                 sscb[:, :],
                                                 Alu.add, Alu.mult))
                        return v.tensor_scalar(pwsb[:, 0:TT], pwsb[:, 0:TT],
                                               20.0, -20.0, Alu.min, Alu.max)
                    ev_sc = V([ev_cv, prev_p_S], v_sc)
                    b4_rd[0] = ev_sc
                    ev_p = S([ev_sc], lambda s, i=i: [
                        s.activation(pwsb[:, 0:TT], pwsb[:, 0:TT], Act.Exp,
                                     scale=cst(1.0, K), bias=cst(0.0, K)),
                        s.activation(pwsb[:, TT:2 * TT], iotasb[:, :],
                                     Act.Exp, scale=twsc[:, :],
                                     bias=twbias[:, i:i + 1]),
                    ][-1])
                    prev_p_S = ev_p
                    if "pw" in dev and i == 0:
                        G([ev_p], lambda g: g.dma_start(
                            out=d_pw.ap(), in_=pwsb[:, :]), ao)
                    ev_pw_bf = V([ev_p, st_prev, ev_stz], lambda v, i=i: v.scalar_tensor_tensor(
                        stsb[0:K, ST_PW * TT:ST_PW * TT + TT],
                        pwsb[:, 0:TT], 1.0, pwsb[:, TT:2 * TT],
                        Alu.mult, Alu.mult,
                        accum_out=partials[0:K, ST_PW * NT + i:
                                           ST_PW * NT + i + 1]))
                    continue

                if oc < NXC:
                    j = oc
                    pb = 4 + (j % 2)

                    ev_pwe[j] = T([ev_pw_bf, pwe_rd[j % 2]],
                                  lambda t, j=j, pb=pb: MM(t, 
                        bank(pb, TT),
                        eselb[:, j * 128:(j + 1) * 128],
                        stsb[0:K, ST_PW * TT:ST_PW * TT + TT],
                        start=True, stop=True))

                    ev_u = V([ev_cv, prev_vc[0]], lambda v, j=j, cb=cb:
                             v.tensor_scalar_add(usb[:, :], bank(cb, TT),
                                                 cbsb[:, j:j + 1]))

                    def s_chain(s, j=j, cb=cb):
                        xv = usb[:, :]
                        s.activation(scp1[:, 0:TT], xv, Act.Identity,
                                     scale=theta2p[:, j:j + 1],
                                     bias=cst(MAGIC))
                        s.activation(scp1[:, 0:TT], scp1[:, 0:TT], Act.Identity,
                                     scale=cst(1.0), bias=cst(-MAGIC))
                        return s.activation(scp1[:, TT:2 * TT] if False else
                                            scp1[:, 0:TT], scp1[:, 0:TT],
                                            Act.Identity, scale=cst(TWO_PI),
                                            bias=cst(0.0))
                    ev_s1 = S([ev_u], s_chain)

                    def v_r(v, j=j, cb=cb):
                        xv = usb[:, :]
                        v.scalar_tensor_tensor(chainr[:, 0:TT], xv,
                                               thetasb[:, j:j + 1],
                                               scp1[:, 0:TT],
                                               Alu.mult, Alu.subtract)
                        return v.scalar_tensor_tensor(
                            chainr[:, TT:2 * TT], chainr[:, 0:TT], -1.0,
                            chainr[:, 0:TT], Alu.mult, Alu.max)
                    ev_r = V([ev_s1], v_r)

                    def s_tr(s, j=j, cb=cb):
                        xv = usb[:, :]
                        s.activation(sincos[:, 0:TT], chainr[:, 0:TT],
                                     Act.Sin, scale=cst(1.0), bias=cst(0.0))
                        s.activation(sincos[:, TT:2 * TT],
                                     chainr[:, TT:2 * TT], Act.Sin,
                                     scale=cst(-1.0), bias=cst(math.pi / 2))
                        return s.activation(scp1[:, 0:TT], xv, Act.Identity,
                                            scale=wpoly[:, 2:3],
                                            bias=wpoly[:, 1:2])
                    ev_s2 = S([ev_r], s_tr)

                    def v_chain(v, j=j, cb=cb, i=i):
                        xv = usb[:, :]
                        v.tensor_tensor(chainr[:, 0:TT], scp1[:, 0:TT], xv,
                                        Alu.mult)
                        v.tensor_scalar_add(polyb[:, 0:TT], chainr[:, 0:TT],
                                            wpoly[:, 0:1])
                        v.tensor_tensor(remb[:, 0:TT], polyb[:, 0:TT],
                                        sincos[:, TT:2 * TT], Alu.mult)
                        v.tensor_tensor(remb[:, TT:2 * TT], polyb[:, 0:TT],
                                        sincos[:, 0:TT], Alu.mult)
                        v.scalar_tensor_tensor(
                            stsb[:, st_re(j) * TT:(st_re(j) + 1) * TT],
                            remb[:, 0:TT], 1.0, bank(4 + (j % 2), TT),
                            Alu.mult, Alu.mult,
                            accum_out=partials[:, st_re(j) * NT + i:
                                               st_re(j) * NT + i + 1])
                        return v.scalar_tensor_tensor(
                            stsb[:, st_im(j) * TT:(st_im(j) + 1) * TT],
                            remb[:, TT:2 * TT], 1.0, bank(4 + (j % 2), TT),
                            Alu.mult, Alu.mult,
                            accum_out=partials[:, st_im(j) * NT + i:
                                               st_im(j) * NT + i + 1])
                    ev_vc = V([ev_s2, ev_pwe[j]], v_chain)
                    zc_rd[0] = ev_u
                    pwe_rd[j % 2] = ev_vc
                    prev_vc[0] = ev_vc
                else:
                    jg = oc - NXC
                    ev_g = S([ev_cv, gst_prev], lambda s, jg=jg, cb=cb, oc=oc:
                             s.activation(
                        gtsb[:, (jg % 8) * TT:(jg % 8 + 1) * TT],
                        bank(cb, TT), Act.Silu, scale=cst(1.0),
                        bias=cbsb[:, oc:oc + 1]))
                    zc_rd[0] = ev_g
                    if jg % 8 == 7:
                        if "gst" in dev:
                            G([ev_g], lambda g, i=i, jg=jg: g.dma_start(
                                out=d_gst.ap()[i, :, (jg - 7) * TT:
                                               (jg + 1) * TT],
                                in_=gtsb[:, :]), ao)
                        gst_prev = SY([ev_g], lambda sy, i=i, jg=jg:
                                      sy.dma_start(
                            out=gstash.ap()[i, :, (jg - 7) * TT:
                                            (jg + 1) * TT],
                            in_=gtsb[:, :]), so)

            last_v = ve.ev()
            last_s = se.ev()
            ev_st = SY([last_v, st_prev], lambda sy, i=i: sy.dma_start(
                out=stash.ap()[i, :, :], in_=stsb[:, :]), so)
            st_prev = ev_st
            tb = []
            if "stash" in dev:
                st_prev = G([last_v, st_prev], lambda g, i=i: g.dma_start(
                    out=d_stash.ap()[i, :, :], in_=stsb[:, :]), ao)

        if "part" in dev:
            G([ve.ev()], lambda g: g.dma_start(
                out=d_part.ap(), in_=partials[:, :]), ao)


    def barrier():
        evs = [te.ev(), ve.ev(), se.ev(), ge.ev(), (S_ld, ld.n),
               (S_so, so.n), (S_wl, wl.n), (S_ao, ao.n)]
        outs = []
        outs.append(T(evs, lambda t: t.nop()))
        outs.append(V(evs, lambda v: v.engine_nop()))
        outs.append(S(evs, lambda s: s.nop()))
        outs.append(G(evs, lambda g: g.nop()))
        outs.append(SY(evs, lambda sy: sy.nop(), None))
        return [o for o in outs if o is not None]

    if (1 in phases) and (2 in phases):
        ev_tot0 = V([ve.ev(), se.ev()], lambda v: v.tensor_reduce(
            ccsb[:, :],
            partials[:, :].rearrange("p (s t) -> p s t", t=NT),
            AX, Alu.add))
        ev_tot = V([ev_tot0], lambda v: v.tensor_scalar_mul(
            ccsb[:, :], ccsb[:, :], flagsb[:, :]))
        ev_ci = G([ev_tot], lambda g: g.dma_start(out=cc_i.ap(),
                                                  in_=ccsb[:, :]), ao)
        ev_cc = emit("gpsimd", ccc, [ev_ci], lambda g: g.collective_compute(
            "AllReduce", Alu.add,
            replica_groups=[[2 * p, 2 * p + 1] for p in range(4)],
            ins=[cc_i.ap()], outs=[cc_o.ap()]))
        ev_co = G([ev_cc], lambda g: g.dma_start(out=ccsb[:, :],
                                                 in_=cc_o.ap()), wl)
        def v_cin(v):
            fence(v, v.tensor_scalar_mul(smll[:, 13:14], flagsb[:, :], -1.0))
            fence(v, v.tensor_scalar_add(smll[:, 13:14], smll[:, 13:14],
                                         1.0))
            return v.tensor_scalar_mul(carry[:, :], ccsb[:, :],
                                       smll[:, 13:14])
        ev_cin = V([ev_co], v_cin)
        bar1 = barrier()
        if "hbm" in dev:
            G(bar1, lambda g: g.dma_start(out=d_stash.ap(),
                                          in_=stash.ap()), ao)
            G(bar1, lambda g: g.dma_start(out=d_gst.ap(),
                                          in_=gstash.ap()), ao)
            G(bar1, lambda g: g.dma_start(out=d_part.ap(),
                                          in_=partials[:, :]), ao)
            G(bar1, lambda g: g.dma_start(out=d_carry.ap()[:, 0:NZC],
                                          in_=carry[:, :]), ao)
            G(bar1, lambda g: g.dma_start(out=d_carry.ap()[:, NZC:2 * NZC],
                                          in_=ccsb[:, :]), ao)

    for c in reversed(p1_cm):
        c.__exit__(None, None, None)

    if 2 in phases:
        p2_cm = [
            nc.sbuf_tensor("woutb", [128, 16 * D], BF16),
            nc.sbuf_tensor("wcb", [128, NDC * ES], BF16),
            nc.sbuf_tensor("bdre", [128, 128], BF16),
            nc.sbuf_tensor("bdim", [128, 128], BF16),
            nc.sbuf_tensor("wresb", [128, 64], F32),
            nc.sbuf_tensor("wimsb", [128, 64], F32),
            nc.sbuf_tensor("nscsb", [128, 1], F32),
            nc.sbuf_tensor("qselb", [128, NXC * 32], BF16),
            nc.sbuf_tensor("bcsb2", [ES, 1], F32),
            nc.sbuf_tensor("tiesb", [128, GRP], F32),
            nc.sbuf_tensor("stsb2", [128, NZC * TT], BF16),
            nc.sbuf_tensor("gtsb2", [128, NXC * TT], BF16),
            nc.sbuf_tensor("numrot", [128, 4 * TT], BF16),
            nc.sbuf_tensor("sqrot", [128, 2 * TT], BF16),
            nc.sbuf_tensor("densb", [32, TT], F32),
            nc.sbuf_tensor("invd", [32, TT], F32),
            nc.sbuf_tensor("snormb", [32, TT], BF16),
            nc.sbuf_tensor("ypsb", [128, NXC * TT], BF16),
            nc.sbuf_tensor("y1b", [128, 2 * TT], BF16),
            nc.sbuf_tensor("ygsb", [128, NXC * TT], BF16),
            nc.sbuf_tensor("xtm2", [128, BPT * D], F32),
            nc.sbuf_tensor("x1sb", [128, BPT * D], F32),
            nc.sbuf_tensor("ln2tm", [128, BPT * D], BF16),
            nc.sbuf_tensor("lnfm2", [128, NDC * TT], BF16),
            nc.sbuf_tensor("lgsb", [128, TT], F32),
            nc.sbuf_tensor("gmx", [128, GPT], F32),
            nc.sbuf_tensor("ohsb", [128, SEG_T], BF16),
            nc.sbuf_tensor("lnst2", [128, 16], F32),
            nc.sbuf_tensor("scp2", [128, max(D, 2 * TT)], F32),
            nc.sbuf_tensor("t2sb", [32, 2 * TT], F32),
        ]
        (woutb, wcb, bdre, bdim, wresb, wimsb, nscsb, qselb, bcsb2, tiesb,
         stsb2, gtsb2, numrot, sqrot, densb, invd, snormb, ypsb, y1b, ygsb,
         xtm2, x1sb, ln2tm, lnfm2, lgsb, gmx, ohsb, lnst2, scp2,
         t2sb) = [c.__enter__() for c in p2_cm]

        def g2_loads(g, inc):
            n = 0
            for i16 in range(16):
                inc(g.dma_start(out=woutb[:, i16 * D:(i16 + 1) * D],
                                in_=wout_in.ap()[i16, :, :]))
                n += 1
            for dc in range(NDC):
                inc(g.dma_start(out=wcb[:, dc * ES:(dc + 1) * ES],
                                in_=wc_in.ap()[dc, :, :]))
                n += 1
            for dst, srcp in [
                (wresb[:, :], wre_in.ap()), (wimsb[:, :], wim_in.ap()),
                (nscsb[:, :], nsc_in.ap()), (qselb[:, :], qsel_in.ap()),
                (bcsb2[:, :], bc_in.ap()), (tiesb[:, :], tie_in.ap()),
            ]:
                inc(g.dma_start(out=dst, in_=srcp))
                n += 1
            return n
        ev_g2 = emit_multi("gpsimd", wl, bar1, g2_loads, 16 + NDC + 6)

        ev_bdz = V(bar1, lambda v: [v.memset(bdre[:, :], 0.0),
                                    v.memset(bdim[:, :], 0.0)][-1])
        ev_bd = V([ev_g2, ev_bdz], lambda v: [
            v.tensor_scalar_mul(bdre[0:64, 0:64], wresb[0:64, :],
                                nscsb[0:64, :]),
            v.tensor_scalar_mul(bdre[64:128, 64:128], wresb[64:128, :],
                                nscsb[0:64, :]),
            v.tensor_scalar_mul(bdim[0:64, 0:64], wimsb[0:64, :],
                                nscsb[64:128, :]),
            v.tensor_scalar_mul(bdim[64:128, 64:128], wimsb[64:128, :],
                                nscsb[64:128, :]),
        ][-1])

        ev_st_rd = None
        ev_gt_rd = None
        slot_free = [None] * 4
        sq_free = [None, None]
        yp_free = [None, None]
        se_free = [None, None]
        pout_free = [None, None]
        ev_oh_dma = None
        ev_q_rd = None
        tb2 = []
        x1_free = []
        ln2_free = []
        prev_rs = None
        snorm_rd = None
        op_rd = None

        for i in range(NT):
            seg = i // SEG_TILES
            tis = i % SEG_TILES

            ev_sld = SY([ev_st_rd] + bar1, lambda sy, i=i: sy.dma_start(
                out=stsb2[:, :], in_=stash.ap()[i, :, :]), ld)
            ev_gld = SY([ev_gt_rd], lambda sy, i=i: sy.dma_start(
                out=gtsb2[:, :], in_=gstash.ap()[i, :, :]), ld)
            ev_xld = SY([pout_free[0], pout_free[1]],
                        lambda sy, i=i: sy.dma_start(
                out=xtm2[:, :].rearrange("p (b d) -> p b d", d=D),
                in_=x_in.ap()[128 + i * TT:128 + (i + 1) * TT, :].rearrange(
                    "(b p) d -> p b d", p=128)), ld)

            if i > 0:
                ev_cu = V([], lambda v, i=i: v.tensor_add(
                    carry[:, :], carry[:, :],
                    partials[:, :].rearrange("p (s t) -> p t s",
                                             t=NT)[:, i - 1, :]))
            else:
                ev_cu = ev_cin if (1 in phases) else None

            last_scan = {}
            ev_ypre = {}
            for j in range(NXC):
                sre_slot = (2 * j) % 4
                sim_slot = (2 * j + 1) % 4

                sfre = slot_free[sre_slot] or []
                sfim = slot_free[sim_slot] or []
                ev_s_re = V([ev_sld, ev_cu] + sfre,
                            lambda v, j=j, s=sre_slot: v.tensor_tensor_scan(
                    numrot[:, s * TT:(s + 1) * TT],
                    stsb2[:, st_re(j) * TT:(st_re(j) + 1) * TT],
                    stsb2[:, st_re(j) * TT:(st_re(j) + 1) * TT],
                    carry[:, st_re(j):st_re(j) + 1], Alu.add, Alu.bypass))
                ev_s_im = V(sfim,
                            lambda v, j=j, s=sim_slot: v.tensor_tensor_scan(
                    numrot[:, s * TT:(s + 1) * TT],
                    stsb2[:, st_im(j) * TT:(st_im(j) + 1) * TT],
                    stsb2[:, st_im(j) * TT:(st_im(j) + 1) * TT],
                    carry[:, st_im(j):st_im(j) + 1], Alu.add, Alu.bypass))

                ev_q_re = S([ev_s_re, sq_free[0]], lambda s, j=j,
                            ss=sre_slot: s.activation(
                    sqrot[:, 0:TT], numrot[:, ss * TT:(ss + 1) * TT],
                    Act.Square, scale=cst(1.0), bias=cst(0.0)))
                ev_q_im = S([ev_s_im, sq_free[1]], lambda s, j=j,
                            ss=sim_slot: s.activation(
                    sqrot[:, TT:2 * TT], numrot[:, ss * TT:(ss + 1) * TT],
                    Act.Square, scale=cst(1.0), bias=cst(0.0)))

                ev_qm_re = T([ev_q_re, ev_q_rd], lambda t, j=j: MM(t, 
                    bank(0, TT, 32), qselb[:, j * 32:(j + 1) * 32],
                    sqrot[:, 0:TT], start=(j == 0), stop=False))
                ev_qm_im = T([ev_q_im], lambda t, j=j: MM(t, 
                    bank(0, TT, 32), qselb[:, j * 32:(j + 1) * 32],
                    sqrot[:, TT:2 * TT], start=False, stop=(j == NXC - 1)))
                sq_free[0] = ev_qm_re
                sq_free[1] = ev_qm_im

                ypb = 1 + (j % 2)
                ev_yp = T([ev_s_re, ev_s_im, ev_bd, yp_free[j % 2]],
                          lambda t, j=j, ypb=ypb, sr=sre_slot,
                          si=sim_slot: [
                    MM(t, bank(ypb, TT), bdre[:, :],
                             numrot[:, sr * TT:(sr + 1) * TT],
                             start=True, stop=False),
                    MM(t, bank(ypb, TT), bdim[:, :],
                             numrot[:, si * TT:(si + 1) * TT],
                             start=False, stop=True),
                ][-1])
                ev_ype = V([ev_yp], lambda v, j=j, ypb=ypb: v.tensor_copy(
                    ypsb[:, j * TT:(j + 1) * TT], bank(ypb, TT)))
                yp_free[j % 2] = ev_ype
                slot_free[sre_slot] = [ev_ype, ev_q_re]
                slot_free[sim_slot] = [ev_ype, ev_q_im]
                ev_ypre[j] = ev_ype

            ev_den = V([ev_sld, ev_cu], lambda v: v.tensor_tensor_scan(
                densb[:, :], stsb2[0:K, ST_PW * TT:ST_PW * TT + TT],
                stsb2[0:K, ST_PW * TT:ST_PW * TT + TT],
                carry[0:K, ST_PW:ST_PW + 1], Alu.add, Alu.bypass))
            ev_st_rd = ev_den
            def v_inv(v):
                fence(v, v.tensor_scalar_max(invd[:, :], densb[:, :], 1e-4))
                return v.reciprocal(invd[:, :], invd[:, :])
            ev_inv = V([ev_den], v_inv)
            def v_t2(v):
                fence(v, v.tensor_tensor(t2sb[:, 0:TT], bank(0, TT, 32),
                                         invd[:, :], Alu.mult))
                return v.tensor_tensor(t2sb[:, 0:TT], t2sb[:, 0:TT],
                                       invd[:, :], Alu.mult)
            ev_t2 = V([ev_inv, sq_free[1], prev_rs], v_t2)
            ev_q_rd = ev_t2
            ev_rs = S([ev_t2], lambda s: [
                s.activation(t2sb[:, TT:2 * TT], t2sb[:, 0:TT], Act.Ln,
                             scale=cst(1.0 / (2.0 * Hd), 32),
                             bias=cst(1e-5, 32)),
                s.activation(t2sb[:, TT:2 * TT], t2sb[:, TT:2 * TT],
                             Act.Exp, scale=cst(-0.5, 32), bias=cst(0.0, 32)),
            ][-1])
            prev_rs = ev_rs
            ev_sn = V([ev_rs, snorm_rd], lambda v: v.tensor_tensor(
                snormb[:, :], t2sb[:, TT:2 * TT], invd[:, :], Alu.mult))

            ev_yg = {}
            for j in range(NXC):
                seb = 3
                ev_se = T([ev_sn, se_free[0]], lambda t, j=j, seb=seb:
                          MM(t, bank(seb, TT), eselb[:, j * 128:
                                                        (j + 1) * 128],
                                   snormb[:, :], start=True, stop=True))
                ev_y1 = V([ev_se, ev_ypre[j]], lambda v, j=j, seb=seb:
                          v.tensor_tensor(y1b[:, (j % 2) * TT:
                                              (j % 2 + 1) * TT],
                                          ypsb[:, j * TT:(j + 1) * TT],
                                          bank(seb, TT), Alu.mult))
                ev_yg[j] = V([ev_y1, ev_gld, op_rd if j == 0 else None],
                             lambda v, j=j: v.tensor_tensor(
                    ygsb[:, j * TT:(j + 1) * TT],
                    y1b[:, (j % 2) * TT:(j % 2 + 1) * TT],
                    gtsb2[:, j * TT:(j + 1) * TT], Alu.mult))
                se_free[0] = ev_y1
                snorm_rd = ev_se
            ev_gt_rd = ev_yg[NXC - 1]

            ev_x1 = {}
            for b in range(BPT):
                evs_po = [ev_yg[j] for j in range(NXC)]
                def t_op(t, b=b):
                    last = None
                    for ch in range(16):
                        for hf in range(2):
                            last = MM(t, 
                                bank(4 + hf, 512),
                                ygsb[:, ch * TT + b * 128:
                                     ch * TT + (b + 1) * 128],
                                woutb[:, ch * D + hf * 512:
                                      ch * D + hf * 512 + 512],
                                start=(ch == 0), stop=(ch == 15))
                    return last
                ev_po = T(evs_po + [pout_free[0], pout_free[1], ev_g2],
                          t_op)
                op_rd = ev_po
                ev_r = V([ev_po, ev_xld] + x1_free, lambda v, b=b: [
                    v.tensor_tensor(x1sb[:, b * D:b * D + 512],
                                    bank(4, 512),
                                    xtm2[:, b * D:b * D + 512], Alu.add),
                    v.tensor_tensor(x1sb[:, b * D + 512:(b + 1) * D],
                                    bank(5, 512),
                                    xtm2[:, b * D + 512:(b + 1) * D],
                                    Alu.add),
                ][-1])
                pout_free[0] = ev_r
                pout_free[1] = ev_r
                ev_x1[b] = ev_r
            ev_x1_all = [ev_x1[b] for b in range(BPT)]
            ev_xst = SY(ev_x1_all, lambda sy, i=i: sy.dma_start(
                out=out.ap()[i * TT:(i + 1) * TT, :].rearrange(
                    "(b p) d -> p b d", p=128),
                in_=x1sb[:, :].rearrange("p (b d) -> p b d", d=D)), so)
            if "x1" in dev:
                G(ev_x1_all, lambda g, i=i: g.dma_start(
                    out=d_x1.ap()[i * TT:(i + 1) * TT, :].rearrange(
                        "(b p) d -> p b d", p=128),
                    in_=x1sb[:, :].rearrange("p (b d) -> p b d", d=D)), ao)

            ev_ln2, _ = ln_norm(x1sb, ln2tm, BPT, lnst2,
                                ev_x1_all + ln2_free, scp2)
            ev_l2st = SY([ev_ln2], lambda sy, i=i: sy.dma_start(
                out=ln2st.ap()[i, :, :], in_=ln2tm[:, :]), so)
            ev_fm2, _ = tm_to_fm(ln2tm, BPT,
                                 lambda b, dc: lnfm2[:, dc * TT + b * 128:
                                                     dc * TT + (b + 1) * 128],
                                 [ev_ln2] + ln2_free)

            def t_lg(t):
                last = None
                for dc in range(NDC):
                    last = MM(t, bank(0, TT), wcb[:, dc * ES:
                                                     (dc + 1) * ES],
                                    lnfm2[:, dc * TT:(dc + 1) * TT],
                                    start=(dc == 0), stop=(dc == NDC - 1))
                return last
            ev_lg = T([ev_fm2, ev_q_rd, ev_g2], t_lg)
            ev_lge = V([ev_lg], lambda v: v.scalar_tensor_tensor(
                lgsb[:, :].rearrange("p (g t) -> p g t", t=GRP),
                bank(0, TT).rearrange("p (g t) -> p g t", t=GRP),
                bcsb2[:, :],
                tiesb[:, :].unsqueeze(1).broadcast_to([128, GPT, GRP]),
                Alu.add, Alu.add))
            ev_q_rd = ev_lge
            ev_mx = V([ev_lge], lambda v: v.tensor_reduce(
                gmx[:, :], lgsb[:, :].rearrange("p (g t) -> p g t", t=GRP),
                AX, Alu.max))
            ev_oh = V([ev_mx, ev_oh_dma], lambda v, tis=tis: v.tensor_tensor(
                ohsb[:, tis * TT:(tis + 1) * TT].rearrange(
                    "p (g t) -> p g t", t=GRP),
                lgsb[:, :].rearrange("p (g t) -> p g t", t=GRP),
                gmx[:, :].unsqueeze(2).broadcast_to([128, GPT, GRP]),
                Alu.is_equal))

            if tis == SEG_TILES - 1:
                ev_oh_dma = SY([ev_oh], lambda sy, seg=seg: sy.dma_start(
                    out=ohst.ap()[seg, :, :], in_=ohsb[:, :]), so)
                if "oh" in dev:
                    G([ev_oh], lambda g, seg=seg: g.dma_start(
                        out=d_oh.ap()[seg, :, :], in_=ohsb[:, :]), ao)
            x1_free = [ se.ev(), ev_xst ]
            ln2_free = [ev_l2st, ev_lg]

        for c in reversed(p2_cm):
            c.__exit__(None, None, None)


    if 3 in phases:
        bar2 = barrier()
        p3_cm = [
            nc.sbuf_tensor("w2b", [128, NPACK * D], BF16),
            nc.sbuf_tensor("w1buf", [128, 2 * ES * EF], BF16),
            nc.sbuf_tensor("ln2seg", [128, SEG_TILES * BPT * D], BF16),
            nc.sbuf_tensor("permseg", [128, BLK_SEG * 512], BF16),
            nc.sbuf_tensor("ohsegb", [128, SEG_T], BF16),
            nc.sbuf_tensor("ohT", [128, 128], BF16),
            nc.sbuf_tensor("xdc", [128, BLK_SEG * 512], BF16),
            nc.sbuf_tensor("innersb", [128, NPACK * GSEG], BF16),
            nc.sbuf_tensor("padsb", [128, 2 * MSL], BF16),
            nc.sbuf_tensor("outsb", [128, 2 * D], BF16),
            nc.sbuf_tensor("outg", [128, 2 * D], BF16),
            nc.sbuf_tensor("b1pb", [128, NPACK], F32),
            nc.sbuf_tensor("b2pb", [128, D], BF16),
            nc.sbuf_tensor("bmaskb", [128, 4], BF16),
            nc.sbuf_tensor("x2c", [128, 2 * D], F32),
        ]
        (w2b, w1buf, ln2seg, permseg, ohsegb, ohT, xdc, innersb, padsb,
         outsb, outg, b1pb, b2pb, bmaskb, x2c) = \
            [c.__enter__() for c in p3_cm]

        def g3_loads(g, inc):
            n = 0
            for pk in range(NPACK):
                inc(g.dma_start(out=w2b[:, pk * D:(pk + 1) * D],
                                in_=w2p_in.ap()[pk, :, :]))
                n += 1
            for dst, srcp in [(b1pb[:, :], b1p_in.ap()),
                              (b2pb[:, :], b2p_in.ap()),
                              (bmaskb[:, :], bmask_in.ap())]:
                inc(g.dma_start(out=dst, in_=srcp))
                n += 1
            return n
        ev_g3 = emit_multi("gpsimd", wl, bar2, g3_loads, NPACK + 3)

        ev_perm_rd = None
        ev_oh_rd = None
        ev_w1_rd = [None, None]
        ev_xdc_rd = None
        ev_inner_rd = None
        ev_outsb_rd = None
        ev_outg_rd = [None, None]
        ev_x2c_rd = [None, None]
        ev_px2_rd = [None, None]
        ev_pb_last = None

        for seg in range(NSEG):
            ev_ohl = SY([ev_oh_rd] + bar2, lambda sy, seg=seg: sy.dma_start(
                out=ohsegb[:, :], in_=ohst.ap()[seg, :, :]), ld)
            def l2l(sy, inc, seg=seg):
                for tis in range(SEG_TILES):
                    inc(sy.dma_start(
                        out=ln2seg[:, tis * BPT * D:(tis + 1) * BPT * D],
                        in_=ln2st.ap()[seg * SEG_TILES + tis, :, :]))
                return SEG_TILES
            ev_l2l = emit_multi("sync", ld, [ev_perm_rd] + bar2, l2l,
                                SEG_TILES)

            ev_pm = []
            for blk in range(BLK_SEG):
                evt = T([ev_ohl,
                         ev_pm[-2][0] if len(ev_pm) >= 2 else None],
                        lambda t, blk=blk: t.transpose(
                    pbf_slot(blk % 4), ohsegb[:, blk * 128:(blk + 1) * 128],
                    identb[:, :]))
                eve = V([evt, ev_perm_rd], lambda v, blk=blk: v.tensor_copy(
                    ohT[:, :], pbf_slot(blk % 4)))
                evz = V([eve], lambda v, blk=blk: v.memset(
                    permseg[:, blk * 512:(blk + 1) * 512], 0.0))
                def cpy(v, blk=blk):
                    last = None
                    for g_ in range(GPB):
                        last = v.tensor_copy(
                            permseg[32 * g_:32 * (g_ + 1),
                                    blk * 512 + g_ * 128:
                                    blk * 512 + (g_ + 1) * 128],
                            ohT[32 * g_:32 * (g_ + 1), :])
                    return last
                evc = V([evz], cpy)
                ev_pm.append((evc, evt))
            ev_perm = ev_pm[-1][0]

            if "perm" in dev and seg == 0:
                G([ev_perm, ev_l2l], lambda g: g.dma_start(
                    out=d_perm.ap(), in_=permseg[:, :]), ao)
                G([ev_perm, ev_l2l], lambda g: g.dma_start(
                    out=d_l2s.ap(), in_=ln2seg[:, :]), ao)
            ev_w1mm_last = None
            for dc in range(NDC):
                buf = dc % 2
                ev_wl1 = G([ev_g3, ev_w1_rd[buf]], lambda g, dc=dc, buf=buf:
                           g.dma_start(out=w1buf[:, buf * ES * EF:
                                                 (buf + 1) * ES * EF],
                                       in_=w1p_in.ap()[dc, :, :]), wl)
                ev_ga = {}
                for blk in range(BLK_SEG):
                    tis, b = blk // BPT, blk % BPT
                    evg = T([ev_perm, ev_l2l,
                             ev_ga[blk - 1][0] if blk - 1 in ev_ga else
                             ev_xdc_rd,
                             None if (blk or dc) else ev_px2_rd[0],
                             None if (blk or dc) else ev_x2c_rd[0],
                             None if (blk or dc) else ev_x2c_rd[1]],
                            lambda t, tis=tis, b=b, dc=dc, blk=blk: MM(t, 
                        bank(5, 512),
                        ln2seg[:, (tis * BPT + b) * D + dc * 128:
                               (tis * BPT + b) * D + (dc + 1) * 128],
                        permseg[:, blk * 512:(blk + 1) * 512],
                        start=True, stop=True))
                    eve = V([evg], lambda v, blk=blk: v.tensor_copy(
                        xdc[:, blk * 512:(blk + 1) * 512], bank(5, 512)))
                    ev_ga[blk] = (eve, evg)
                ev_xall = ev_ga[BLK_SEG - 1][0]

                if "xdc" in dev and seg == 0 and dc == 0:
                    G([ev_xall], lambda g: g.dma_start(
                        out=d_xdc.ap(), in_=xdc[:, :]), ao)
                for es in range(ES):
                    esl = es % 4
                    pk = es // 4
                    gates = [ev_xall, ev_wl1]
                    if dc == 0 and es == 0:
                        gates += [ev_inner_rd, ev_px2_rd[0]]
                    ev_w1mm_last = T(
                        gates,
                        lambda t, es=es, esl=esl, pk=pk, dc=dc, buf=buf:
                        MM(t, 
                            pall[32 * esl:32 * (esl + 1),
                                 pk * GSEG:(pk + 1) * GSEG],
                            w1buf[:, buf * ES * EF + es * EF:
                                  buf * ES * EF + (es + 1) * EF],
                            xdc[:, :].rearrange(
                                "p (blk g es) -> p es blk g",
                                g=GPB, es=128)[:, es, :, :],
                            start=(dc == 0 and
                                   (pk * GSEG * 4) % 2048 == 0),
                            stop=(dc == NDC - 1),
                            tile_position=(0, 32 * esl)))
                ev_w1_rd[buf] = ev_w1mm_last
                ev_xdc_rd = ev_w1mm_last

            def s_ie(s):
                last = None
                for pk in range(NPACK):
                    last = s.activation(
                        innersb[:, pk * GSEG:(pk + 1) * GSEG],
                        pall[:, pk * GSEG:(pk + 1) * GSEG], Act.Relu,
                        scale=cst(1.0), bias=b1pb[:, pk:pk + 1])
                return last
            ev_ie = S([ev_w1mm_last, ev_g3, ev_pb_last], s_ie)

            if "inner" in dev and seg == 0:
                G([ev_ie], lambda g: g.dma_start(
                    out=d_inner.ap(), in_=innersb[:, :]), ao)
            ev_obst = {}
            for sub in range(NSUB):
                ev_w2 = {}
                for pk in range(NPACK):
                    pr = pk % 2
                    bnk0 = 2 * pr
                    ev_pb = V([ev_ie, ev_w2.get(pk - 2, (None,) * 3)[0]],
                              lambda v, pk=pk, sub=sub, pr=pr:
                              v.tensor_tensor(
                        padsb[:, pr * MSL:(pr + 1) * MSL].rearrange(
                            "p (e g) -> p e g", e=4),
                        innersb[:, pk * GSEG + sub * SUB:
                                pk * GSEG + (sub + 1) * SUB].unsqueeze(
                            1).broadcast_to([128, 4, SUB]),
                        bmaskb[:, :].unsqueeze(2).broadcast_to(
                            [128, 4, SUB]),
                        Alu.mult))
                    ev_pb_last = ev_pb

                    def t_w2(t, pk=pk, pr=pr, bnk0=bnk0):
                        last = None
                        for hf in range(2):
                            last = MM(t, 
                                pall[0:MSL, (bnk0 + hf) * 512:
                                     (bnk0 + hf) * 512 + 512],
                                padsb[:, pr * MSL:(pr + 1) * MSL],
                                w2b[:, pk * D + hf * 512:
                                    pk * D + hf * 512 + 512],
                                start=True, stop=True)
                        return last
                    evm = T([ev_pb, ev_g3,
                             ev_w2.get(pk - 2, (None,) * 3)[0],
                             ev_outsb_rd if pk < 2 else None,
                             ev_ie if pk < 2 else None], t_w2)

                    prev_st = ev_obst.get((sub, pk - 2)) or \
                        ev_obst.get((sub - 1, pk + NPACK - 2))
                    if pr == 0:
                        def v_ev(v, pk=pk, bnk0=bnk0, pr=pr):
                            v.tensor_copy(
                                outsb[0:MSL, pr * D:pr * D + 512],
                                pall[0:MSL, bnk0 * 512:bnk0 * 512 + 512])
                            return v.tensor_copy(
                                outsb[0:MSL, pr * D + 512:(pr + 1) * D],
                                pall[0:MSL, (bnk0 + 1) * 512:
                                     (bnk0 + 1) * 512 + 512])
                        eve = V([evm, prev_st], v_ev)
                    else:
                        def s_ev(s, pk=pk, bnk0=bnk0, pr=pr):
                            s.activation(
                                outsb[0:MSL, pr * D:pr * D + 512],
                                pall[0:MSL, bnk0 * 512:bnk0 * 512 + 512],
                                Act.Identity, scale=cst(1.0, MSL),
                                bias=cst(0.0, MSL))
                            return s.activation(
                                outsb[0:MSL, pr * D + 512:(pr + 1) * D],
                                pall[0:MSL, (bnk0 + 1) * 512:
                                     (bnk0 + 1) * 512 + 512],
                                Act.Identity, scale=cst(1.0, MSL),
                                bias=cst(0.0, MSL))
                        eve = S([evm, prev_st], s_ev)
                    ev_st_ob = emit("sync", sq, [eve],
                                    lambda sy, pk=pk, pr=pr: sy.dma_start(
                        out=ob.ap()[pk // 2, pk % 2].rearrange(
                            "e g d -> (e g) d"),
                        in_=outsb[0:MSL, pr * D:(pr + 1) * D]))
                    ev_obst[(sub, pk)] = ev_st_ob
                    ev_w2[pk] = (eve, evm, ev_st_ob)
                ev_ob_all = [ev_obst[(sub, NPACK - 2)],
                             ev_obst[(sub, NPACK - 1)]]

                ev_og_last = None
                for bls in range(SUB // GPB):
                    blk = sub * (SUB // GPB) + bls

                    def t_b2(t, blk=blk):
                        last = None
                        for hf in range(2):
                            last = MM(t,
                                pall[:, (4 + hf) * 512:
                                     (4 + hf) * 512 + 512],
                                ohsegb[:, blk * 128:(blk + 1) * 128],
                                b2pb[:, hf * 512:(hf + 1) * 512],
                                start=True, stop=False)
                        return last
                    evb2 = T([ev_g3, ev_px2_rd[0], ev_ie], t_b2)

                    gev = []
                    for gl in range(GPB):
                        g_loc = bls * GPB + gl
                        g_glob = sub * SUB + g_loc
                        slot = gl % 2

                        ev_og = SY(ev_ob_all + [ev_outg_rd[slot]],
                                   lambda sy, g_loc=g_loc, slot=slot:
                                   sy.dma_start(
                            out=outg[:, slot * D:(slot + 1) * D],
                            in_=ob.ap()[:, :, :, g_loc, :].rearrange(
                                "q p e d -> (q p e) d")), ld)
                        ev_og_last = ev_og
                        if "outg" in dev and seg == 0 and sub == 0 and g_glob == 0:
                            G([ev_og], lambda g: g.dma_start(
                                out=d_outg.ap(),
                                in_=outg[:, 0 * D:1 * D]), ao)

                        def t_sc(t, g_glob=g_glob, gl=gl, slot=slot):
                            last = None
                            for hf in range(2):
                                last = MM(t,
                                    pall[32 * gl:32 * (gl + 1),
                                         (4 + hf) * 512:
                                         (4 + hf) * 512 + 512],
                                    ohsegb[:, g_glob * 32:
                                           (g_glob + 1) * 32],
                                    outg[:, slot * D + hf * 512:
                                         slot * D + hf * 512 + 512],
                                    start=False, stop=(gl == GPB - 1),
                                    tile_position=(0, 32 * gl))
                            return last
                        evsc = T([ev_og, evb2], t_sc)
                        ev_outg_rd[slot] = evsc
                        gev.append(evsc)

                    evx2 = V(gev + [ev_x2c_rd[bls % 2]],
                             lambda v, bls=bls: [
                        v.tensor_copy(x2c[:, (bls % 2) * D:
                                          (bls % 2) * D + 512],
                                      bank(4, 512)),
                        v.tensor_copy(x2c[:, (bls % 2) * D + 512:
                                          (bls % 2 + 1) * D],
                                      bank(5, 512)),
                    ][-1])
                    ev_px2_rd[0] = evx2
                    row0 = seg * SEG_T + blk * 128
                    ev_ac = G([evx2], lambda g, row0=row0, bls=bls:
                              g.dma_start(
                        out=out.ap()[row0:row0 + 128, :],
                        in_=x2c[:, (bls % 2) * D:(bls % 2 + 1) * D],
                        accum_op=Alu.add), ao)
                    ev_x2c_rd[bls % 2] = ev_ac
                ev_obst[(sub, NPACK - 2)] = ev_og_last
                ev_obst[(sub, NPACK - 1)] = ev_og_last
                ev_outsb_rd = te.ev()
            ev_perm_rd = te.ev()
            ev_oh_rd = te.ev()
            ev_inner_rd = te.ev()

        for c in reversed(p3_cm):
            c.__exit__(None, None, None)

    with nc.Block() as block:
        @block.sync
        def _(eng):
            for th in prog["sync"]:
                th(eng)
            eng.wait_ge(S_so, so.n)

        @block.gpsimd
        def _(eng):
            for th in prog["gpsimd"]:
                th(eng)
            eng.wait_ge(S_ao, ao.n)

        @block.tensor
        def _(eng):
            for th in prog["tensor"]:
                th(eng)

        @block.vector
        def _(eng):
            for th in prog["vector"]:
                th(eng)

        @block.scalar
        def _(eng):
            for th in prog["scalar"]:
                th(eng)

    for c in reversed(per_cm + sem_cm):
        c.__exit__(None, None, None)
    return nc


def stage_core(inputs, core, TPC, TT, seq_len, n_cores):
    NT = TPC // TT
    seq = core // 2 if n_cores > 1 else 0
    half = core % 2 if n_cores > 1 else 0
    t0 = half * TPC

    xfull = np.asarray(inputs["L0_attn_x"], dtype=np.float32)
    xs = xfull[seq]
    x_ext = np.zeros((TPC + 128, D), np.float32)
    x_ext[128:] = xs[t0:t0 + TPC]
    if t0 >= 3:
        x_ext[125:128] = xs[t0 - 3:t0]

    W_in = np.asarray(inputs["L0_attn_in_proj_w"], np.float32)
    conv_k = np.asarray(inputs["L0_attn_conv_k"], np.float32)
    conv_b = np.asarray(inputs["L0_attn_conv_b"], np.float32)
    theta = np.asarray(inputs["L0_attn_theta"], np.float32).reshape(K, Hd)
    dsl = np.asarray(inputs["L0_attn_decay_slopes"], np.float32)
    asl = np.asarray(inputs["L0_attn_anchor_slopes"], np.float32)
    ssc = np.asarray(inputs["L0_attn_score_scale"], np.float32)
    dlg = np.asarray(inputs["L0_attn_deriv_logits"], np.float32)
    W_re = np.asarray(inputs["L0_attn_W_re"], np.float32)
    W_im = np.asarray(inputs["L0_attn_W_im"], np.float32)
    nsc = np.asarray(inputs["L0_attn_norm_scale"], np.float32)
    W_out = np.asarray(inputs["L0_attn_out_proj_w"], np.float32)
    Wc = np.asarray(inputs["L0_ffn_Wc"], np.float32)
    bc = np.asarray(inputs["L0_ffn_bc"], np.float32)
    W1 = np.asarray(inputs["L0_ffn_W1"], np.float32)
    b1 = np.asarray(inputs["L0_ffn_b1"], np.float32)
    W2 = np.asarray(inputs["L0_ffn_W2"], np.float32)
    b2 = np.asarray(inputs["L0_ffn_b2"], np.float32)

    convd = np.zeros((128, KS * NZC * 128), np.float32)
    for tap in range(KS):
        for oc in range(NZC):
            pc = 128 if oc < NZC - 1 else 32
            w = conv_k[tap, 0, oc * 128:oc * 128 + pc]
            blk = np.zeros((128, 128), np.float32)
            blk[:pc, :pc] = np.diag(w)
            convd[:, (tap * NZC + oc) * 128:(tap * NZC + oc + 1) * 128] = blk

    cb_fm = np.zeros((128, NZC), np.float32)
    for oc in range(NZC):
        pc = 128 if oc < NZC - 1 else 32
        cb_fm[:pc, oc] = conv_b[oc * 128:oc * 128 + pc]

    theta_fm = np.zeros((128, NXC), np.float32)
    for j in range(NXC):
        for p in range(128):
            theta_fm[p, j] = theta[2 * j + (p >= 64), p % 64]

    esel = np.zeros((32, NXC * 128), np.float32)
    for j in range(NXC):
        for p in range(128):
            esel[2 * j + (p >= 64), j * 128 + p] = 1.0
    qsel = np.zeros((128, NXC * 32), np.float32)
    for j in range(NXC):
        for p in range(128):
            qsel[p, j * 32 + 2 * j + (p >= 64)] = 1.0
    bmask = np.zeros((128, 4), np.float32)
    for p in range(128):
        bmask[p, p // 32] = 1.0
    tie = np.broadcast_to(
        np.linspace(0.0, 1e-6, GRP, dtype=np.float32)[None, :],
        (128, GRP)).copy()
    iota_t = np.broadcast_to(
        (t0 + np.arange(TT, dtype=np.float32))[None, :], (K, TT)).copy()

    W1es = W1.reshape(ES, D, EF)
    w1p = np.zeros((NDC, 128, ES * EF), np.float32)
    for dc in range(NDC):
        w1p[dc] = W1es[:, dc * 128:(dc + 1) * 128, :].transpose(
            1, 0, 2).reshape(128, ES * EF)
    W2es = W2.reshape(ES, D, EF)
    w2p = np.zeros((NPACK, 128, D), np.float32)
    for pk in range(NPACK):
        for esl in range(4):
            w2p[pk, esl * EF:(esl + 1) * EF, :] = \
                W2es[4 * pk + esl].T
    b1es = b1.reshape(ES, EF)
    b1p = np.zeros((128, NPACK), np.float32)
    for pk in range(NPACK):
        for esl in range(4):
            b1p[esl * EF:(esl + 1) * EF, pk] = b1es[4 * pk + esl]
    b2p = b2.reshape(ES, D)

    return {
        "x": x_ext,
        "win": W_in.reshape(NDC, 128, C),
        "convd": convd,
        "cbias": cb_fm,
        "wout": W_out.reshape(16, 128, D),
        "theta_fm": theta_fm,
        "dslope": np.concatenate([dsl, asl]).reshape(K, 1),
        "sscale": ssc.reshape(K, 1),
        "dlogit": dlg.reshape(1, 3),
        "wre2": np.concatenate([W_re, W_re], 0),
        "wim2": np.concatenate([W_im, W_im], 0),
        "nsc": nsc.reshape(128, 1),
        "wc": Wc.reshape(D, ES).reshape(NDC, 128, ES),
        "bcv": bc.reshape(ES, 1),
        "w1p": w1p, "w2p": w2p, "b1p": b1p, "b2p": b2p,
        "esel": esel, "qsel": qsel, "bmask": bmask, "tie": tie,
        "iota_t": iota_t,
        "twsgn": np.concatenate([np.ones(ND_K), -np.ones(4)]).reshape(
            K, 1).astype(np.float32),
        "twbm": np.concatenate([np.full(ND_K, -float(seq_len - 1)),
                                np.zeros(4)]).reshape(K, 1).astype(
                                    np.float32),
        "flag_first": np.full((128, 1), 1.0 if half == 0 else 0.0,
                              np.float32),
        "ident": np.eye(128, dtype=np.float32),
    }


_TPC, _TT, _SEGT, _SL = 4096, 512, 4, 8192


def kernel(**inputs):
    nc = build_nc(_TPC, _TT, _SEGT, _SL, phases=(1, 2, 3), dev=False)
    in_maps = [stage_core(inputs, c, _TPC, _TT, _SL, N_CORES)
               for c in range(N_CORES)]
    res = run_bass_kernel_spmd(nc, in_maps, core_ids=list(range(N_CORES)))
    out = np.zeros((4, _SL, D), np.float32)
    for c in range(N_CORES):
        seq, half = c // 2, c % 2
        out[seq, half * _TPC:(half + 1) * _TPC] = \
            np.asarray(res.results[c]["out"]).reshape(_TPC, D)
    return out


# version 15 (final)
# speedup vs baseline: 1.0569x; 1.0459x over previous
import math
import os
import numpy as np

import concourse.bass as bass
import concourse.mybir as mybir
from concourse.bass_utils import run_bass_kernel_spmd

F32 = mybir.dt.float32
BF16 = mybir.dt.bfloat16
Alu = mybir.AluOpType
Act = mybir.ActivationFunctionType
AX = mybir.AxisListType.X

D = 1024
DI = 2048
K, Hd = 32, 64
KS = 4
C = 2 * DI + K
NXC = DI // 128
NZC = 2 * NXC + 1
NDC = D // 128
ES, EF = 128, 32
GRP = 32
NPACK = 32
MAGIC = 1.5 * 2 ** 23
TWO_PI = 2.0 * math.pi
ND_K = K - 4
ST_PW = 2 * NXC
N_CORES = 8


def st_re(j):
    return 2 * j


def st_im(j):
    return 2 * j + 1


def MM(t, *a, **kw):
    kw.setdefault("skip_group_check", True)
    return t.matmul(*a, **kw)


def build_nc(TPC, TT, SEG_TILES, seq_len, phases=(1, 2, 3), dev=False):
    if dev is True:
        dev = {"stash", "gst", "part", "x1", "oh", "xdc", "inner", "outg",
               "perm", "l2s", "slp", "tw", "pw", "z", "lnfm"}
    elif not dev:
        dev = set()
    NT = TPC // TT
    BPT = TT // 128
    NSEG = NT // SEG_TILES
    SEG_T = SEG_TILES * TT
    GSEG = SEG_T // GRP
    BLK_SEG = SEG_T // 128
    GPT = TT // GRP
    GPB = 128 // GRP
    SUB = 16 if GSEG >= 16 else GSEG
    NSUB = GSEG // SUB
    MSL = 4 * SUB
    NPG = 8
    NPGRP = NPACK // NPG
    ZW = TT + 3

    nc = bass.Bass("TRN2", target_bir_lowering=False,
                   detect_race_conditions=False)

    def din(n, s, d=F32):
        return nc.dram_tensor(n, s, d, kind="ExternalInput")

    x_in = din("x", [TPC + 128, D])
    win = din("win", [NDC, 128, C])
    convd = din("convd", [128, KS * NZC * 128])
    cbias_in = din("cbias", [128, NZC])
    wout_in = din("wout", [16, 128, D])
    theta_in = din("theta_fm", [128, NXC])
    dslope = din("dslope", [K, 1])
    sscale = din("sscale", [K, 1])
    dlogit = din("dlogit", [1, 3])
    wre_in = din("wre2", [128, 64])
    wim_in = din("wim2", [128, 64])
    nsc_in = din("nsc", [128, 1])
    wc_in = din("wc", [NDC, 128, ES])
    bc_in = din("bcv", [ES, 1])
    w1p_in = din("w1p", [NDC, 128, ES * EF])
    w2p_in = din("w2p", [NPACK, 128, D])
    b1p_in = din("b1p", [128, NPACK])
    b2p_in = din("b2p", [128, D])
    esel_in = din("esel", [32, NXC * 128])
    qsel_in = din("qsel", [128, NXC * 32])
    bmask_in = din("bmask", [128, 4])
    tie_in = din("tie", [128, GRP])
    iota_in = din("iota_t", [K, TT])
    twsgn_in = din("twsgn", [K, 1])
    twbm_in = din("twbm", [K, 1])
    flag_in = din("flag_first", [128, 1])
    ident_in = din("ident", [128, 128])

    out = nc.dram_tensor("out", [TPC, D], F32, kind="ExternalOutput")

    stash = nc.dram_tensor("stash", [NT, 128, NZC * TT], BF16)
    gstash = nc.dram_tensor("gstash", [NT, 128, NXC * TT], BF16)
    ln2st = nc.dram_tensor("ln2st", [NT, 128, BPT * D], BF16)
    ohst = nc.dram_tensor("ohst", [max(NSEG, 1), 128, SEG_T], BF16)
    cc_i = nc.dram_tensor("cc_i", [128, NZC], F32)
    cc_o = nc.dram_tensor("cc_o", [128, NZC], F32)
    ob = nc.dram_tensor("ob", [16, 2, 4, SUB, D], BF16)
    if dev:
        d_stash = nc.dram_tensor("d_stash", [NT, 128, NZC * TT], F32,
                                 kind="ExternalOutput")
        d_gst = nc.dram_tensor("d_gst", [NT, 128, NXC * TT], F32,
                               kind="ExternalOutput")
        d_part = nc.dram_tensor("d_part", [128, NZC * NT], F32,
                                kind="ExternalOutput")
        d_x1 = nc.dram_tensor("d_x1", [TPC, D], F32, kind="ExternalOutput")
        d_oh = nc.dram_tensor("d_oh", [max(NSEG, 1), 128, SEG_T], F32,
                              kind="ExternalOutput")
        d_xdc = nc.dram_tensor("d_xdc", [128, BLK_SEG * 512], F32,
                               kind="ExternalOutput")
        d_inner = nc.dram_tensor("d_inner", [128, NPACK * GSEG], F32,
                                 kind="ExternalOutput")
        d_outg = nc.dram_tensor("d_outg", [128, D], F32,
                                kind="ExternalOutput")
        d_perm = nc.dram_tensor("d_perm", [128, BLK_SEG * 512], F32,
                                kind="ExternalOutput")
        d_l2s = nc.dram_tensor("d_l2s", [128, SEG_TILES * BPT * D], F32,
                               kind="ExternalOutput")
        d_slp = nc.dram_tensor("d_slp", [K, 8], F32, kind="ExternalOutput")
        d_tw = nc.dram_tensor("d_tw", [K, NT + 2], F32,
                              kind="ExternalOutput")
        d_pw = nc.dram_tensor("d_pw", [K, 2 * TT], F32,
                              kind="ExternalOutput")
        d_z = nc.dram_tensor("d_z", [128, 5 * (TT + 3)], F32,
                             kind="ExternalOutput")
        d_lnfm = nc.dram_tensor("d_lnfm", [128, NDC * TT], F32,
                                kind="ExternalOutput")
        d_carry = nc.dram_tensor("d_carry", [128, 2 * NZC], F32,
                                 kind="ExternalOutput")
        d_lad = nc.dram_tensor("d_lad", [128, 16 * TT], F32,
                               kind="ExternalOutput")

    prog = {k: [] for k in ("sync", "gpsimd", "tensor", "vector", "scalar")}

    class Ctr:
        def __init__(self, h, step=1):
            self.h, self.n, self.step = h, 0, step

        def inc(self, instr, k=None):
            k = self.step if k is None else k
            instr.then_inc(self.h, k)
            self.n += k

        def ev(self):
            return (self.h, self.n)

    sem_cm = [nc.semaphore(n) for n in
              ("ld", "wl", "te", "ve", "se", "ge", "so", "ao", "ccs", "sq",
               "chv", "chs")]
    sems = [s.__enter__() for s in sem_cm]
    (S_ld, S_wl, S_te, S_ve, S_se, S_ge, S_so, S_ao, S_cc, S_sq,
     S_chv, S_chs) = sems
    _chain_n = {"v": 0, "s": 0}

    def fence(eng, instr, which="v"):
        _chain_n[which] += 1
        h = S_chv if which == "v" else S_chs
        instr.then_inc(h, 1)
        eng.wait_ge(h, _chain_n[which])
    ld = Ctr(S_ld, 16); wl = Ctr(S_wl, 16); te = Ctr(S_te); ve = Ctr(S_ve)
    se = Ctr(S_se); ge = Ctr(S_ge); so = Ctr(S_so, 16); ao = Ctr(S_ao, 16)
    ccc = Ctr(S_cc); sq = Ctr(S_sq, 16)

    def emit_multi(engname, ctr, after, body, n):
        evs = tuple(e for e in after if e is not None and e[1] > 0)
        ctr.n += ctr.step * n
        h, st = ctr.h, ctr.step

        def th(eng, _evs=evs, _body=body):
            for hh, nn in _evs:
                eng.wait_ge(hh, nn)
            cnt = _body(eng, lambda ins: ins.then_inc(h, st))
            assert cnt == n, f"emit_multi expected {n} DMAs, got {cnt}"
        prog[engname].append(th)
        return ctr.ev()

    def emit(engname, ctr, after, body):
        evs = tuple(e for e in after if e is not None and e[1] > 0)
        if ctr is not None:
            ctr.n += ctr.step
            tgt = (ctr.h, ctr.step)
        else:
            tgt = None

        def th(eng, _evs=evs, _body=body, _tgt=tgt):
            for h, n in _evs:
                eng.wait_ge(h, n)
            last = _body(eng)
            if _tgt is not None:
                assert last is not None, "emit body returned no instruction"
                last.then_inc(_tgt[0], _tgt[1])
        prog[engname].append(th)
        return ctr.ev() if ctr is not None else None

    def T(after, body):
        return emit("tensor", te, after, body)

    def V(after, body):
        return emit("vector", ve, after, body)

    def S(after, body):
        return emit("scalar", se, after, body)

    def G(after, body, ctr=None):
        return emit("gpsimd", ctr if ctr is not None else ge, after, body)

    def SY(after, body, ctr=None):
        return emit("sync", ctr if ctr is not None else so, after, body)

    per_cm = [
        nc.sbuf_tensor("consts", [128, 16], F32),
        nc.sbuf_tensor("partials", [128, NZC * NT], F32),
        nc.sbuf_tensor("carry", [128, NZC], F32),
        nc.sbuf_tensor("flagsb", [128, 1], F32),
        nc.sbuf_tensor("smll", [128, 16], F32),
        nc.sbuf_tensor("identb", [128, 128], BF16),
        nc.sbuf_tensor("eselb", [32, NXC * 128], BF16),
        nc.sbuf_tensor("ccsb", [128, NZC], F32),
        nc.psum_tensor("pall", [128, 3072], F32),
        nc.psum_tensor("pbf", [128, 2048], BF16),
    ]
    (consts, partials, carry, flagsb, smll, identb, eselb, ccsb,
     pall, pbf) = [c.__enter__() for c in per_cm]

    CONST_VALS = [0.5, 1.0, MAGIC, -MAGIC, TWO_PI, math.pi / 2.0,
                  1e-5, -20.0, 20.0, 1.0 / (2.0 * Hd), 1e-4, -1.0,
                  0.0, -0.5, 1.0 / D]
    CV = {v: i for i, v in enumerate(CONST_VALS)}
    for v, i in CV.items():
        nc.const_aps.register(F32, v, consts[:, i:i + 1])

    def cst(v, p=128):
        return consts[0:p, CV[v]:CV[v] + 1]

    def bank(i, w=512, part=128, off=0):
        return pall[0:part, i * 512 + off:i * 512 + off + w]

    def pbf_slot(s, w=128):
        off = (s % 2) * 1024 + (s // 2) * 256
        return pbf[:, off:off + w]

    KP = os.environ.get("KPART", "")
    if os.environ.get("KFRAME"):
        ev0 = None
        ev_id = None
    elif KP == "id":
        ev0 = None
        ev_id = G([], lambda g: g.dma_start(out=identb[:, :],
                                            in_=ident_in.ap()), wl)
    elif KP == "c0":
        ev_id = None
        ev0 = V([], lambda v: [v.memset(consts[:, i:i + 1], val)
                               for val, i in CV.items()] and
                v.memset(partials[:, :], 0.0) and
                v.memset(carry[:, :], 0.0))
    elif KP == "c1":
        ev_id = None
        ev0 = V([], lambda v: v.memset(consts[:, 0:1], 0.5))
    else:
        ev0 = V([], lambda v: [v.memset(consts[:, i:i + 1], val)
                               for val, i in CV.items()] and
                v.memset(partials[:, :], 0.0) and
                v.memset(carry[:, :], 0.0))
        ev_id = G([], lambda g: g.dma_start(out=identb[:, :],
                                            in_=ident_in.ap()), wl)

    def ln_norm(src, dst, nb, stc, after, scr):
        LNCUT = int(os.environ.get("LNCUT", "9"))
        evm0 = V(after, lambda v: [
            v.reduce_sum(stc[:, b:b + 1], src[:, b * D:(b + 1) * D], AX)
            for b in range(nb)][-1])
        evm = V([evm0], lambda v: v.tensor_scalar_mul(
            stc[:, 0:nb], stc[:, 0:nb], 1.0 / D))
        if LNCUT < 2:
            return evm, evm
        evq = S(after, lambda s: [
            s.activation(scr[:, 0:D], src[:, b * D:(b + 1) * D], Act.Square,
                         scale=cst(1.0), bias=cst(0.0),
                         accum_out=stc[:, nb + b:nb + b + 1])
            for b in range(nb)][-1])
        if LNCUT < 3:
            return evq, evq
        def v_var(v):
            fence(v, v.tensor_scalar_mul(stc[:, nb:2 * nb],
                                         stc[:, nb:2 * nb], 1.0 / D))
            fence(v, v.tensor_mul(stc[:, 2 * nb:3 * nb], stc[:, 0:nb],
                                  stc[:, 0:nb]))
            return v.tensor_sub(stc[:, nb:2 * nb], stc[:, nb:2 * nb],
                                stc[:, 2 * nb:3 * nb])
        evv = V([evm, evq], v_var)
        if LNCUT < 4:
            return evv, evv
        evr = S([evv], lambda s: [
            s.activation(stc[:, 2 * nb:3 * nb], stc[:, nb:2 * nb], Act.Ln,
                         scale=cst(1.0), bias=cst(1e-5)),
            s.activation(stc[:, 2 * nb:3 * nb], stc[:, 2 * nb:3 * nb],
                         Act.Exp, scale=cst(-0.5), bias=cst(0.0)),
        ][-1])
        if LNCUT < 5:
            return evr, evr
        evb = V([evr], lambda v: v.scalar_tensor_tensor(
            stc[:, 3 * nb:4 * nb], stc[:, 0:nb], -1.0,
            stc[:, 2 * nb:3 * nb], Alu.mult, Alu.mult))
        if LNCUT < 6:
            return evb, evb
        evn = S([evb], lambda s: [
            s.activation(dst[:, b * D:(b + 1) * D], src[:, b * D:(b + 1) * D],
                         Act.Identity, scale=stc[:, 2 * nb + b:2 * nb + b + 1],
                         bias=stc[:, 3 * nb + b:3 * nb + b + 1])
            for b in range(nb)][-1])
        return evn, evr

    def tm_to_fm(src, nb, put, after):
        if not isinstance(after, list):
            after = [after]
        if os.environ.get("TFCUT"):
            e = V(after, lambda v: v.engine_nop())
            return e, [e]
        evs = []
        for b in range(nb):
            for dc in range(NDC):
                slot = (b * NDC + dc) % 4
                gate = [evs[-2][0]] if len(evs) >= 2 else []
                evt = T(after + gate,
                        lambda t, b=b, dc=dc, slot=slot: t.transpose(
                            pbf_slot(slot),
                            src[:, b * D + dc * 128:b * D + (dc + 1) * 128],
                            identb[:, :]))
                eve = V([evt], lambda v, b=b, dc=dc, slot=slot: v.tensor_copy(
                    put(b, dc), pbf_slot(slot)))
                evs.append((eve, evt))
        return evs[-1][0], [e[0] for e in evs]

    p1_cm = [
        nc.sbuf_tensor("theta2p", [128, NXC], F32),
        nc.sbuf_tensor("thetasb", [128, NXC], F32),
        nc.sbuf_tensor("slp", [K, 8], F32),
        nc.sbuf_tensor("twsc", [K, 1], F32),
        nc.sbuf_tensor("twbias", [K, NT], F32),
        nc.sbuf_tensor("sscb", [K, 1], F32),
        nc.sbuf_tensor("wpoly", [128, 3], F32),
        nc.sbuf_tensor("iotasb", [K, TT], F32),
        nc.sbuf_tensor("winb", [128, NDC * C], BF16),
        nc.sbuf_tensor("convb", [128, KS * NZC * 128], BF16),
        nc.sbuf_tensor("xtm", [128, 2 * D], F32),
        nc.sbuf_tensor("lntm", [128, BPT * D], BF16),
        nc.sbuf_tensor("lnfm", [128, NDC * TT], BF16),
        nc.sbuf_tensor("zsb", [128, 5 * ZW], BF16),
        nc.sbuf_tensor("zhalo", [128, NZC * 3], BF16),
        nc.sbuf_tensor("stsb", [128, NZC * TT], BF16),
        nc.sbuf_tensor("gtsb", [128, 2 * TT], BF16),
        nc.sbuf_tensor("lnst", [128, 4 * 4], F32),
        nc.sbuf_tensor("pwsb", [K, TT], F32),
        nc.sbuf_tensor("scp1", [128, max(D, 2 * TT)], F32),
        nc.sbuf_tensor("chainr", [128, 4 * TT], F32),
        nc.sbuf_tensor("sincos", [128, 4 * TT], BF16),
        nc.sbuf_tensor("m2b", [128, 3 * TT], F32),
        nc.sbuf_tensor("remb", [128, 2 * TT], BF16),
        nc.sbuf_tensor("ones_row", [1, 128], F32),
        nc.sbuf_tensor("cbsb", [128, NZC], F32),
        nc.sbuf_tensor("usb", [128, 5 * TT], F32),
    ]
    (theta2p, thetasb, slp, twsc, twbias, sscb, wpoly, iotasb, winb, convb,
     xtm, lntm, lnfm, zsb, zhalo, stsb, gtsb, lnst, pwsb,
     scp1, chainr, sincos, m2b, remb, ones_row, cbsb, usb) = \
        [c.__enter__() for c in p1_cm]

    P1SETUP = int(os.environ.get("P1SETUP", "99"))
    if 1 in phases and not os.environ.get("KFRAME"):
        def g_loads(g, inc):
            n = 0
            for dst, srcp in [
                (theta2p[:, :], theta_in.ap()),
                (thetasb[:, :], theta_in.ap()),
                (slp[:, 0:1], dslope.ap()),
                (sscb[:, :], sscale.ap()),
                (smll[0:1, 0:3], dlogit.ap()),
                (flagsb[:, :], flag_in.ap()),
                (iotasb[:, :], iota_in.ap()),
                (slp[:, 7:8], twsgn_in.ap()),
                (smll[0:K, 12:13], twbm_in.ap()),
                (convb[:, :], convd.ap()),
                (eselb[:, :], esel_in.ap()),
                (cbsb[:, :], cbias_in.ap()),
            ]:
                inc(g.dma_start(out=dst, in_=srcp))
                n += 1
            for dc in range(NDC):
                inc(g.dma_start(out=winb[:, dc * C:(dc + 1) * C],
                                in_=win.ap()[dc, :, :]))
                n += 1
            return n
        ev_gl = emit_multi("gpsimd", wl, [], g_loads, 12 + NDC) if P1SETUP >= 1 else None

        ev_or = V([ev0], lambda v: v.memset(ones_row[:, :], 1.0)) if P1SETUP >= 2 else None
        ev_stz = V([], lambda v: v.memset(stsb[:, :], 0.0)) if P1SETUP >= 3 else None
        ev1 = None if P1SETUP < 4 else V([ev_gl, ev0], lambda v: [
            v.tensor_scalar_mul(theta2p[:, :], theta2p[:, :], 1.0 / TWO_PI),
            v.scalar_tensor_tensor(slp[:, 1:2], slp[:, 0:1], -1.0,
                                   slp[:, 0:1], Alu.mult, Alu.max),
            v.tensor_scalar_max(slp[:, 2:3], slp[:, 0:1], 0.0),
        ][-1])
        ev2 = None if P1SETUP < 5 else S([ev1], lambda s: [
            s.activation(slp[:, 3:4], slp[:, 1:2], Act.Exp,
                         scale=cst(-1.0, K), bias=cst(0.0, K)),
            s.activation(slp[:, 4:5], slp[:, 3:4], Act.Ln,
                         scale=cst(1.0, K), bias=cst(1.0, K)),
            s.activation(smll[0:1, 4:7], smll[0:1, 0:3], Act.Exp,
                         scale=cst(1.0, 1), bias=cst(0.0, 1)),
        ][-1])

        def v_setup2(v):
            fence(v, v.tensor_add(slp[:, 5:6], slp[:, 2:3],
                                  slp[:, 4:5]))
            v.tensor_mul(twsc[:, :], slp[:, 5:6], slp[:, 7:8])
            fence(v, v.tensor_mul(slp[:, 6:7], slp[:, 5:6],
                                  smll[0:K, 12:13]))
            for i in range(NT):
                v.scalar_tensor_tensor(twbias[:, i:i + 1], twsc[:, :],
                                       float(i * TT), slp[:, 6:7],
                                       Alu.mult, Alu.add)
            return v.reduce_sum(smll[0:1, 7:8], smll[0:1, 4:7], AX)
        ev3a = V([ev2], v_setup2) if P1SETUP >= 6 else None
        ev3b = None if P1SETUP < 6 else V([ev3a], lambda v: v.reciprocal(
            smll[0:1, 8:9], smll[0:1, 7:8]))
        ev3 = None if P1SETUP < 6 else V([ev3b], lambda v:
            v.scalar_tensor_tensor(
                smll[0:1, 9:12], smll[0:1, 4:7], 1.0,
                smll[0:1, 8:9].broadcast_to([1, 3]), Alu.mult, Alu.mult))
        ev4 = T([ev3, ev_or], lambda t: MM(t, 
            bank(0, 3), ones_row[:, :], smll[0:1, 9:12],
            start=True, stop=True))
        ev_wp = None if P1SETUP < 8 else V([ev4], lambda v: [
            v.tensor_copy(wpoly[:, 0:2], bank(0, 2)),
            v.tensor_scalar_mul(wpoly[:, 2:3], bank(0, 1, off=2), -1.0),
        ][-1])

        xpfx = xtm
        lnpfx = lnfm
        SKIP_PFX = os.environ.get('P1CUT') in ('nopfx', 'setup')
        if not SKIP_PFX:
            evx = SY([], lambda sy: sy.dma_start(out=xpfx[:, 0:D],
                                                 in_=x_in.ap()[0:128, :]),
                     ld)
            evn, _ = ln_norm(xtm, lntm, 1, lnst, [evx, ev0, ev_id], scp1)
            evfm, _ = tm_to_fm(lntm, 1,
                               lambda b, dc: lnfm[:, dc * 128:dc * 128 + 128],
                               evn)
        else:
            evn = evfm = None
        ev_zh = []
        for oc in (range(NZC) if not SKIP_PFX and os.environ.get('P1CUT') != 'zh' else []):
            pc = 128 if oc < NZC - 1 else 32
            bnk = oc % 2

            def t_pfx(t, oc=oc, pc=pc, bnk=bnk):
                last = None
                for dc in range(NDC):
                    last = MM(t, 
                        bank(bnk, 128, pc), winb[:, dc * C + oc * 128:
                                                 dc * C + oc * 128 + pc],
                        lnpfx[:, dc * 128:(dc + 1) * 128],
                        start=(dc == 0), stop=(dc == NDC - 1))
                return last
            gate = [ev_zh[-2][1]] if len(ev_zh) >= 2 else [ev_wp, evfm]
            evt = T(gate + [evfm, ev_gl], t_pfx)
            eve = V([evt], lambda v, oc=oc, pc=pc, bnk=bnk: v.tensor_copy(
                zhalo[0:pc, oc * 3:oc * 3 + 3], bank(bnk, 3, pc, off=125)))
            ev_zh.append((evt, eve))
        ev_zhalo = ev_zh[-1][1] if ev_zh else None
        if SKIP_PFX:
            ev_zhalo = V([], lambda v: v.memset(zhalo[:, :], 0.0))

        slot_rd = [None] * 6
        zc_rd = [None, None]
        pwe_rd = [None, None]
        zb_free = [None, None]
        prev_use = [evn]
        st_prev = None
        gstp = [None]
        if "slp" in dev:
            G([ev3, ev_wp], lambda g: g.dma_start(
                out=d_slp.ap(), in_=slp[:, :]), ao)
        if "tw" in dev:
            G([ev3, ev_wp], lambda g: g.dma_start(
                out=d_tw.ap()[:, 0:NT], in_=twbias[:, :],
                allow_slow_non_contiguous=True), ao)
            G([ev3, ev_wp], lambda g: g.dma_start(
                out=d_tw.ap()[:, NT:NT + 1], in_=twsc[:, :],
                allow_slow_non_contiguous=True), ao)
            G([ev3, ev_wp], lambda g: g.dma_start(
                out=d_tw.ap()[:, NT + 1:NT + 2], in_=sscb[:, :],
                allow_slow_non_contiguous=True), ao)
        halo_wr = [ev_zhalo] * NZC
        prev_vc = [None]
        usb_rd = [None] * 5
        m2_rd = [None] * 3
        chr_rd = [None] * 2
        sc_rd = [None] * 2
        prev_fm = evfm
        prev_p_S = None

        tb = []
        for i in (range(NT) if os.environ.get("P1CUT") not in ("setup", "pfx", "zh") else []):
            ev_lns = []
            for hb in range(2):
                evx = SY([prev_use[0]], lambda sy, i=i, hb=hb: sy.dma_start(
                    out=xtm[:, :].rearrange("p (b d) -> p b d", d=D),
                    in_=x_in.ap()[128 + i * TT + hb * 256:
                                  128 + i * TT + hb * 256 + 256, :].rearrange(
                        "(b p) d -> p b d", p=128)), ld)
                evn, _ = ln_norm(xtm, lntm[:, hb * 2 * D:(hb + 1) * 2 * D],
                                 2, lnst, [evx, prev_fm], scp1)
                prev_use[0] = evn
                ev_lns.append(evn)
            evfm, _ = tm_to_fm(lntm, BPT,
                               lambda b, dc: lnfm[:, dc * TT + b * 128:
                                                  dc * TT + (b + 1) * 128],
                               ev_lns + [ev_zhalo])
            prev_fm = evfm
            if os.environ.get("P1CUT") == "fm":
                tb = []
                continue

            oc_order = [NZC - 1] + list(range(NZC - 1))
            ev_pw_bf = None
            ev_pwe = {}
            st_ctx = {}
            pend_g = []

            def em_u(c):
                ub = c % 5
                cb2 = st_ctx[c]['cb']
                ev = V([st_ctx[c]['cv'], usb_rd[ub]],
                       lambda v, c=c, cb2=cb2, ub=ub:
                       v.tensor_scalar_add(
                           usb[:, ub * TT:(ub + 1) * TT],
                           bank(cb2, TT), cbsb[:, c:c + 1]))
                zc_rd[cb2 - 2] = ev
                st_ctx[c]['u'] = ev

            def em_gate(g):
                oc, cb2, ev_cvg = g
                jg = oc - NXC
                ev_g = S([ev_cvg, gstp[0]],
                         lambda s, jg=jg, cb2=cb2, oc=oc: s.activation(
                    gtsb[:, (jg % 2) * TT:(jg % 2 + 1) * TT],
                    bank(cb2, TT), Act.Silu, scale=cst(1.0),
                    bias=cbsb[:, oc:oc + 1]))
                zc_rd[cb2 - 2] = ev_g
                if jg % 2 == 1:
                    gstp[0] = SY([ev_g], lambda sy, i=i, jg=jg:
                                 sy.dma_start(
                        out=gstash.ap()[i, :, (jg - 1) * TT:
                                        (jg + 1) * TT],
                        in_=gtsb[:, :]), so)

            def em_s1(c):
                ub, mb = c % 5, c % 3
                ev = S([st_ctx[c]['u'], m2_rd[mb]],
                       lambda s, c=c, ub=ub, mb=mb: [
                    s.activation(m2b[:, mb * TT:(mb + 1) * TT],
                                 usb[:, ub * TT:(ub + 1) * TT], Act.Identity,
                                 scale=theta2p[:, c:c + 1], bias=cst(MAGIC)),
                    s.activation(m2b[:, mb * TT:(mb + 1) * TT],
                                 m2b[:, mb * TT:(mb + 1) * TT], Act.Identity,
                                 scale=cst(1.0), bias=cst(-MAGIC)),
                    s.activation(m2b[:, mb * TT:(mb + 1) * TT],
                                 m2b[:, mb * TT:(mb + 1) * TT], Act.Identity,
                                 scale=cst(TWO_PI), bias=cst(0.0)),
                ][-1])
                st_ctx[c]['s1'] = ev

            def em_r(c):
                ub, mb, rb = c % 5, c % 3, c % 2

                def v_r(v, c=c, ub=ub, mb=mb, rb=rb):
                    v.scalar_tensor_tensor(
                        chainr[:, rb * 2 * TT:rb * 2 * TT + TT],
                        usb[:, ub * TT:(ub + 1) * TT],
                        thetasb[:, c:c + 1],
                        m2b[:, mb * TT:(mb + 1) * TT],
                        Alu.mult, Alu.subtract)
                    return v.scalar_tensor_tensor(
                        chainr[:, rb * 2 * TT + TT:rb * 2 * TT + 2 * TT],
                        chainr[:, rb * 2 * TT:rb * 2 * TT + TT], -1.0,
                        chainr[:, rb * 2 * TT:rb * 2 * TT + TT],
                        Alu.mult, Alu.max)
                ev = V([st_ctx[c]['s1'], chr_rd[rb]], v_r)
                st_ctx[c]['r'] = ev

            def em_s2(c):
                ub, mb, rb, sb = c % 5, c % 3, c % 2, c % 2

                def s_tr(s, c=c, ub=ub, mb=mb, rb=rb, sb=sb):
                    s.activation(sincos[:, sb * 2 * TT:sb * 2 * TT + TT],
                                 chainr[:, rb * 2 * TT:rb * 2 * TT + TT],
                                 Act.Sin, scale=cst(1.0), bias=cst(0.0))
                    s.activation(
                        sincos[:, sb * 2 * TT + TT:sb * 2 * TT + 2 * TT],
                        chainr[:, rb * 2 * TT + TT:rb * 2 * TT + 2 * TT],
                        Act.Sin, scale=cst(-1.0), bias=cst(math.pi / 2))
                    return s.activation(m2b[:, mb * TT:(mb + 1) * TT],
                                        usb[:, ub * TT:(ub + 1) * TT],
                                        Act.Identity, scale=wpoly[:, 2:3],
                                        bias=wpoly[:, 1:2])
                ev = S([st_ctx[c]['r'], sc_rd[sb]], s_tr)
                st_ctx[c]['s2'] = ev

            def em_chain(c):
                ub, mb, rb, sb = c % 5, c % 3, c % 2, c % 2
                pb = 4 + (c % 2)

                def v_chain(v, c=c, ub=ub, mb=mb, rb=rb, sb=sb, pb=pb, i=i):
                    v.tensor_tensor(chainr[:, rb * 2 * TT:rb * 2 * TT + TT],
                                    m2b[:, mb * TT:(mb + 1) * TT],
                                    usb[:, ub * TT:(ub + 1) * TT], Alu.mult)
                    v.scalar_tensor_tensor(
                        remb[:, 0:TT],
                        chainr[:, rb * 2 * TT:rb * 2 * TT + TT],
                        wpoly[:, 0:1],
                        sincos[:, sb * 2 * TT + TT:sb * 2 * TT + 2 * TT],
                        Alu.add, Alu.mult)
                    v.scalar_tensor_tensor(
                        remb[:, TT:2 * TT],
                        chainr[:, rb * 2 * TT:rb * 2 * TT + TT],
                        wpoly[:, 0:1],
                        sincos[:, sb * 2 * TT:sb * 2 * TT + TT],
                        Alu.add, Alu.mult)
                    v.scalar_tensor_tensor(
                        stsb[:, st_re(c) * TT:(st_re(c) + 1) * TT],
                        remb[:, 0:TT], 1.0, bank(pb, TT),
                        Alu.mult, Alu.mult,
                        accum_out=partials[:, st_re(c) * NT + i:
                                           st_re(c) * NT + i + 1])
                    return v.scalar_tensor_tensor(
                        stsb[:, st_im(c) * TT:(st_im(c) + 1) * TT],
                        remb[:, TT:2 * TT], 1.0, bank(pb, TT),
                        Alu.mult, Alu.mult,
                        accum_out=partials[:, st_im(c) * NT + i:
                                           st_im(c) * NT + i + 1])
                ev = V([st_ctx[c]['s2'], ev_pwe[c]], v_chain)
                usb_rd[ub] = ev
                m2_rd[mb] = ev
                chr_rd[rb] = ev
                sc_rd[sb] = ev
                pwe_rd[c % 2] = ev
                prev_vc[0] = ev

            for idx, oc in enumerate(oc_order):
                pc = 32 if oc == NZC - 1 else 128
                zb = idx % 2
                slot = oc % 5
                cb = 2 + (idx % 2)

                def t_ip(t, oc=oc, pc=pc, zb=zb):
                    last = None
                    for dc in range(NDC):
                        last = MM(t, 
                            bank(zb, TT, pc),
                            winb[:, dc * C + oc * 128:dc * C + oc * 128 + pc],
                            lnfm[:, dc * TT:(dc + 1) * TT],
                            start=(dc == 0), stop=(dc == NDC - 1))
                    return last
                ev_ip = T([evfm, zb_free[zb]], t_ip)

                def v_ev(v, oc=oc, pc=pc, zb=zb, slot=slot):
                    v.tensor_copy(zsb[0:pc, slot * ZW:slot * ZW + 3],
                                  zhalo[0:pc, oc * 3:oc * 3 + 3])
                    return v.tensor_copy(
                        zsb[0:pc, slot * ZW + 3:slot * ZW + 3 + TT],
                        bank(zb, TT, pc))
                ev_ev = V([ev_ip, slot_rd[slot], halo_wr[oc]], v_ev)
                zb_free[zb] = ev_ev

                def t_cv(t, oc=oc, pc=pc, slot=slot, cb=cb):
                    last = None
                    for tap in range(KS):
                        last = MM(t, 
                            bank(cb, TT, pc),
                            convb[0:pc, (tap * NZC + oc) * 128:
                                  (tap * NZC + oc) * 128 + pc],
                            zsb[0:pc, slot * ZW + tap:slot * ZW + tap + TT],
                            start=(tap == 0), stop=(tap == KS - 1))
                    return last
                ev_cv = T([ev_ev, zc_rd[cb - 2]], t_cv)

                halo_wr[oc] = V([ev_cv], lambda v, oc=oc, pc=pc, slot=slot:
                                v.tensor_copy(
                    zhalo[0:pc, oc * 3:oc * 3 + 3],
                    zsb[0:pc, slot * ZW + TT:slot * ZW + TT + 3]))
                slot_rd[slot] = halo_wr[oc]

                if oc == NZC - 1:
                    def v_sc(v, cb=cb):
                        fence(v, v.tensor_scalar(pwsb[:, 0:TT],
                                                 bank(cb, TT, 32),
                                                 cbsb[0:K, NZC - 1:NZC],
                                                 sscb[:, :],
                                                 Alu.add, Alu.mult))
                        return v.tensor_scalar(pwsb[:, 0:TT], pwsb[:, 0:TT],
                                               20.0, -20.0, Alu.min, Alu.max)
                    ev_sc = V([ev_cv, prev_p_S], v_sc)
                    zc_rd[cb - 2] = ev_sc
                    ev_p = S([ev_sc, usb_rd[4]], lambda s, i=i: [
                        s.activation(pwsb[:, 0:TT], pwsb[:, 0:TT], Act.Exp,
                                     scale=cst(1.0, K), bias=cst(0.0, K)),
                        s.activation(usb[0:K, 4 * TT:5 * TT], iotasb[:, :],
                                     Act.Exp, scale=twsc[:, :],
                                     bias=twbias[:, i:i + 1]),
                    ][-1])
                    prev_p_S = ev_p
                    ev_pw_bf = V([ev_p, st_prev, ev_stz],
                                 lambda v, i=i: v.scalar_tensor_tensor(
                        stsb[0:K, ST_PW * TT:ST_PW * TT + TT],
                        pwsb[:, 0:TT], 1.0, usb[0:K, 4 * TT:5 * TT],
                        Alu.mult, Alu.mult,
                        accum_out=partials[0:K, ST_PW * NT + i:
                                           ST_PW * NT + i + 1]))
                elif oc < NXC:
                    st_ctx[oc] = {'cv': ev_cv, 'cb': cb}
                else:
                    pend_g.append((oc, cb, ev_cv))

                cx = idx - 1
                if len(pend_g) > 1 or (pend_g and idx == NZC - 1):
                    em_gate(pend_g.pop(0))
                    if idx == NZC - 1 and pend_g:
                        em_gate(pend_g.pop(0))
                if 0 <= cx - 1 < NXC:
                    em_u(cx - 1)
                if 0 <= cx - 2 < NXC:
                    em_s1(cx - 2)
                if 0 <= cx - 4 < NXC:
                    em_s2(cx - 4)
                if 0 <= cx - 5 < NXC:
                    em_chain(cx - 5)
                if 0 <= cx - 3 < NXC:
                    em_r(cx - 3)
                if 0 <= cx - 3 < NXC:
                    c2 = cx - 3
                    ev_pwe[c2] = T([ev_pw_bf, pwe_rd[c2 % 2]],
                                   lambda t, c2=c2: MM(t, 
                        bank(4 + (c2 % 2), TT),
                        eselb[:, c2 * 128:(c2 + 1) * 128],
                        stsb[0:K, ST_PW * TT:ST_PW * TT + TT],
                        start=True, stop=True))

            if dev and "lad" in dev and i == 0:
                evd = emit_multi("gpsimd", ao, [ve.ev(), se.ev()],
                                 lambda g, inc: [
                    inc(g.dma_start(out=d_lad.ap()[:, 0:5 * TT],
                                    in_=usb[:, :])),
                    inc(g.dma_start(out=d_lad.ap()[:, 5 * TT:8 * TT],
                                    in_=m2b[:, :])),
                    inc(g.dma_start(out=d_lad.ap()[:, 8 * TT:12 * TT],
                                    in_=chainr[:, :])),
                    inc(g.dma_start(out=d_lad.ap()[:, 12 * TT:16 * TT],
                                    in_=sincos[:, :])),
                ] and 4, 4)
                usb_rd = [evd] * 5
                m2_rd = [evd] * 3
                chr_rd = [evd] * 2
                sc_rd = [evd] * 2

            last_v = ve.ev()
            last_s = se.ev()
            ev_st = SY([last_v, st_prev], lambda sy, i=i: sy.dma_start(
                out=stash.ap()[i, :, :], in_=stsb[:, :]), so)
            st_prev = ev_st
            tb = []
            if "stash" in dev:
                st_prev = G([last_v, st_prev], lambda g, i=i: g.dma_start(
                    out=d_stash.ap()[i, :, :], in_=stsb[:, :]), ao)

        if "part" in dev:
            G([ve.ev()], lambda g: g.dma_start(
                out=d_part.ap(), in_=partials[:, :]), ao)


    def barrier():
        evs = [te.ev(), ve.ev(), se.ev(), ge.ev(), (S_ld, ld.n),
               (S_so, so.n), (S_wl, wl.n), (S_ao, ao.n)]
        outs = []
        outs.append(T(evs, lambda t: t.nop()))
        outs.append(V(evs, lambda v: v.engine_nop()))
        outs.append(S(evs, lambda s: s.nop()))
        outs.append(G(evs, lambda g: g.nop()))
        outs.append(SY(evs, lambda sy: sy.nop(), None))
        return [o for o in outs if o is not None]

    if (1 in phases) and (2 in phases):
        ev_tot0 = V([ve.ev(), se.ev()], lambda v: v.tensor_reduce(
            ccsb[:, :],
            partials[:, :].rearrange("p (s t) -> p s t", t=NT),
            AX, Alu.add))
        ev_tot = V([ev_tot0], lambda v: v.tensor_scalar_mul(
            ccsb[:, :], ccsb[:, :], flagsb[:, :]))
        ev_ci = G([ev_tot], lambda g: g.dma_start(out=cc_i.ap(),
                                                  in_=ccsb[:, :]), ao)
        ev_cc = emit("gpsimd", ccc, [ev_ci], lambda g: g.collective_compute(
            "AllReduce", Alu.add,
            replica_groups=[[2 * p, 2 * p + 1] for p in range(4)],
            ins=[cc_i.ap()], outs=[cc_o.ap()]))
        ev_co = G([ev_cc], lambda g: g.dma_start(out=ccsb[:, :],
                                                 in_=cc_o.ap()), wl)
        def v_cin(v):
            fence(v, v.tensor_scalar_mul(smll[:, 13:14], flagsb[:, :], -1.0))
            fence(v, v.tensor_scalar_add(smll[:, 13:14], smll[:, 13:14],
                                         1.0))
            return v.tensor_scalar_mul(carry[:, :], ccsb[:, :],
                                       smll[:, 13:14])
        ev_cin = V([ev_co], v_cin)
        bar1 = barrier()
        if "hbm" in dev:
            G(bar1, lambda g: g.dma_start(out=d_stash.ap(),
                                          in_=stash.ap()), ao)
            G(bar1, lambda g: g.dma_start(out=d_gst.ap(),
                                          in_=gstash.ap()), ao)
            G(bar1, lambda g: g.dma_start(out=d_part.ap(),
                                          in_=partials[:, :]), ao)
            G(bar1, lambda g: g.dma_start(out=d_carry.ap()[:, 0:NZC],
                                          in_=carry[:, :]), ao)
            G(bar1, lambda g: g.dma_start(out=d_carry.ap()[:, NZC:2 * NZC],
                                          in_=ccsb[:, :]), ao)

    for c in reversed(p1_cm):
        c.__exit__(None, None, None)

    if 2 in phases:
        p2_cm = [
            nc.sbuf_tensor("woutb", [128, 16 * D], BF16),
            nc.sbuf_tensor("wcb", [128, NDC * ES], BF16),
            nc.sbuf_tensor("bdre", [128, 128], BF16),
            nc.sbuf_tensor("bdim", [128, 128], BF16),
            nc.sbuf_tensor("wresb", [128, 64], F32),
            nc.sbuf_tensor("wimsb", [128, 64], F32),
            nc.sbuf_tensor("nscsb", [128, 1], F32),
            nc.sbuf_tensor("qselb", [128, NXC * 32], BF16),
            nc.sbuf_tensor("bcsb2", [ES, 1], F32),
            nc.sbuf_tensor("tiesb", [128, GRP], F32),
            nc.sbuf_tensor("stsb2", [128, NZC * TT], BF16),
            nc.sbuf_tensor("gtsb2", [128, NXC * TT], BF16),
            nc.sbuf_tensor("numrot", [128, 4 * TT], BF16),
            nc.sbuf_tensor("sqrot", [128, 2 * TT], BF16),
            nc.sbuf_tensor("densb", [32, TT], F32),
            nc.sbuf_tensor("invd", [32, TT], F32),
            nc.sbuf_tensor("snormb", [32, TT], BF16),
            nc.sbuf_tensor("ypsb", [128, NXC * TT], BF16),
            nc.sbuf_tensor("y1b", [128, 2 * TT], BF16),
            nc.sbuf_tensor("ygsb", [128, NXC * TT], BF16),
            nc.sbuf_tensor("xtm2", [128, BPT * D], F32),
            nc.sbuf_tensor("x1sb", [128, BPT * D], F32),
            nc.sbuf_tensor("ln2tm", [128, BPT * D], BF16),
            nc.sbuf_tensor("lnfm2", [128, NDC * TT], BF16),
            nc.sbuf_tensor("lgsb", [128, TT], F32),
            nc.sbuf_tensor("gmx", [128, GPT], F32),
            nc.sbuf_tensor("ohsb", [128, SEG_T], BF16),
            nc.sbuf_tensor("lnst2", [128, 16], F32),
            nc.sbuf_tensor("scp2", [128, max(D, 2 * TT)], F32),
            nc.sbuf_tensor("t2sb", [32, 2 * TT], F32),
        ]
        (woutb, wcb, bdre, bdim, wresb, wimsb, nscsb, qselb, bcsb2, tiesb,
         stsb2, gtsb2, numrot, sqrot, densb, invd, snormb, ypsb, y1b, ygsb,
         xtm2, x1sb, ln2tm, lnfm2, lgsb, gmx, ohsb, lnst2, scp2,
         t2sb) = [c.__enter__() for c in p2_cm]

        def g2_loads(g, inc):
            n = 0
            for i16 in range(16):
                inc(g.dma_start(out=woutb[:, i16 * D:(i16 + 1) * D],
                                in_=wout_in.ap()[i16, :, :]))
                n += 1
            for dc in range(NDC):
                inc(g.dma_start(out=wcb[:, dc * ES:(dc + 1) * ES],
                                in_=wc_in.ap()[dc, :, :]))
                n += 1
            for dst, srcp in [
                (wresb[:, :], wre_in.ap()), (wimsb[:, :], wim_in.ap()),
                (nscsb[:, :], nsc_in.ap()), (qselb[:, :], qsel_in.ap()),
                (bcsb2[:, :], bc_in.ap()), (tiesb[:, :], tie_in.ap()),
            ]:
                inc(g.dma_start(out=dst, in_=srcp))
                n += 1
            return n
        ev_g2 = emit_multi("gpsimd", wl, bar1, g2_loads, 16 + NDC + 6)

        ev_bdz = V(bar1, lambda v: [v.memset(bdre[:, :], 0.0),
                                    v.memset(bdim[:, :], 0.0)][-1])
        ev_bd = V([ev_g2, ev_bdz], lambda v: [
            v.tensor_scalar_mul(bdre[0:64, 0:64], wresb[0:64, :],
                                nscsb[0:64, :]),
            v.tensor_scalar_mul(bdre[64:128, 64:128], wresb[64:128, :],
                                nscsb[0:64, :]),
            v.tensor_scalar_mul(bdim[0:64, 0:64], wimsb[0:64, :],
                                nscsb[64:128, :]),
            v.tensor_scalar_mul(bdim[64:128, 64:128], wimsb[64:128, :],
                                nscsb[64:128, :]),
        ][-1])

        ev_st_rd = None
        ev_gt_rd = None
        slot_free = [None] * 4
        sq_free = [None, None]
        yp_free = [None, None]
        se_free = [None, None]
        pout_free = [None, None]
        ev_oh_dma = None
        ev_q_rd = None
        tb2 = []
        x1_free = []
        ln2_free = []
        prev_rs = None
        snorm_rd = None
        op_rd = None

        for i in range(NT):
            seg = i // SEG_TILES
            tis = i % SEG_TILES

            ev_sld = SY([ev_st_rd] + bar1, lambda sy, i=i: sy.dma_start(
                out=stsb2[:, :], in_=stash.ap()[i, :, :]), ld)
            ev_gld = SY([ev_gt_rd], lambda sy, i=i: sy.dma_start(
                out=gtsb2[:, :], in_=gstash.ap()[i, :, :]), ld)
            ev_xld = SY([pout_free[0], pout_free[1]],
                        lambda sy, i=i: sy.dma_start(
                out=xtm2[:, :].rearrange("p (b d) -> p b d", d=D),
                in_=x_in.ap()[128 + i * TT:128 + (i + 1) * TT, :].rearrange(
                    "(b p) d -> p b d", p=128)), ld)

            if i > 0:
                ev_cu = V([], lambda v, i=i: v.tensor_add(
                    carry[:, :], carry[:, :],
                    partials[:, :].rearrange("p (s t) -> p t s",
                                             t=NT)[:, i - 1, :]))
            else:
                ev_cu = ev_cin if (1 in phases) else None

            last_scan = {}
            ev_ypre = {}
            for j in range(NXC):
                sre_slot = (2 * j) % 4
                sim_slot = (2 * j + 1) % 4

                sfre = slot_free[sre_slot] or []
                sfim = slot_free[sim_slot] or []
                ev_s_re = V([ev_sld, ev_cu] + sfre,
                            lambda v, j=j, s=sre_slot: v.tensor_tensor_scan(
                    numrot[:, s * TT:(s + 1) * TT],
                    stsb2[:, st_re(j) * TT:(st_re(j) + 1) * TT],
                    stsb2[:, st_re(j) * TT:(st_re(j) + 1) * TT],
                    carry[:, st_re(j):st_re(j) + 1], Alu.add, Alu.bypass))
                ev_s_im = V(sfim,
                            lambda v, j=j, s=sim_slot: v.tensor_tensor_scan(
                    numrot[:, s * TT:(s + 1) * TT],
                    stsb2[:, st_im(j) * TT:(st_im(j) + 1) * TT],
                    stsb2[:, st_im(j) * TT:(st_im(j) + 1) * TT],
                    carry[:, st_im(j):st_im(j) + 1], Alu.add, Alu.bypass))

                ev_q_re = S([ev_s_re, sq_free[0]], lambda s, j=j,
                            ss=sre_slot: s.activation(
                    sqrot[:, 0:TT], numrot[:, ss * TT:(ss + 1) * TT],
                    Act.Square, scale=cst(1.0), bias=cst(0.0)))
                ev_q_im = S([ev_s_im, sq_free[1]], lambda s, j=j,
                            ss=sim_slot: s.activation(
                    sqrot[:, TT:2 * TT], numrot[:, ss * TT:(ss + 1) * TT],
                    Act.Square, scale=cst(1.0), bias=cst(0.0)))

                ev_qm_re = T([ev_q_re, ev_q_rd], lambda t, j=j: MM(t, 
                    bank(0, TT, 32), qselb[:, j * 32:(j + 1) * 32],
                    sqrot[:, 0:TT], start=(j == 0), stop=False))
                ev_qm_im = T([ev_q_im], lambda t, j=j: MM(t, 
                    bank(0, TT, 32), qselb[:, j * 32:(j + 1) * 32],
                    sqrot[:, TT:2 * TT], start=False, stop=(j == NXC - 1)))
                sq_free[0] = ev_qm_re
                sq_free[1] = ev_qm_im

                ypb = 1 + (j % 2)
                ev_yp = T([ev_s_re, ev_s_im, ev_bd, yp_free[j % 2]],
                          lambda t, j=j, ypb=ypb, sr=sre_slot,
                          si=sim_slot: [
                    MM(t, bank(ypb, TT), bdre[:, :],
                             numrot[:, sr * TT:(sr + 1) * TT],
                             start=True, stop=False),
                    MM(t, bank(ypb, TT), bdim[:, :],
                             numrot[:, si * TT:(si + 1) * TT],
                             start=False, stop=True),
                ][-1])
                ev_ype = V([ev_yp], lambda v, j=j, ypb=ypb: v.tensor_copy(
                    ypsb[:, j * TT:(j + 1) * TT], bank(ypb, TT)))
                yp_free[j % 2] = ev_ype
                slot_free[sre_slot] = [ev_ype, ev_q_re]
                slot_free[sim_slot] = [ev_ype, ev_q_im]
                ev_ypre[j] = ev_ype

            ev_den = V([ev_sld, ev_cu], lambda v: v.tensor_tensor_scan(
                densb[:, :], stsb2[0:K, ST_PW * TT:ST_PW * TT + TT],
                stsb2[0:K, ST_PW * TT:ST_PW * TT + TT],
                carry[0:K, ST_PW:ST_PW + 1], Alu.add, Alu.bypass))
            ev_st_rd = ev_den
            def v_inv(v):
                fence(v, v.tensor_scalar_max(invd[:, :], densb[:, :], 1e-4))
                return v.reciprocal(invd[:, :], invd[:, :])
            ev_inv = V([ev_den], v_inv)
            def v_t2(v):
                fence(v, v.tensor_tensor(t2sb[:, 0:TT], bank(0, TT, 32),
                                         invd[:, :], Alu.mult))
                return v.tensor_tensor(t2sb[:, 0:TT], t2sb[:, 0:TT],
                                       invd[:, :], Alu.mult)
            ev_t2 = V([ev_inv, sq_free[1], prev_rs], v_t2)
            ev_q_rd = ev_t2
            ev_rs = S([ev_t2], lambda s: [
                s.activation(t2sb[:, TT:2 * TT], t2sb[:, 0:TT], Act.Ln,
                             scale=cst(1.0 / (2.0 * Hd), 32),
                             bias=cst(1e-5, 32)),
                s.activation(t2sb[:, TT:2 * TT], t2sb[:, TT:2 * TT],
                             Act.Exp, scale=cst(-0.5, 32), bias=cst(0.0, 32)),
            ][-1])
            prev_rs = ev_rs
            ev_sn = V([ev_rs, snorm_rd], lambda v: v.tensor_tensor(
                snormb[:, :], t2sb[:, TT:2 * TT], invd[:, :], Alu.mult))

            ev_yg = {}
            for j in range(NXC):
                seb = 3
                ev_se = T([ev_sn, se_free[0]], lambda t, j=j, seb=seb:
                          MM(t, bank(seb, TT), eselb[:, j * 128:
                                                        (j + 1) * 128],
                                   snormb[:, :], start=True, stop=True))
                ev_y1 = V([ev_se, ev_ypre[j]], lambda v, j=j, seb=seb:
                          v.tensor_tensor(y1b[:, (j % 2) * TT:
                                              (j % 2 + 1) * TT],
                                          ypsb[:, j * TT:(j + 1) * TT],
                                          bank(seb, TT), Alu.mult))
                ev_yg[j] = V([ev_y1, ev_gld, op_rd if j == 0 else None],
                             lambda v, j=j: v.tensor_tensor(
                    ygsb[:, j * TT:(j + 1) * TT],
                    y1b[:, (j % 2) * TT:(j % 2 + 1) * TT],
                    gtsb2[:, j * TT:(j + 1) * TT], Alu.mult))
                se_free[0] = ev_y1
                snorm_rd = ev_se
            ev_gt_rd = ev_yg[NXC - 1]

            ev_x1 = {}
            for b in range(BPT):
                evs_po = [ev_yg[j] for j in range(NXC)]
                def t_op(t, b=b):
                    last = None
                    for ch in range(16):
                        for hf in range(2):
                            last = MM(t, 
                                bank(4 + hf, 512),
                                ygsb[:, ch * TT + b * 128:
                                     ch * TT + (b + 1) * 128],
                                woutb[:, ch * D + hf * 512:
                                      ch * D + hf * 512 + 512],
                                start=(ch == 0), stop=(ch == 15))
                    return last
                ev_po = T(evs_po + [pout_free[0], pout_free[1], ev_g2],
                          t_op)
                op_rd = ev_po
                ev_r = V([ev_po, ev_xld] + x1_free, lambda v, b=b: [
                    v.tensor_tensor(x1sb[:, b * D:b * D + 512],
                                    bank(4, 512),
                                    xtm2[:, b * D:b * D + 512], Alu.add),
                    v.tensor_tensor(x1sb[:, b * D + 512:(b + 1) * D],
                                    bank(5, 512),
                                    xtm2[:, b * D + 512:(b + 1) * D],
                                    Alu.add),
                ][-1])
                pout_free[0] = ev_r
                pout_free[1] = ev_r
                ev_x1[b] = ev_r
            ev_x1_all = [ev_x1[b] for b in range(BPT)]
            ev_xst = SY(ev_x1_all, lambda sy, i=i: sy.dma_start(
                out=out.ap()[i * TT:(i + 1) * TT, :].rearrange(
                    "(b p) d -> p b d", p=128),
                in_=x1sb[:, :].rearrange("p (b d) -> p b d", d=D)), so)
            if "x1" in dev:
                G(ev_x1_all, lambda g, i=i: g.dma_start(
                    out=d_x1.ap()[i * TT:(i + 1) * TT, :].rearrange(
                        "(b p) d -> p b d", p=128),
                    in_=x1sb[:, :].rearrange("p (b d) -> p b d", d=D)), ao)

            ev_ln2, _ = ln_norm(x1sb, ln2tm, BPT, lnst2,
                                ev_x1_all + ln2_free, scp2)
            ev_l2st = SY([ev_ln2], lambda sy, i=i: sy.dma_start(
                out=ln2st.ap()[i, :, :], in_=ln2tm[:, :]), so)
            ev_fm2, _ = tm_to_fm(ln2tm, BPT,
                                 lambda b, dc: lnfm2[:, dc * TT + b * 128:
                                                     dc * TT + (b + 1) * 128],
                                 [ev_ln2] + ln2_free)

            def t_lg(t):
                last = None
                for dc in range(NDC):
                    last = MM(t, bank(0, TT), wcb[:, dc * ES:
                                                     (dc + 1) * ES],
                                    lnfm2[:, dc * TT:(dc + 1) * TT],
                                    start=(dc == 0), stop=(dc == NDC - 1))
                return last
            ev_lg = T([ev_fm2, ev_q_rd, ev_g2], t_lg)
            ev_lge = V([ev_lg], lambda v: v.scalar_tensor_tensor(
                lgsb[:, :].rearrange("p (g t) -> p g t", t=GRP),
                bank(0, TT).rearrange("p (g t) -> p g t", t=GRP),
                bcsb2[:, :],
                tiesb[:, :].unsqueeze(1).broadcast_to([128, GPT, GRP]),
                Alu.add, Alu.add))
            ev_q_rd = ev_lge
            ev_mx = V([ev_lge], lambda v: v.tensor_reduce(
                gmx[:, :], lgsb[:, :].rearrange("p (g t) -> p g t", t=GRP),
                AX, Alu.max))
            ev_oh = V([ev_mx, ev_oh_dma], lambda v, tis=tis: v.tensor_tensor(
                ohsb[:, tis * TT:(tis + 1) * TT].rearrange(
                    "p (g t) -> p g t", t=GRP),
                lgsb[:, :].rearrange("p (g t) -> p g t", t=GRP),
                gmx[:, :].unsqueeze(2).broadcast_to([128, GPT, GRP]),
                Alu.is_equal))

            if tis == SEG_TILES - 1:
                ev_oh_dma = SY([ev_oh], lambda sy, seg=seg: sy.dma_start(
                    out=ohst.ap()[seg, :, :], in_=ohsb[:, :]), so)
                if "oh" in dev:
                    G([ev_oh], lambda g, seg=seg: g.dma_start(
                        out=d_oh.ap()[seg, :, :], in_=ohsb[:, :]), ao)
            x1_free = [ se.ev(), ev_xst ]
            ln2_free = [ev_l2st, ev_lg]

        for c in reversed(p2_cm):
            c.__exit__(None, None, None)


    if 3 in phases:
        bar2 = barrier()
        p3_cm = [
            nc.sbuf_tensor("w2b", [128, NPACK * D], BF16),
            nc.sbuf_tensor("w1buf", [128, 2 * ES * EF], BF16),
            nc.sbuf_tensor("ln2seg", [128, SEG_TILES * BPT * D], BF16),
            nc.sbuf_tensor("permseg", [128, BLK_SEG * 512], BF16),
            nc.sbuf_tensor("ohsegb", [128, SEG_T], BF16),
            nc.sbuf_tensor("ohT", [128, 128], BF16),
            nc.sbuf_tensor("xdc", [128, BLK_SEG * 512], BF16),
            nc.sbuf_tensor("innersb", [128, NPACK * GSEG], BF16),
            nc.sbuf_tensor("padsb", [128, 2 * MSL], BF16),
            nc.sbuf_tensor("outsb", [128, 2 * D], BF16),
            nc.sbuf_tensor("outg", [128, 2 * D], BF16),
            nc.sbuf_tensor("b1pb", [128, NPACK], F32),
            nc.sbuf_tensor("b2pb", [128, D], BF16),
            nc.sbuf_tensor("bmaskb", [128, 4], BF16),
            nc.sbuf_tensor("x2c", [128, 2 * D], F32),
        ]
        (w2b, w1buf, ln2seg, permseg, ohsegb, ohT, xdc, innersb, padsb,
         outsb, outg, b1pb, b2pb, bmaskb, x2c) = \
            [c.__enter__() for c in p3_cm]

        def g3_loads(g, inc):
            n = 0
            for pk in range(NPACK):
                inc(g.dma_start(out=w2b[:, pk * D:(pk + 1) * D],
                                in_=w2p_in.ap()[pk, :, :]))
                n += 1
            for dst, srcp in [(b1pb[:, :], b1p_in.ap()),
                              (b2pb[:, :], b2p_in.ap()),
                              (bmaskb[:, :], bmask_in.ap())]:
                inc(g.dma_start(out=dst, in_=srcp))
                n += 1
            return n
        ev_g3 = emit_multi("gpsimd", wl, bar2, g3_loads, NPACK + 3)

        ev_perm_rd = None
        ev_oh_rd = None
        ev_w1_rd = [None, None]
        ev_xdc_rd = None
        ev_inner_rd = None
        ev_outsb_rd = None
        ev_outg_rd = [None, None]
        ev_x2c_rd = [None, None]
        ev_px2_rd = [None, None]
        ev_pb_last = None

        for seg in range(NSEG):
            ev_ohl = SY([ev_oh_rd] + bar2, lambda sy, seg=seg: sy.dma_start(
                out=ohsegb[:, :], in_=ohst.ap()[seg, :, :]), ld)
            def l2l(sy, inc, seg=seg):
                for tis in range(SEG_TILES):
                    inc(sy.dma_start(
                        out=ln2seg[:, tis * BPT * D:(tis + 1) * BPT * D],
                        in_=ln2st.ap()[seg * SEG_TILES + tis, :, :]))
                return SEG_TILES
            ev_l2l = emit_multi("sync", ld, [ev_perm_rd] + bar2, l2l,
                                SEG_TILES)

            ev_pm = []
            for blk in range(BLK_SEG):
                evt = T([ev_ohl,
                         ev_pm[-2][0] if len(ev_pm) >= 2 else None],
                        lambda t, blk=blk: t.transpose(
                    pbf_slot(blk % 4), ohsegb[:, blk * 128:(blk + 1) * 128],
                    identb[:, :]))
                eve = V([evt, ev_perm_rd], lambda v, blk=blk: v.tensor_copy(
                    ohT[:, :], pbf_slot(blk % 4)))
                evz = V([eve], lambda v, blk=blk: v.memset(
                    permseg[:, blk * 512:(blk + 1) * 512], 0.0))
                def cpy(v, blk=blk):
                    last = None
                    for g_ in range(GPB):
                        last = v.tensor_copy(
                            permseg[32 * g_:32 * (g_ + 1),
                                    blk * 512 + g_ * 128:
                                    blk * 512 + (g_ + 1) * 128],
                            ohT[32 * g_:32 * (g_ + 1), :])
                    return last
                evc = V([evz], cpy)
                ev_pm.append((evc, evt))
            ev_perm = ev_pm[-1][0]

            if "perm" in dev and seg == 0:
                G([ev_perm, ev_l2l], lambda g: g.dma_start(
                    out=d_perm.ap(), in_=permseg[:, :]), ao)
                G([ev_perm, ev_l2l], lambda g: g.dma_start(
                    out=d_l2s.ap(), in_=ln2seg[:, :]), ao)
            ev_w1mm_last = None
            for dc in range(NDC):
                buf = dc % 2
                ev_wl1 = G([ev_g3, ev_w1_rd[buf]], lambda g, dc=dc, buf=buf:
                           g.dma_start(out=w1buf[:, buf * ES * EF:
                                                 (buf + 1) * ES * EF],
                                       in_=w1p_in.ap()[dc, :, :]), wl)
                ev_ga = {}
                for blk in range(BLK_SEG):
                    tis, b = blk // BPT, blk % BPT
                    evg = T([ev_perm, ev_l2l,
                             ev_ga[blk - 1][0] if blk - 1 in ev_ga else
                             ev_xdc_rd,
                             None if (blk or dc) else ev_px2_rd[0],
                             None if (blk or dc) else ev_x2c_rd[0],
                             None if (blk or dc) else ev_x2c_rd[1]],
                            lambda t, tis=tis, b=b, dc=dc, blk=blk: MM(t, 
                        bank(5, 512),
                        ln2seg[:, (tis * BPT + b) * D + dc * 128:
                               (tis * BPT + b) * D + (dc + 1) * 128],
                        permseg[:, blk * 512:(blk + 1) * 512],
                        start=True, stop=True))
                    eve = V([evg], lambda v, blk=blk: v.tensor_copy(
                        xdc[:, blk * 512:(blk + 1) * 512], bank(5, 512)))
                    ev_ga[blk] = (eve, evg)
                ev_xall = ev_ga[BLK_SEG - 1][0]

                if "xdc" in dev and seg == 0 and dc == 0:
                    G([ev_xall], lambda g: g.dma_start(
                        out=d_xdc.ap(), in_=xdc[:, :]), ao)
                for es in range(ES):
                    esl = es % 4
                    pk = es // 4
                    gates = [ev_xall, ev_wl1]
                    if dc == 0 and es == 0:
                        gates += [ev_inner_rd, ev_px2_rd[0]]
                    ev_w1mm_last = T(
                        gates,
                        lambda t, es=es, esl=esl, pk=pk, dc=dc, buf=buf:
                        MM(t, 
                            pall[32 * esl:32 * (esl + 1),
                                 pk * GSEG:(pk + 1) * GSEG],
                            w1buf[:, buf * ES * EF + es * EF:
                                  buf * ES * EF + (es + 1) * EF],
                            xdc[:, :].rearrange(
                                "p (blk g es) -> p es blk g",
                                g=GPB, es=128)[:, es, :, :],
                            start=(dc == 0 and
                                   (pk * GSEG * 4) % 2048 == 0),
                            stop=(dc == NDC - 1),
                            tile_position=(0, 32 * esl)))
                ev_w1_rd[buf] = ev_w1mm_last
                ev_xdc_rd = ev_w1mm_last

            def s_ie(s):
                last = None
                for pk in range(NPACK):
                    last = s.activation(
                        innersb[:, pk * GSEG:(pk + 1) * GSEG],
                        pall[:, pk * GSEG:(pk + 1) * GSEG], Act.Relu,
                        scale=cst(1.0), bias=b1pb[:, pk:pk + 1])
                return last
            ev_ie = S([ev_w1mm_last, ev_g3, ev_pb_last], s_ie)

            if "inner" in dev and seg == 0:
                G([ev_ie], lambda g: g.dma_start(
                    out=d_inner.ap(), in_=innersb[:, :]), ao)
            ev_obst = {}
            for sub in range(NSUB):
                ev_w2 = {}
                for pk in range(NPACK):
                    pr = pk % 2
                    bnk0 = 2 * pr
                    ev_pb = V([ev_ie, ev_w2.get(pk - 2, (None,) * 3)[0]],
                              lambda v, pk=pk, sub=sub, pr=pr:
                              v.tensor_tensor(
                        padsb[:, pr * MSL:(pr + 1) * MSL].rearrange(
                            "p (e g) -> p e g", e=4),
                        innersb[:, pk * GSEG + sub * SUB:
                                pk * GSEG + (sub + 1) * SUB].unsqueeze(
                            1).broadcast_to([128, 4, SUB]),
                        bmaskb[:, :].unsqueeze(2).broadcast_to(
                            [128, 4, SUB]),
                        Alu.mult))
                    ev_pb_last = ev_pb

                    def t_w2(t, pk=pk, pr=pr, bnk0=bnk0):
                        last = None
                        for hf in range(2):
                            last = MM(t, 
                                pall[0:MSL, (bnk0 + hf) * 512:
                                     (bnk0 + hf) * 512 + 512],
                                padsb[:, pr * MSL:(pr + 1) * MSL],
                                w2b[:, pk * D + hf * 512:
                                    pk * D + hf * 512 + 512],
                                start=True, stop=True)
                        return last
                    evm = T([ev_pb, ev_g3,
                             ev_w2.get(pk - 2, (None,) * 3)[0],
                             ev_outsb_rd if pk < 2 else None,
                             ev_ie if pk < 2 else None], t_w2)

                    prev_st = ev_obst.get((sub, pk - 2)) or \
                        ev_obst.get((sub - 1, pk + NPACK - 2))
                    if pr == 0:
                        def v_ev(v, pk=pk, bnk0=bnk0, pr=pr):
                            v.tensor_copy(
                                outsb[0:MSL, pr * D:pr * D + 512],
                                pall[0:MSL, bnk0 * 512:bnk0 * 512 + 512])
                            return v.tensor_copy(
                                outsb[0:MSL, pr * D + 512:(pr + 1) * D],
                                pall[0:MSL, (bnk0 + 1) * 512:
                                     (bnk0 + 1) * 512 + 512])
                        eve = V([evm, prev_st], v_ev)
                    else:
                        def s_ev(s, pk=pk, bnk0=bnk0, pr=pr):
                            s.activation(
                                outsb[0:MSL, pr * D:pr * D + 512],
                                pall[0:MSL, bnk0 * 512:bnk0 * 512 + 512],
                                Act.Identity, scale=cst(1.0, MSL),
                                bias=cst(0.0, MSL))
                            return s.activation(
                                outsb[0:MSL, pr * D + 512:(pr + 1) * D],
                                pall[0:MSL, (bnk0 + 1) * 512:
                                     (bnk0 + 1) * 512 + 512],
                                Act.Identity, scale=cst(1.0, MSL),
                                bias=cst(0.0, MSL))
                        eve = S([evm, prev_st], s_ev)
                    ev_st_ob = emit("sync", sq, [eve],
                                    lambda sy, pk=pk, pr=pr: sy.dma_start(
                        out=ob.ap()[pk // 2, pk % 2].rearrange(
                            "e g d -> (e g) d"),
                        in_=outsb[0:MSL, pr * D:(pr + 1) * D]))
                    ev_obst[(sub, pk)] = ev_st_ob
                    ev_w2[pk] = (eve, evm, ev_st_ob)
                ev_ob_all = [ev_obst[(sub, NPACK - 2)],
                             ev_obst[(sub, NPACK - 1)]]

                ev_og_last = None
                for bls in range(SUB // GPB):
                    blk = sub * (SUB // GPB) + bls

                    def t_b2(t, blk=blk):
                        last = None
                        for hf in range(2):
                            last = MM(t,
                                pall[:, (4 + hf) * 512:
                                     (4 + hf) * 512 + 512],
                                ohsegb[:, blk * 128:(blk + 1) * 128],
                                b2pb[:, hf * 512:(hf + 1) * 512],
                                start=True, stop=False)
                        return last
                    evb2 = T([ev_g3, ev_px2_rd[0], ev_ie], t_b2)

                    gev = []
                    for gl in range(GPB):
                        g_loc = bls * GPB + gl
                        g_glob = sub * SUB + g_loc
                        slot = gl % 2

                        ev_og = SY(ev_ob_all + [ev_outg_rd[slot]],
                                   lambda sy, g_loc=g_loc, slot=slot:
                                   sy.dma_start(
                            out=outg[:, slot * D:(slot + 1) * D],
                            in_=ob.ap()[:, :, :, g_loc, :].rearrange(
                                "q p e d -> (q p e) d")), ld)
                        ev_og_last = ev_og
                        if "outg" in dev and seg == 0 and sub == 0 and g_glob == 0:
                            G([ev_og], lambda g: g.dma_start(
                                out=d_outg.ap(),
                                in_=outg[:, 0 * D:1 * D]), ao)

                        def t_sc(t, g_glob=g_glob, gl=gl, slot=slot):
                            last = None
                            for hf in range(2):
                                last = MM(t,
                                    pall[32 * gl:32 * (gl + 1),
                                         (4 + hf) * 512:
                                         (4 + hf) * 512 + 512],
                                    ohsegb[:, g_glob * 32:
                                           (g_glob + 1) * 32],
                                    outg[:, slot * D + hf * 512:
                                         slot * D + hf * 512 + 512],
                                    start=False, stop=(gl == GPB - 1),
                                    tile_position=(0, 32 * gl))
                            return last
                        evsc = T([ev_og, evb2], t_sc)
                        ev_outg_rd[slot] = evsc
                        gev.append(evsc)

                    evx2 = V(gev + [ev_x2c_rd[bls % 2]],
                             lambda v, bls=bls: [
                        v.tensor_copy(x2c[:, (bls % 2) * D:
                                          (bls % 2) * D + 512],
                                      bank(4, 512)),
                        v.tensor_copy(x2c[:, (bls % 2) * D + 512:
                                          (bls % 2 + 1) * D],
                                      bank(5, 512)),
                    ][-1])
                    ev_px2_rd[0] = evx2
                    row0 = seg * SEG_T + blk * 128
                    ev_ac = G([evx2], lambda g, row0=row0, bls=bls:
                              g.dma_start(
                        out=out.ap()[row0:row0 + 128, :],
                        in_=x2c[:, (bls % 2) * D:(bls % 2 + 1) * D],
                        accum_op=Alu.add), ao)
                    ev_x2c_rd[bls % 2] = ev_ac
                ev_obst[(sub, NPACK - 2)] = ev_og_last
                ev_obst[(sub, NPACK - 1)] = ev_og_last
                ev_outsb_rd = te.ev()
            ev_perm_rd = te.ev()
            ev_oh_rd = te.ev()
            ev_inner_rd = te.ev()

        for c in reversed(p3_cm):
            c.__exit__(None, None, None)

    with nc.Block() as block:
        @block.sync
        def _(eng):
            for th in prog["sync"]:
                th(eng)
            eng.wait_ge(S_so, so.n)

        @block.gpsimd
        def _(eng):
            for th in prog["gpsimd"]:
                th(eng)
            eng.wait_ge(S_ao, ao.n)

        @block.tensor
        def _(eng):
            for th in prog["tensor"]:
                th(eng)

        @block.vector
        def _(eng):
            for th in prog["vector"]:
                th(eng)

        @block.scalar
        def _(eng):
            for th in prog["scalar"]:
                th(eng)

    for c in reversed(per_cm + sem_cm):
        c.__exit__(None, None, None)
    return nc


def stage_core(inputs, core, TPC, TT, seq_len, n_cores):
    NT = TPC // TT
    seq = core // 2 if n_cores > 1 else 0
    half = core % 2 if n_cores > 1 else 0
    t0 = half * TPC

    xfull = np.asarray(inputs["L0_attn_x"], dtype=np.float32)
    xs = xfull[seq]
    x_ext = np.zeros((TPC + 128, D), np.float32)
    x_ext[128:] = xs[t0:t0 + TPC]
    if t0 >= 3:
        x_ext[125:128] = xs[t0 - 3:t0]

    W_in = np.asarray(inputs["L0_attn_in_proj_w"], np.float32)
    conv_k = np.asarray(inputs["L0_attn_conv_k"], np.float32)
    conv_b = np.asarray(inputs["L0_attn_conv_b"], np.float32)
    theta = np.asarray(inputs["L0_attn_theta"], np.float32).reshape(K, Hd)
    dsl = np.asarray(inputs["L0_attn_decay_slopes"], np.float32)
    asl = np.asarray(inputs["L0_attn_anchor_slopes"], np.float32)
    ssc = np.asarray(inputs["L0_attn_score_scale"], np.float32)
    dlg = np.asarray(inputs["L0_attn_deriv_logits"], np.float32)
    W_re = np.asarray(inputs["L0_attn_W_re"], np.float32)
    W_im = np.asarray(inputs["L0_attn_W_im"], np.float32)
    nsc = np.asarray(inputs["L0_attn_norm_scale"], np.float32)
    W_out = np.asarray(inputs["L0_attn_out_proj_w"], np.float32)
    Wc = np.asarray(inputs["L0_ffn_Wc"], np.float32)
    bc = np.asarray(inputs["L0_ffn_bc"], np.float32)
    W1 = np.asarray(inputs["L0_ffn_W1"], np.float32)
    b1 = np.asarray(inputs["L0_ffn_b1"], np.float32)
    W2 = np.asarray(inputs["L0_ffn_W2"], np.float32)
    b2 = np.asarray(inputs["L0_ffn_b2"], np.float32)

    convd = np.zeros((128, KS * NZC * 128), np.float32)
    for tap in range(KS):
        for oc in range(NZC):
            pc = 128 if oc < NZC - 1 else 32
            w = conv_k[tap, 0, oc * 128:oc * 128 + pc]
            blk = np.zeros((128, 128), np.float32)
            blk[:pc, :pc] = np.diag(w)
            convd[:, (tap * NZC + oc) * 128:(tap * NZC + oc + 1) * 128] = blk

    cb_fm = np.zeros((128, NZC), np.float32)
    for oc in range(NZC):
        pc = 128 if oc < NZC - 1 else 32
        cb_fm[:pc, oc] = conv_b[oc * 128:oc * 128 + pc]

    theta_fm = np.zeros((128, NXC), np.float32)
    for j in range(NXC):
        for p in range(128):
            theta_fm[p, j] = theta[2 * j + (p >= 64), p % 64]

    esel = np.zeros((32, NXC * 128), np.float32)
    for j in range(NXC):
        for p in range(128):
            esel[2 * j + (p >= 64), j * 128 + p] = 1.0
    qsel = np.zeros((128, NXC * 32), np.float32)
    for j in range(NXC):
        for p in range(128):
            qsel[p, j * 32 + 2 * j + (p >= 64)] = 1.0
    bmask = np.zeros((128, 4), np.float32)
    for p in range(128):
        bmask[p, p // 32] = 1.0
    tie = np.broadcast_to(
        np.linspace(0.0, 1e-6, GRP, dtype=np.float32)[None, :],
        (128, GRP)).copy()
    iota_t = np.broadcast_to(
        (t0 + np.arange(TT, dtype=np.float32))[None, :], (K, TT)).copy()

    W1es = W1.reshape(ES, D, EF)
    w1p = np.zeros((NDC, 128, ES * EF), np.float32)
    for dc in range(NDC):
        w1p[dc] = W1es[:, dc * 128:(dc + 1) * 128, :].transpose(
            1, 0, 2).reshape(128, ES * EF)
    W2es = W2.reshape(ES, D, EF)
    w2p = np.zeros((NPACK, 128, D), np.float32)
    for pk in range(NPACK):
        for esl in range(4):
            w2p[pk, esl * EF:(esl + 1) * EF, :] = \
                W2es[4 * pk + esl].T
    b1es = b1.reshape(ES, EF)
    b1p = np.zeros((128, NPACK), np.float32)
    for pk in range(NPACK):
        for esl in range(4):
            b1p[esl * EF:(esl + 1) * EF, pk] = b1es[4 * pk + esl]
    b2p = b2.reshape(ES, D)

    return {
        "x": x_ext,
        "win": W_in.reshape(NDC, 128, C),
        "convd": convd,
        "cbias": cb_fm,
        "wout": W_out.reshape(16, 128, D),
        "theta_fm": theta_fm,
        "dslope": np.concatenate([dsl, asl]).reshape(K, 1),
        "sscale": ssc.reshape(K, 1),
        "dlogit": dlg.reshape(1, 3),
        "wre2": np.concatenate([W_re, W_re], 0),
        "wim2": np.concatenate([W_im, W_im], 0),
        "nsc": nsc.reshape(128, 1),
        "wc": Wc.reshape(D, ES).reshape(NDC, 128, ES),
        "bcv": bc.reshape(ES, 1),
        "w1p": w1p, "w2p": w2p, "b1p": b1p, "b2p": b2p,
        "esel": esel, "qsel": qsel, "bmask": bmask, "tie": tie,
        "iota_t": iota_t,
        "twsgn": np.concatenate([np.ones(ND_K), -np.ones(4)]).reshape(
            K, 1).astype(np.float32),
        "twbm": np.concatenate([np.full(ND_K, -float(seq_len - 1)),
                                np.zeros(4)]).reshape(K, 1).astype(
                                    np.float32),
        "flag_first": np.full((128, 1), 1.0 if half == 0 else 0.0,
                              np.float32),
        "ident": np.eye(128, dtype=np.float32),
    }


_TPC, _TT, _SEGT, _SL = 4096, 512, 4, 8192


def kernel(**inputs):
    nc = build_nc(_TPC, _TT, _SEGT, _SL, phases=(1, 2, 3), dev=False)
    in_maps = [stage_core(inputs, c, _TPC, _TT, _SL, N_CORES)
               for c in range(N_CORES)]
    res = run_bass_kernel_spmd(nc, in_maps, core_ids=list(range(N_CORES)))
    out = np.zeros((4, _SL, D), np.float32)
    for c in range(N_CORES):
        seq, half = c // 2, c % 2
        out[seq, half * _TPC:(half + 1) * _TPC] = \
            np.asarray(res.results[c]["out"]).reshape(_TPC, D)
    return out
